# version 12
# baseline (speedup 1.0000x reference)
import numpy as np
import ml_dtypes
import concourse.bass as bass
import concourse.mybir as mybir
from concourse.bass_utils import run_bass_kernel_spmd

F32 = mybir.dt.float32
BF16 = mybir.dt.bfloat16
I32 = mybir.dt.int32
AF = mybir.ActivationFunctionType
ALU = mybir.AluOpType
AX = mybir.AxisListType

D = 1024
H = 16
DK = 64
NT = 4096
CH = 128
NCH = 32
FC = 8
TB = 512
NTB = 8
GRP = 512
NGRP = 8
INNER = 4096
EPS = 1e-5
BF = ml_dtypes.bfloat16


def build_program(na=NCH, relay=True, nb=NCH, tern=True, ngr=NGRP, dbg=False,
                  nstop=None):
    nc = bass.Bass("TRN2", target_bir_lowering=False)

    def din(name, shape, dt=F32):
        return nc.dram_tensor(name, shape, dt, kind="ExternalInput").ap()

    x_in = din("x_in", [NT, D])
    WHBM = {w: din("W" + w, [D, D]) for w in ["q", "k", "v", "gK", "gC", "o"]}
    bgK = din("bgK", [1, D])
    bgC = din("bgC", [1, D])
    wgT = din("wgT", [D, INNER])
    wuT = din("wuT", [D, INNER])
    wdT = din("wdT", [INNER, D])
    m_incl_neg = din("m_incl_neg", [CH, CH])
    m_rec = din("m_rec", [CH, CH])
    m_rear_neg = din("m_rear_neg", [CH, CH])
    sm_incl4 = din("sm_incl4", [CH, 4 * CH], BF16)
    sm_strict4 = din("sm_strict4", [CH, 4 * CH], BF16)
    ident_f = din("ident_f", [CH, CH])
    ident_b = din("ident_b", [CH, CH], BF16)
    ones_row = din("ones_row", [1, CH])
    onecol_f = din("onecol_f", [CH, 1])
    negcol_f = din("negcol_f", [CH, 1])
    predsel = din("predsel", [CH, 9])

    out_t = nc.dram_tensor("out_t", [NT, D], F32, kind="ExternalOutput").ap()

    def scr(name, shape, dt):
        kind = "ExternalOutput" if dbg else "Internal"
        return nc.dram_tensor(name, shape, dt, kind=kind).ap()

    S_qt = scr("S_qt", [NT, D], BF16)
    S_kt = scr("S_kt", [NT, D], BF16)
    S_vt = scr("S_vt", [NT, D], BF16)
    S_lA = scr("S_lA", [NT, D], F32)
    S_lB = scr("S_lB", [NT, D], F32)
    S_xp = scr("S_xp", [NT, D], F32)
    TERN_G = scr("TERN_G", [CH, 32 * 8 * CH], BF16)
    TERN_U = scr("TERN_U", [CH, 32 * 8 * CH], BF16)
    TERN_D = scr("TERN_D", [INNER, D], BF16)
    cc_kind = "ExternalOutput" if (dbg and not relay) else "Internal"
    CC_IN = nc.dram_tensor("CC_IN", [CH, 2080], F32, kind=cc_kind).ap()
    CC_OUT = nc.dram_tensor("CC_OUT", [4 * CH, 2080], F32).ap()

    from contextlib import ExitStack
    ges = ExitStack()

    def gsb(name, shape, dt=F32):
        return ges.enter_context(nc.sbuf_tensor(name, shape, dt))

    cmask = gsb("cmask", [CH, 3 * CH])
    smasks = gsb("smasks", [CH, 8 * CH], BF16)
    ident_fs = gsb("ident_fs", [CH, CH])
    ident_bs = gsb("ident_bs", [CH, CH], BF16)
    ones_row_s = gsb("ones_row_s", [1, CH])
    onecol_fs = gsb("onecol_fs", [CH, 1])
    negcol_fs = gsb("negcol_fs", [CH, 1])
    psc = gsb("psc", [CH, 9])
    rowscr = gsb("rowscr", [1, 2 * D])
    rowsf = gsb("rowsf", [1, 4 * D])
    gcols = gsb("gcols", [CH, FC * 4])
    St_S = gsb("St_S", [CH, FC * DK])
    St_C = gsb("St_C", [CH, FC * DK])
    St_G = gsb("St_G", [CH, FC * DK])
    St_H = gsb("St_H", [CH, FC * DK])
    tmp_st = gsb("tmp_st", [CH, FC * DK])
    avec = gsb("avec", [CH, FC])
    bvec = gsb("bvec", [CH, FC])
    Sb_S = gsb("Sb_S", [CH, FC * DK], BF16)
    Sb_G = gsb("Sb_G", [CH, FC * DK], BF16)
    c0m = gsb("c0m", [CH, FC * DK], BF16)
    comp_bf = gsb("comp_bf", [CH, FC * DK], BF16)
    comp_bf2 = gsb("comp_bf2", [CH, FC * DK], BF16)
    agg = gsb("agg", [CH, 2080])
    mcol = gsb("mcol", [CH, 4])
    vcol = gsb("vcol", [CH, 4])
    rcol = gsb("rcol", [CH, 4])
    nmcol = gsb("nmcol", [CH, 4])
    tcol = gsb("tcol", [CH, 4])
    ncol = gsb("ncol", [CH, 4])
    f_M = gsb("f_M", [CH, D], BF16)
    f_Qh = gsb("f_Qh", [CH, D], BF16)
    t_Kr = gsb("t_Kr", [CH, D], BF16)
    t_Qr = gsb("t_Qr", [CH, D], BF16)
    t_Mt = gsb("t_Mt", [CH, D], BF16)
    mS3 = gsb("mS3", [CH, 16 * CH], BF16)
    dneg = gsb("dneg", [CH, 16 * DK], BF16)

    PB = [ges.enter_context(nc.psum_tensor(f"PB{i}", [CH, 512], F32))
          for i in range(8)]
    SB = ges.enter_context(nc.semaphore("SB"))

    SEGS = []
    EN = {"on": True}

    def stage(engine, cb, ndma=0, extra=0):
        if EN["on"]:
            SEGS.append((engine, cb, ndma, extra))

    def emit_ln_tmajor(src_aps, nsl, sqbuf, w=D):
        def sums(e):
            for s in range(nsl):
                e.tensor_reduce(mcol[:, s:s + 1], src_aps(s), AX.X, ALU.add)
            for s in range(nsl):
                e.tensor_tensor(sqbuf[:, 0:w], src_aps(s), src_aps(s), ALU.mult)
                e.drain()
                e.tensor_reduce(vcol[:, s:s + 1], sqbuf[:, 0:w], AX.X, ALU.add)
                e.drain()
            e.tensor_tensor(tcol[:, 0:nsl], mcol[:, 0:nsl], mcol[:, 0:nsl], ALU.mult)
            e.tensor_scalar(ncol[:, 0:nsl], mcol[:, 0:nsl], -1.0 / w, None, ALU.mult)
            e.drain()
            e.scalar_tensor_tensor(vcol[:, 0:nsl], tcol[:, 0:nsl], -1.0 / w,
                                   vcol[:, 0:nsl], ALU.mult, ALU.add)
            e.drain()
            e.tensor_scalar(vcol[:, 0:nsl], vcol[:, 0:nsl], 1.0 / w, EPS,
                            ALU.mult, ALU.add)
        stage("vector", sums)

        def fln(e):
            e.activation(tcol[:, 0:nsl], vcol[:, 0:nsl], AF.Ln, bias=0.0, scale=1.0)
        stage("scalar", fln)

        def fexp(e):
            e.activation(rcol[:, 0:nsl], tcol[:, 0:nsl], AF.Exp, bias=0.0, scale=-0.5)
        stage("scalar", fexp)

        def fin(e):
            e.tensor_tensor(nmcol[:, 0:nsl], ncol[:, 0:nsl], rcol[:, 0:nsl], ALU.mult)
        stage("vector", fin)

    def emit_rows(lA_ap, lB_ap):
        def rows_mm(e):
            e.matmul(PB[0][0:1, :], negcol_fs[:], lA_ap()[:, 0:512], start=True, stop=True)
            e.matmul(PB[1][0:1, :], negcol_fs[:], lA_ap()[:, 512:1024], start=True, stop=True)
            e.matmul(PB[2][0:1, :], negcol_fs[:], lB_ap()[:, 0:512], start=True, stop=True)
            e.matmul(PB[3][0:1, :], negcol_fs[:], lB_ap()[:, 512:1024], start=True, stop=True)
            e.matmul(PB[4][0:1, :], negcol_fs[0:DK, :], lA_ap()[0:DK, 0:512], start=True, stop=True)
            e.matmul(PB[5][0:1, :], negcol_fs[0:DK, :], lA_ap()[0:DK, 512:1024], start=True, stop=True)
            e.matmul(PB[6][0:1, :], negcol_fs[0:DK, :], lB_ap()[0:DK, 0:512], start=True, stop=True)
            e.matmul(PB[7][0:1, :], negcol_fs[0:DK, :], lB_ap()[0:DK, 512:1024], start=True, stop=True)
        stage("tensor", rows_mm)

        def rows_exp(e):
            for j, (b0, b1) in enumerate([(0, 1), (2, 3), (4, 5), (6, 7)]):
                e.activation(rowsf[0:1, j * D:j * D + 512], PB[b0][0:1, :],
                             AF.Exp, bias=0.0, scale=1.0)
                e.activation(rowsf[0:1, j * D + 512:(j + 1) * D], PB[b1][0:1, :],
                             AF.Exp, bias=0.0, scale=1.0)
        stage("scalar", rows_exp)

        def rows_tr(e):
            for fc in range(FC):
                for j in range(4):
                    e.transpose(PB[0][:, fc * 4 + j: fc * 4 + j + 1],
                                rowsf[0:1, j * D + fc * CH: j * D + (fc + 1) * CH],
                                ident_fs[0:1, 0:1])
        stage("tensor", rows_tr)

        def rows_ev(e):
            e.tensor_copy(gcols[:, 0:32], PB[0][:, 0:32])
        stage("vector", rows_ev)

    def emit_states(kr_buf, qr_buf, v_buf, is_A, htmp=None):
        def stmm(e):
            for h in range(H):
                par, fc = h % 2, h // 2
                p0, c0 = par * DK, fc * DK
                kr = kr_buf[:, h * DK:(h + 1) * DK]
                qr = qr_buf[:, h * DK:(h + 1) * DK]
                vv = v_buf[:, h * DK:(h + 1) * DK]
                dn = dneg[:, h * DK:(h + 1) * DK]
                e.matmul(PB[0][p0:p0 + DK, c0:c0 + DK], kr, kr,
                         start=True, stop=True, tile_position=(0, p0))
                e.matmul(PB[1][p0:p0 + DK, c0:c0 + DK], qr, vv,
                         start=True, stop=True, tile_position=(0, p0))
                e.matmul(PB[2][p0:p0 + DK, c0:c0 + DK], kr, dn,
                         start=True, stop=True, tile_position=(0, p0))
                if is_A:
                    mt = t_Mt[:, h * DK:(h + 1) * DK]
                    e.matmul(PB[3][p0:p0 + DK, c0:c0 + DK], mt, kr,
                             start=True, stop=True, tile_position=(0, p0))
            for h in range(H):
                par, fc = h % 2, h // 2
                p0, c0 = par * DK, fc * DK
                grow = rowsf[0:1, fc * CH + par * DK: fc * CH + (par + 1) * DK]
                e.matmul(PB[4][p0:p0 + DK, c0:c0 + DK], grow, grow,
                         start=True, stop=True, tile_position=(0, p0))
                if is_A:
                    e.matmul(PB[5][p0:p0 + DK, c0:c0 + DK],
                             ones_row_s[0:1, 0:DK], grow,
                             start=True, stop=True, tile_position=(0, p0))
        stage("tensor", stmm)

        def stev(e):
            e.tensor_tensor(tmp_st[:], St_S[:], PB[4][:, :], ALU.mult)
            if is_A:
                e.tensor_tensor(htmp[:, 0:FC * DK], St_H[:], PB[5][:, :], ALU.mult)
            e.drain()
            e.tensor_tensor(St_S[:], tmp_st[:], PB[0][:, :], ALU.add)
            for fc in range(FC):
                s = slice(fc * DK, (fc + 1) * DK)
                e.scalar_tensor_tensor(St_C[:, s], St_C[:, s],
                                       gcols[:, fc * 4 + 1: fc * 4 + 2],
                                       PB[1][:, s], ALU.mult, ALU.add)
                e.scalar_tensor_tensor(St_G[:, s], St_G[:, s],
                                       gcols[:, fc * 4 + 0: fc * 4 + 1],
                                       PB[2][:, s], ALU.mult, ALU.add)
                if is_A:
                    e.scalar_tensor_tensor(St_H[:, s], PB[3][:, s],
                                           bvec[:, fc:fc + 1],
                                           htmp[:, s], ALU.mult, ALU.add)
            if is_A:
                for fc in range(FC):
                    e.tensor_tensor(avec[:, fc:fc + 1], avec[:, fc:fc + 1],
                                    gcols[:, fc * 4:fc * 4 + 1], ALU.mult)
                    e.tensor_tensor(bvec[:, fc:fc + 1], bvec[:, fc:fc + 1],
                                    gcols[:, fc * 4 + 1:fc * 4 + 2], ALU.mult)
        stage("vector", stev)

    aes = ExitStack()

    def asb(name, shape, dt=F32):
        return aes.enter_context(nc.sbuf_tensor(name, shape, dt))

    Wsb = {w: asb(f"Wsb_{w}", [CH, FC * D], BF16) for w in ["q", "k", "v", "gK", "gC"]}
    bg_bcast = asb("bg_bcast", [CH, 2 * D])
    xr_t = asb("xr_t", [CH, D])
    xq1 = asb("xq1", [CH, D])
    xhat_t = asb("xhat_t", [CH, D], BF16)
    xhat_f = asb("xhat_f", [CH, D], BF16)
    qt_t = asb("qt_t", [CH, D], BF16)
    kt_t = asb("kt_t", [CH, D], BF16)
    vt_t = asb("vt_t", [CH, D], BF16)
    lA_t = asb("lA_t", [CH, D])
    lB_t = asb("lB_t", [CH, D])
    qf = asb("qf", [CH, D], BF16)
    kf = asb("kf", [CH, D], BF16)

    def p0_loads(e):
        e.dma_start(out=cmask[:, 0:CH], in_=m_incl_neg)
        e.dma_start(out=cmask[:, CH:2 * CH], in_=m_rec)
        e.dma_start(out=cmask[:, 2 * CH:3 * CH], in_=m_rear_neg)
        e.dma_start(out=smasks[:, 0:4 * CH], in_=sm_incl4)
        e.dma_start(out=smasks[:, 4 * CH:8 * CH], in_=sm_strict4)
        e.dma_start(out=ident_fs[:], in_=ident_f)
        e.dma_start(out=ident_bs[:], in_=ident_b)
        e.dma_start(out=ones_row_s[:], in_=ones_row)
        e.dma_start(out=onecol_fs[:], in_=onecol_f)
        e.dma_start(out=negcol_fs[:], in_=negcol_f)
        e.dma_start(out=psc[:], in_=predsel)
        e.dma_start(out=rowscr[0:1, 0:D], in_=bgK)
        e.dma_start(out=rowscr[0:1, D:2 * D], in_=bgC)
    stage("sync", p0_loads, ndma=13)

    def zero_states(e):
        e.memset(St_S[:], 0.0)
        e.memset(St_C[:], 0.0)
        e.memset(St_G[:], 0.0)
        e.memset(St_H[:], 0.0)
        e.memset(avec[:], 1.0)
        e.memset(bvec[:], 1.0)
    stage("vector", zero_states)

    def bg_mm(e):
        for i in range(4):
            e.matmul(PB[i][:, :], ones_row_s[0:1, :],
                     rowscr[0:1, i * 512:(i + 1) * 512], start=True, stop=True)
    stage("tensor", bg_mm)

    def bg_ev(e):
        for i in range(4):
            e.tensor_copy(bg_bcast[:, i * 512:(i + 1) * 512], PB[i][:, :])
    stage("vector", bg_ev)

    for wname in ["q", "k", "v", "gK", "gC"]:
        for hf in range(4):
            def wload(e, wname=wname, hf=hf):
                src = WHBM[wname].rearrange("(c p) o -> p c o", p=CH)
                e.dma_start(out=xr_t[:].rearrange("p (c o) -> p c o", c=1),
                            in_=src[:, 2 * hf:2 * hf + 1, :])
                e.dma_start(out=xq1[:].rearrange("p (c o) -> p c o", c=1),
                            in_=src[:, 2 * hf + 1:2 * hf + 2, :])
            stage("sync", wload, ndma=2)

            def wcast(e, wname=wname, hf=hf):
                e.tensor_copy(Wsb[wname][:, (2 * hf) * D:(2 * hf + 1) * D], xr_t[:])
                e.tensor_copy(Wsb[wname][:, (2 * hf + 1) * D:(2 * hf + 2) * D], xq1[:])
            stage("vector", wcast)

    def emit_A(cc):
        r0 = cc * CH

        def xload(e, r0=r0):
            e.dma_start(out=xr_t[:], in_=x_in[r0:r0 + CH, :])
        stage("sync", xload, ndma=1)

        emit_ln_tmajor(lambda s: xr_t[:, 0:D], 1, xq1)

        def xhat_mk(e):
            e.tensor_scalar(xhat_t[:], xr_t[:], rcol[:, 0:1], nmcol[:, 0:1],
                            ALU.mult, ALU.add)
        stage("vector", xhat_mk)

        def xtr(e):
            e.dma_start_transpose(
                xhat_f[:].rearrange("p (c t) -> p c t", c=FC), xhat_t[:])
        stage("sync", xtr, ndma=1)

        for half in range(2):
            def tproj(e, half=half):
                for wi, wname in enumerate(["q", "k", "v", "gK", "gC"]):
                    for dc in range(FC):
                        e.matmul(PB[wi][:, :],
                                 xhat_f[:, dc * CH:(dc + 1) * CH],
                                 Wsb[wname][:, dc * D + half * 512: dc * D + (half + 1) * 512],
                                 start=(dc == 0), stop=(dc == FC - 1))
            stage("tensor", tproj)

            def bias_add(e, half=half):
                e.tensor_tensor(PB[3][:, :], PB[3][:, :],
                                bg_bcast[:, half * 512:(half + 1) * 512], ALU.add)
                e.tensor_tensor(PB[4][:, :], PB[4][:, :],
                                bg_bcast[:, D + half * 512: D + (half + 1) * 512], ALU.add)
            stage("vector", bias_add)

            def evac(e, half=half):
                hs = slice(half * 512, (half + 1) * 512)
                e.activation(qt_t[:, hs], PB[0][:, :], AF.Identity, bias=0.0, scale=1.0)
                e.activation(kt_t[:, hs], PB[1][:, :], AF.Identity, bias=0.0, scale=1.0)
                e.activation(vt_t[:, hs], PB[2][:, :], AF.Identity, bias=0.0, scale=1.0)
                e.activation(xq1[:, 0:512], PB[3][:, :], AF.Exp, bias=0.0, scale=-1.0)
                e.activation(xq1[:, 512:1024], PB[4][:, :], AF.Exp, bias=0.0, scale=-1.0)
            stage("scalar", evac)

            def evac_p1(e, half=half):
                e.tensor_scalar(xq1[:], xq1[:], 1.0, None, ALU.add)
            stage("vector", evac_p1)

            def evac_ln(e, half=half):
                hs = slice(half * 512, (half + 1) * 512)
                e.activation(lA_t[:, hs], xq1[:, 0:512], AF.Ln, bias=0.0, scale=1.0)
                e.activation(lB_t[:, hs], xq1[:, 512:1024], AF.Ln, bias=0.0, scale=1.0)
            stage("scalar", evac_ln)

        def qk_tr(e):
            e.dma_start_transpose(
                qf[:].rearrange("p (c t) -> p c t", c=FC), qt_t[:])
            e.dma_start_transpose(
                kf[:].rearrange("p (c t) -> p c t", c=FC), kt_t[:])
        stage("sync", qk_tr, ndma=2)

        def store(e, r0=r0):
            e.dma_start(out=S_qt[r0:r0 + CH, :], in_=qt_t[:])
            e.dma_start(out=S_kt[r0:r0 + CH, :], in_=kt_t[:])
            e.dma_start(out=S_vt[r0:r0 + CH, :], in_=vt_t[:])
            e.dma_start(out=S_lA[r0:r0 + CH, :], in_=lA_t[:])
            e.dma_start(out=S_lB[r0:r0 + CH, :], in_=lB_t[:])
        stage("gpsimd", store, ndma=5)

        emit_rows(lambda: lA_t[:], lambda: lB_t[:])

        def cums(e):
            for fc in range(FC):
                sl = PB[fc // 4][:, (fc % 4) * CH:(fc % 4 + 1) * CH]
                e.matmul(sl, lB_t[:, fc * CH:(fc + 1) * CH], cmask[:, CH:2 * CH],
                         start=True, stop=True)
            e.matmul(PB[2][:, :], cmask[:, 2 * CH:3 * CH], lA_t[:, 0:512], start=True, stop=True)
            e.matmul(PB[3][:, :], cmask[:, 2 * CH:3 * CH], lA_t[:, 512:1024], start=True, stop=True)
            e.matmul(PB[4][:, :], cmask[:, 2 * CH:3 * CH], lB_t[:, 0:512], start=True, stop=True)
            e.matmul(PB[5][:, :], cmask[:, 2 * CH:3 * CH], lB_t[:, 512:1024], start=True, stop=True)
            e.matmul(PB[6][:, :], cmask[:, 0:CH], lB_t[:, 0:512], start=True, stop=True)
            e.matmul(PB[7][:, :], cmask[:, 0:CH], lB_t[:, 512:1024], start=True, stop=True)
        stage("tensor", cums)

        def exps(e):
            for fc in range(FC):
                sl = PB[fc // 4][:, (fc % 4) * CH:(fc % 4 + 1) * CH]
                e.activation(f_M[:, fc * CH:(fc + 1) * CH], sl, AF.Exp, bias=0.0, scale=1.0)
                e.activation(f_Qh[:, fc * CH:(fc + 1) * CH], sl, AF.Exp, bias=0.0, scale=-1.0)
            e.activation(t_Kr[:, 0:512], PB[2][:, :], AF.Exp, bias=0.0, scale=1.0)
            e.activation(t_Kr[:, 512:1024], PB[3][:, :], AF.Exp, bias=0.0, scale=1.0)
            e.activation(t_Qr[:, 0:512], PB[4][:, :], AF.Exp, bias=0.0, scale=1.0)
            e.activation(t_Qr[:, 512:1024], PB[5][:, :], AF.Exp, bias=0.0, scale=1.0)
            e.activation(t_Mt[:, 0:512], PB[6][:, :], AF.Exp, bias=0.0, scale=1.0)
            e.activation(t_Mt[:, 512:1024], PB[7][:, :], AF.Exp, bias=0.0, scale=1.0)
        stage("scalar", exps)

        def facA(e):
            e.tensor_tensor(f_Qh[:], qf[:], f_Qh[:], ALU.mult)
            e.tensor_tensor(f_M[:], kf[:], f_M[:], ALU.mult)
            e.tensor_tensor(t_Kr[:], kt_t[:], t_Kr[:], ALU.mult)
            e.tensor_tensor(t_Qr[:], qt_t[:], t_Qr[:], ALU.mult)
            e.tensor_tensor(t_Mt[:], kt_t[:], t_Mt[:], ALU.mult)
            for fc in range(FC):
                e.tensor_scalar(c0m[:, fc * DK:(fc + 1) * DK],
                                St_C[:, fc * DK:(fc + 1) * DK],
                                gcols[:, fc * 4 + 3: fc * 4 + 4], None, ALU.mult)
        stage("vector", facA)

        for hg in range(2):
            def s3mm(e, hg=hg):
                for hh in range(8):
                    h = hg * 8 + hh
                    par, fc = h % 2, h // 2
                    p0 = par * DK
                    sl = PB[par][:, (hh // 2) * CH:(hh // 2 + 1) * CH]
                    e.matmul(sl,
                             f_Qh[p0:p0 + DK, fc * CH:(fc + 1) * CH],
                             f_M[p0:p0 + DK, fc * CH:(fc + 1) * CH],
                             start=True, stop=True,
                             tile_position=(p0, 0))
            stage("tensor", s3mm)

            def s3mask(e, hg=hg):
                for hh in range(8):
                    h = hg * 8 + hh
                    e.tensor_tensor(
                        mS3[:, h * CH:(h + 1) * CH],
                        PB[hh % 2][:, (hh // 2) * CH:(hh // 2 + 1) * CH],
                        smasks[:, 4 * CH:5 * CH], ALU.mult)
            stage("vector", s3mask)

            def dmm(e, hg=hg):
                for hh in range(8):
                    h = hg * 8 + hh
                    par, fc = h % 2, h // 2
                    p0 = par * DK
                    dsl = PB[2 + par][:, (hh // 2) * DK:(hh // 2 + 1) * DK]
                    e.matmul(dsl, mS3[:, h * CH:(h + 1) * CH],
                             vt_t[:, h * DK:(h + 1) * DK], start=True, stop=False)
                    e.matmul(dsl,
                             f_M[p0:p0 + DK, fc * CH:(fc + 1) * CH],
                             c0m[p0:p0 + DK, fc * DK:(fc + 1) * DK],
                             start=False, stop=True, tile_position=(p0, 0))
            stage("tensor", dmm)

            def dneg_ev(e, hg=hg):
                for hh in range(8):
                    h = hg * 8 + hh
                    e.tensor_scalar(dneg[:, h * DK:(h + 1) * DK],
                                    PB[2 + hh % 2][:, (hh // 2) * DK:(hh // 2 + 1) * DK],
                                    -1.0, None, ALU.mult)
            stage("vector", dneg_ev)

        emit_states(t_Kr, t_Qr, vt_t, is_A=True, htmp=xq1)

    for cc in range(na):
        emit_A(cc)

    aes.close()

    bes = ExitStack()

    def bsb(name, shape, dt=F32):
        return bes.enter_context(nc.sbuf_tensor(name, shape, dt))

    Wosb = bsb("Wosb", [CH, FC * D], BF16)
    xB_t = bsb("xB_t", [CH, 4 * D])
    a2f = bsb("a2f", [CH, 4 * D])
    bsq = bsb("bsq", [CH, D])
    attn_f = bsb("attn_f", [CH, FC * TB], BF16)
    cf_q = bsb("cf_q", [CH, D], BF16)
    cf_k = bsb("cf_k", [CH, D], BF16)
    t_q = bsb("t_q", [CH, D], BF16)
    t_k = bsb("t_k", [CH, D], BF16)
    t_v = bsb("t_v", [CH, D], BF16)
    t_lA = bsb("t_lA", [CH, D])
    t_lB = bsb("t_lB", [CH, D])
    f_U = bsb("f_U", [CH, D], BF16)
    f_U0 = bsb("f_U0", [CH, D], BF16)
    f_Kh = bsb("f_Kh", [CH, D], BF16)
    f_F1 = bsb("f_F1", [CH, D], BF16)
    f_F2 = bsb("f_F2", [CH, D], BF16)
    t_Kh = bsb("t_Kh", [CH, D], BF16)
    ftmp = bsb("ftmp", [CH, D])
    ftmp2 = bsb("ftmp2", [CH, D])
    mS1 = bsb("mS1", [CH, 16 * CH], BF16)
    mS2 = bsb("mS2", [CH, 16 * CH], BF16)
    pt_f = bsb("pt_f", [CH, D], BF16)

    for hf in range(2):
        def wo_load(e, hf=hf):
            src = WHBM["o"].rearrange("(c p) o -> p c o", p=CH)
            e.dma_start(out=xB_t[:].rearrange("p (c o) -> p c o", c=4),
                        in_=src[:, 4 * hf:4 * hf + 4, :])
        stage("sync", wo_load, ndma=1)

        def wo_cast(e, hf=hf):
            e.tensor_copy(Wosb[:, hf * 4 * D:(hf + 1) * 4 * D], xB_t[:])
        stage("vector", wo_cast)

    EN["on"] = relay or dbg

    def pack_agg(e):
        e.tensor_copy(agg[:, 0:512], St_S[:])
        e.tensor_copy(agg[:, 512:1024], St_C[:])
        e.tensor_copy(agg[:, 1024:1536], St_G[:])
        e.tensor_copy(agg[:, 1536:2048], St_H[:])
        e.tensor_copy(agg[:, 2048:2056], avec[:])
        e.tensor_copy(agg[:, 2056:2064], bvec[:])
        e.memset(agg[:, 2064:2080], 0.0)
    stage("vector", pack_agg)

    def relay_out(e):
        e.dma_start(out=CC_IN[:, :], in_=agg[:, :])
    stage("gpsimd", relay_out, ndma=1)

    EN["on"] = relay

    def relay_cc(e):
        e.collective_compute(
            "AllGather", ALU.bypass,
            replica_groups=[[0, 1, 2, 3], [4, 5, 6, 7]],
            ins=[CC_IN[:, :]], outs=[CC_OUT[:, :]],
        )
    stage("gpsimd", relay_cc)

    EN["on"] = relay or nb > 0

    def comp_zero(e):
        e.memset(St_S[:], 0.0)
        e.memset(St_C[:], 0.0)
        e.memset(St_G[:], 0.0)
    stage("vector", comp_zero)

    EN["on"] = relay

    for q in range(3):
        def agg_load(e, q=q):
            e.dma_start(out=agg[:, :], in_=CC_OUT[q * CH:(q + 1) * CH, :])
        stage("sync", agg_load, ndma=1)

        def comp_pre(e, q=q):
            e.tensor_scalar(avec[:], agg[:, 2048:2056], psc[:, q:q + 1],
                            psc[:, 6 + q:7 + q], ALU.mult, ALU.add)
            e.tensor_scalar(bvec[:], agg[:, 2056:2064], psc[:, q:q + 1],
                            psc[:, 6 + q:7 + q], ALU.mult, ALU.add)
            e.tensor_copy(comp_bf[:], St_C[:])
            e.tensor_copy(comp_bf2[:], agg[:, 1536:2048])
        stage("vector", comp_pre)

        def comp_tr(e, q=q):
            for fc in range(FC):
                e.transpose(PB[fc // 4][0:1, (fc % 4) * CH:(fc % 4 + 1) * CH],
                            avec[:, fc:fc + 1], ident_fs[:, :])
        stage("tensor", comp_tr)

        def comp_trev(e, q=q):
            e.tensor_copy(rowscr[0:1, 0:512], PB[0][0:1, :])
            e.tensor_copy(rowscr[0:1, 512:1024], PB[1][0:1, :])
        stage("vector", comp_trev)

        def comp_mm(e, q=q):
            for h in range(H):
                par, fc = h % 2, h // 2
                p0, c0 = par * DK, fc * DK
                ar = rowscr[0:1, fc * CH + p0: fc * CH + p0 + DK]
                e.matmul(PB[1][p0:p0 + DK, c0:c0 + DK], ar, ar,
                         start=True, stop=True, tile_position=(0, p0))
                e.matmul(PB[3 + par][p0:p0 + DK, c0:c0 + DK],
                         comp_bf2[p0:p0 + DK, c0:c0 + DK],
                         comp_bf[p0:p0 + DK, c0:c0 + DK],
                         start=True, stop=True, tile_position=(p0, p0))
        stage("tensor", comp_mm)

        def comp_upd(e, q=q):
            sel = psc[:, q:q + 1]
            nsel = psc[:, 3 + q:4 + q]
            e.tensor_tensor(St_S[:], St_S[:], PB[1][:, :], ALU.mult)
            e.scalar_tensor_tensor(St_S[:], agg[:, 0:512], sel, St_S[:],
                                   ALU.mult, ALU.add)
            for fc in range(FC):
                s = slice(fc * DK, (fc + 1) * DK)
                e.tensor_scalar(St_G[:, s], St_G[:, s], avec[:, fc:fc + 1],
                                None, ALU.mult)
            e.scalar_tensor_tensor(St_G[0:DK, :], PB[3][0:DK, :],
                                   psc[0:DK, 3 + q:4 + q], St_G[0:DK, :],
                                   ALU.mult, ALU.add)
            e.scalar_tensor_tensor(St_G[DK:CH, :], PB[4][DK:CH, :],
                                   psc[DK:CH, 3 + q:4 + q], St_G[DK:CH, :],
                                   ALU.mult, ALU.add)
            e.scalar_tensor_tensor(St_G[:], agg[:, 1024:1536], sel, St_G[:],
                                   ALU.mult, ALU.add)
            for fc in range(FC):
                s = slice(fc * DK, (fc + 1) * DK)
                e.tensor_scalar(St_C[:, s], St_C[:, s], bvec[:, fc:fc + 1],
                                None, ALU.mult)
            e.scalar_tensor_tensor(St_C[:], agg[:, 512:1024], sel, St_C[:],
                                   ALU.mult, ALU.add)
        stage("vector", comp_upd)

    def emit_B(cc):
        r0 = cc * CH
        tc = cc % 4

        def bload(e, r0=r0):
            e.dma_start(out=t_q[:], in_=S_qt[r0:r0 + CH, :])
            e.dma_start(out=t_k[:], in_=S_kt[r0:r0 + CH, :])
            e.dma_start(out=t_v[:], in_=S_vt[r0:r0 + CH, :])
            e.dma_start(out=t_lA[:], in_=S_lA[r0:r0 + CH, :])
            e.dma_start(out=t_lB[:], in_=S_lB[r0:r0 + CH, :])
        stage("sync", bload, ndma=5)

        def cf_tr(e):
            e.dma_start_transpose(
                cf_q[:].rearrange("p (c t) -> p c t", c=FC), t_q[:])
            e.dma_start_transpose(
                cf_k[:].rearrange("p (c t) -> p c t", c=FC), t_k[:])
        stage("sync", cf_tr, ndma=2)

        emit_rows(lambda: t_lA[:], lambda: t_lB[:])

        def cums_f(e):
            for fc in range(FC):
                sl0 = PB[0 + fc // 4][:, (fc % 4) * CH:(fc % 4 + 1) * CH]
                sl1 = PB[2 + fc // 4][:, (fc % 4) * CH:(fc % 4 + 1) * CH]
                sl2 = PB[4 + fc // 4][:, (fc % 4) * CH:(fc % 4 + 1) * CH]
                e.matmul(sl0, t_lA[:, fc * CH:(fc + 1) * CH], cmask[:, CH:2 * CH],
                         start=True, stop=True)
                e.matmul(sl1, t_lB[:, fc * CH:(fc + 1) * CH], cmask[:, CH:2 * CH],
                         start=True, stop=True)
                e.matmul(sl2, t_lA[:, fc * CH:(fc + 1) * CH], cmask[:, 0:CH],
                         start=True, stop=True)
        stage("tensor", cums_f)

        def fargs(e):
            e.tensor_copy(bsq[:, 0:512], PB[2][:, :])
            e.tensor_copy(bsq[:, 512:1024], PB[3][:, :])
            e.drain()
            e.tensor_tensor(ftmp[:, 0:512], PB[4][:, :], bsq[:, 0:512], ALU.add)
            e.tensor_tensor(ftmp[:, 512:1024], PB[5][:, :], bsq[:, 512:1024], ALU.add)
            e.tensor_tensor(ftmp2[:, 0:512], PB[0][:, :], bsq[:, 0:512], ALU.add)
            e.tensor_tensor(ftmp2[:, 512:1024], PB[1][:, :], bsq[:, 512:1024], ALU.add)
        stage("vector", fargs)

        def exps_f(e):
            for fc in range(FC):
                sl = PB[fc // 4][:, (fc % 4) * CH:(fc % 4 + 1) * CH]
                sl2 = PB[2 + fc // 4][:, (fc % 4) * CH:(fc % 4 + 1) * CH]
                e.activation(f_U[:, fc * CH:(fc + 1) * CH], sl, AF.Exp, bias=0.0, scale=1.0)
                e.activation(f_Kh[:, fc * CH:(fc + 1) * CH], sl, AF.Exp, bias=0.0, scale=-1.0)
                e.activation(f_M[:, fc * CH:(fc + 1) * CH], sl2, AF.Exp, bias=0.0, scale=1.0)
                e.activation(f_Qh[:, fc * CH:(fc + 1) * CH], sl2, AF.Exp, bias=0.0, scale=-1.0)
            e.activation(f_F1[:, 0:512], ftmp[:, 0:512], AF.Exp, bias=0.0, scale=1.0)
            e.activation(f_F1[:, 512:1024], ftmp[:, 512:1024], AF.Exp, bias=0.0, scale=1.0)
            e.activation(f_F2[:, 0:512], ftmp2[:, 0:512], AF.Exp, bias=0.0, scale=1.0)
            e.activation(f_F2[:, 512:1024], ftmp2[:, 512:1024], AF.Exp, bias=0.0, scale=1.0)
        stage("scalar", exps_f)

        def cums_t(e):
            e.matmul(PB[0][:, :], cmask[:, CH:2 * CH], t_lA[:, 0:512], start=True, stop=True)
            e.matmul(PB[1][:, :], cmask[:, CH:2 * CH], t_lA[:, 512:1024], start=True, stop=True)
            e.matmul(PB[2][:, :], cmask[:, 2 * CH:3 * CH], t_lA[:, 0:512], start=True, stop=True)
            e.matmul(PB[3][:, :], cmask[:, 2 * CH:3 * CH], t_lA[:, 512:1024], start=True, stop=True)
            e.matmul(PB[4][:, :], cmask[:, 2 * CH:3 * CH], t_lB[:, 0:512], start=True, stop=True)
            e.matmul(PB[5][:, :], cmask[:, 2 * CH:3 * CH], t_lB[:, 512:1024], start=True, stop=True)
        stage("tensor", cums_t)

        def exps_t(e):
            e.activation(t_Kh[:, 0:512], PB[0][:, :], AF.Exp, bias=0.0, scale=-1.0)
            e.activation(t_Kh[:, 512:1024], PB[1][:, :], AF.Exp, bias=0.0, scale=-1.0)
            e.activation(t_Kr[:, 0:512], PB[2][:, :], AF.Exp, bias=0.0, scale=1.0)
            e.activation(t_Kr[:, 512:1024], PB[3][:, :], AF.Exp, bias=0.0, scale=1.0)
            e.activation(t_Qr[:, 0:512], PB[4][:, :], AF.Exp, bias=0.0, scale=1.0)
            e.activation(t_Qr[:, 512:1024], PB[5][:, :], AF.Exp, bias=0.0, scale=1.0)
        stage("scalar", exps_t)

        def facB(e):
            e.tensor_tensor(f_U[:], cf_q[:], f_U[:], ALU.mult)
            e.tensor_tensor(f_Kh[:], cf_k[:], f_Kh[:], ALU.mult)
            e.tensor_tensor(f_M[:], cf_k[:], f_M[:], ALU.mult)
            e.tensor_tensor(f_Qh[:], cf_q[:], f_Qh[:], ALU.mult)
            e.tensor_tensor(t_Kh[:], t_k[:], t_Kh[:], ALU.mult)
            e.tensor_tensor(t_Kr[:], t_k[:], t_Kr[:], ALU.mult)
            e.tensor_tensor(t_Qr[:], t_q[:], t_Qr[:], ALU.mult)
            for fc in range(FC):
                e.tensor_scalar(c0m[:, fc * DK:(fc + 1) * DK],
                                St_C[:, fc * DK:(fc + 1) * DK],
                                gcols[:, fc * 4 + 3: fc * 4 + 4], None, ALU.mult)
            e.tensor_copy(Sb_S[:], St_S[:])
            e.tensor_copy(Sb_G[:], St_G[:])
            e.drain()
            for fc in range(FC):
                e.tensor_scalar(f_U0[:, fc * CH:(fc + 1) * CH],
                                f_U[:, fc * CH:(fc + 1) * CH],
                                gcols[:, fc * 4 + 2: fc * 4 + 3], None, ALU.mult)
        stage("vector", facB)

        for hg in range(2):
            def s23(e, hg=hg):
                for hh in range(8):
                    h = hg * 8 + hh
                    par, fc = h % 2, h // 2
                    p0 = par * DK
                    kh = f_Kh[p0:p0 + DK, fc * CH:(fc + 1) * CH]
                    uu = f_U[p0:p0 + DK, fc * CH:(fc + 1) * CH]
                    qh = f_Qh[p0:p0 + DK, fc * CH:(fc + 1) * CH]
                    mm = f_M[p0:p0 + DK, fc * CH:(fc + 1) * CH]
                    csl = (hh // 2) * CH
                    e.matmul(PB[par][:, csl:csl + CH], kh, uu,
                             start=True, stop=True, tile_position=(p0, 0))
                    e.matmul(PB[2 + par][:, csl:csl + CH], qh, mm,
                             start=True, stop=True, tile_position=(p0, 0))
            stage("tensor", s23)

            def s23mask(e, hg=hg):
                for hh in range(8):
                    h = hg * 8 + hh
                    csl = (hh // 2) * CH
                    e.tensor_tensor(mS2[:, h * CH:(h + 1) * CH],
                                    PB[hh % 2][:, csl:csl + CH],
                                    smasks[:, 0:CH], ALU.mult)
                    e.tensor_tensor(mS3[:, h * CH:(h + 1) * CH],
                                    PB[2 + hh % 2][:, csl:csl + CH],
                                    smasks[:, 4 * CH:5 * CH], ALU.mult)
            stage("vector", s23mask)

            def pmm(e, hg=hg):
                for hh in range(8):
                    h = hg * 8 + hh
                    par, fc = h % 2, h // 2
                    p0 = par * DK
                    csl = (hh // 2) * CH
                    e.matmul(PB[4 + 2 * par][p0:p0 + DK, csl:csl + CH],
                             Sb_S[p0:p0 + DK, fc * DK:(fc + 1) * DK],
                             f_U0[p0:p0 + DK, fc * CH:(fc + 1) * CH],
                             start=True, stop=True, tile_position=(p0, p0))
                    e.matmul(PB[5][p0:p0 + DK, csl:csl + CH],
                             t_Kh[:, h * DK:(h + 1) * DK],
                             mS2[:, h * CH:(h + 1) * CH],
                             start=True, stop=True, tile_position=(0, p0))
            stage("tensor", pmm)

            def ptilde(e, hg=hg):
                o = hg * 512
                e.tensor_tensor(ftmp[0:DK, 0:512], f_F1[0:DK, o:o + 512],
                                PB[4][0:DK, :], ALU.mult)
                e.tensor_tensor(ftmp[DK:CH, 0:512], f_F1[DK:CH, o:o + 512],
                                PB[6][DK:CH, :], ALU.mult)
                e.tensor_tensor(ftmp2[:, 0:512], f_F2[:, o:o + 512], PB[5][:, :],
                                ALU.mult)
                e.drain()
                e.tensor_tensor(pt_f[:, o:o + 512], ftmp[:, 0:512],
                                ftmp2[:, 0:512], ALU.add)
            stage("vector", ptilde)

            def s1mm(e, hg=hg):
                for hh in range(8):
                    h = hg * 8 + hh
                    par, fc = h % 2, h // 2
                    p0 = par * DK
                    qh = f_Qh[p0:p0 + DK, fc * CH:(fc + 1) * CH]
                    pt = pt_f[p0:p0 + DK, fc * CH:(fc + 1) * CH]
                    csl = (hh // 2) * CH
                    e.matmul(PB[par][:, csl:csl + CH], qh, pt,
                             start=True, stop=True, tile_position=(p0, 0))
            stage("tensor", s1mm)

            def s1mask(e, hg=hg):
                for hh in range(8):
                    h = hg * 8 + hh
                    csl = (hh // 2) * CH
                    e.tensor_tensor(mS1[:, h * CH:(h + 1) * CH],
                                    PB[hh % 2][:, csl:csl + CH],
                                    smasks[:, 0:CH], ALU.mult)
            stage("vector", s1mask)

            def dmmB(e, hg=hg):
                for hh in range(8):
                    h = hg * 8 + hh
                    par, fc = h % 2, h // 2
                    p0 = par * DK
                    dsl = PB[2 + par][:, (hh // 2) * DK:(hh // 2 + 1) * DK]
                    e.matmul(dsl, mS3[:, h * CH:(h + 1) * CH],
                             t_v[:, h * DK:(h + 1) * DK], start=True, stop=False)
                    e.matmul(dsl,
                             f_M[p0:p0 + DK, fc * CH:(fc + 1) * CH],
                             c0m[p0:p0 + DK, fc * DK:(fc + 1) * DK],
                             start=False, stop=True, tile_position=(p0, 0))
            stage("tensor", dmmB)

            def dnegB(e, hg=hg):
                for hh in range(8):
                    h = hg * 8 + hh
                    e.tensor_scalar(dneg[:, h * DK:(h + 1) * DK],
                                    PB[2 + hh % 2][:, (hh // 2) * DK:(hh // 2 + 1) * DK],
                                    -1.0, None, ALU.mult)
            stage("vector", dnegB)

            def omm(e, hg=hg):
                for hh in range(8):
                    h = hg * 8 + hh
                    par, fc = h % 2, h // 2
                    p0 = par * DK
                    osl = PB[6 + par][p0:p0 + DK, (hh // 2) * CH:(hh // 2 + 1) * CH]
                    e.matmul(osl, t_v[:, h * DK:(h + 1) * DK],
                             mS1[:, h * CH:(h + 1) * CH], start=True, stop=False,
                             tile_position=(0, p0))
                    e.matmul(osl, c0m[p0:p0 + DK, fc * DK:(fc + 1) * DK],
                             pt_f[p0:p0 + DK, fc * CH:(fc + 1) * CH],
                             start=False, stop=False, tile_position=(p0, p0))
                    e.matmul(osl, Sb_G[p0:p0 + DK, fc * DK:(fc + 1) * DK],
                             f_U0[p0:p0 + DK, fc * CH:(fc + 1) * CH],
                             start=False, stop=False, tile_position=(p0, p0))
                    e.matmul(osl, dneg[:, h * DK:(h + 1) * DK],
                             mS2[:, h * CH:(h + 1) * CH], start=False, stop=True,
                             tile_position=(0, p0))
            stage("tensor", omm)

            def oev(e, hg=hg, tc=tc):
                e.tensor_copy(
                    attn_f[0:DK, :].rearrange("p (c n) -> p c n", c=FC)[
                        :, hg * 4:(hg + 1) * 4, tc * CH:(tc + 1) * CH],
                    PB[6][0:DK, :].rearrange("p (c n) -> p c n", c=4))
                e.tensor_copy(
                    attn_f[DK:CH, :].rearrange("p (c n) -> p c n", c=FC)[
                        :, hg * 4:(hg + 1) * 4, tc * CH:(tc + 1) * CH],
                    PB[7][DK:CH, :].rearrange("p (c n) -> p c n", c=4))
            stage("vector", oev)

        emit_states(t_Kr, t_Qr, t_v, is_A=False)

    def emit_B_epilogue(t):
        r0 = t * TB

        def xload(e, r0=r0):
            e.dma_start(out=xB_t[:].rearrange("p (s o) -> p s o", s=4),
                        in_=x_in[r0:r0 + TB, :].rearrange("(s p) o -> p s o", p=CH))
        stage("sync", xload, ndma=1)

        def womm(e):
            for s in range(4):
                for half in range(2):
                    for pc in range(FC):
                        e.matmul(PB[s * 2 + half][:, :],
                                 attn_f[:, pc * TB + s * CH: pc * TB + (s + 1) * CH],
                                 Wosb[:, pc * D + half * 512: pc * D + (half + 1) * 512],
                                 start=(pc == 0), stop=(pc == FC - 1))
        stage("tensor", womm)

        def ln2_copy(e):
            for s in range(4):
                for half in range(2):
                    e.tensor_copy(a2f[:, s * D + half * 512: s * D + (half + 1) * 512],
                                  PB[s * 2 + half][:, :])
        stage("vector", ln2_copy)

        emit_ln_tmajor(lambda s: a2f[:, s * D:(s + 1) * D], 4, bsq)

        def resid(e):
            for s in range(4):
                sl = slice(s * D, (s + 1) * D)
                e.scalar_tensor_tensor(xB_t[:, sl], a2f[:, sl], rcol[:, s:s + 1],
                                       xB_t[:, sl], ALU.mult, ALU.add)
            e.drain()
            for s in range(4):
                sl = slice(s * D, (s + 1) * D)
                e.tensor_scalar(xB_t[:, sl], xB_t[:, sl], nmcol[:, s:s + 1],
                                None, ALU.add)
        stage("vector", resid)

        def xpstore(e, r0=r0):
            e.dma_start(out=S_xp[r0:r0 + TB, :].rearrange("(s p) o -> p s o", p=CH),
                        in_=xB_t[:].rearrange("p (s o) -> p s o", s=4))
        stage("gpsimd", xpstore, ndma=1)

    EN["on"] = True
    for t in range(nb // 4):
        for tc in range(4):
            emit_B(t * 4 + tc)
        emit_B_epilogue(t)

    bes.close()

    ces = ExitStack()

    def csb(name, shape, dt=F32):
        return ces.enter_context(nc.sbuf_tensor(name, shape, dt))

    fstage = csb("fstage", [CH, INNER])
    fint = csb("fint", [CH, INNER], I32)
    fbf = csb("fbf", [CH, INNER], BF16)
    wf_acc = csb("wf_acc", [CH, 32])
    wcols = csb("wcols", [CH, 8])
    scol1 = csb("scol1", [CH, 1])
    xp_t = csb("xp_t", [CH, 4 * D])
    csq = csb("csq", [CH, D])
    x3_t = csb("x3_t", [CH, 4 * D], BF16)
    x3q_t = csb("x3q_t", [CH, 4 * D], BF16)
    x3q_f = csb("x3q_f", [CH, FC * GRP], BF16)
    gint = csb("gint", [CH, D], I32)
    mid = csb("mid", [CH, 32 * GRP], BF16)
    tern_sb = csb("tern_sb", [CH, 4 * D], BF16)
    ternd_sb = csb("ternd_sb", [CH, D], BF16)
    gate_sb = csb("gate_sb", [CH, 2 * GRP], BF16)
    maxacc = fstage
    ffn_t = x3_t
    absm = csb("absm", [CH, GRP])
    maxcol = csb("maxcol", [CH, 4])
    qcol = csb("qcol", [CH, 4])
    iqcol = csb("iqcol", [CH, 4])

    def emit_tern(whbm, widx, nrows, ncols, store_swizzled, dst):
        nch = nrows // CH
        for rc in range(nch):
            def ml(e, rc=rc):
                e.dma_start(out=fstage[:, 0:ncols], in_=whbm[rc * CH:(rc + 1) * CH, :])
            stage("sync", ml, ndma=1)

            def mr(e, rc=rc):
                e.tensor_reduce(wf_acc[:, rc:rc + 1], fstage[:, 0:ncols],
                                AX.X, ALU.add, apply_absolute_value=True)
            stage("vector", mr)

        def mfin(e, nch=nch):
            e.tensor_reduce(scol1[:, 0:1], wf_acc[:, 0:nch], AX.X, ALU.add)
        stage("vector", mfin)

        def msum(e):
            e.matmul(PB[0][0:1, 0:1], onecol_fs[:], scol1[:, 0:1], start=True, stop=True)
        stage("tensor", msum)

        def mscal(e, nrows=nrows, ncols=ncols):
            e.tensor_scalar(rowscr[0:1, 0:1], PB[0][0:1, 0:1],
                            1.0 / (nrows * ncols), 1e-5, ALU.mult, ALU.max)
            e.drain()
            e.reciprocal(rowscr[0:1, 1:2], rowscr[0:1, 0:1])
        stage("vector", mscal)

        def mbc(e):
            e.matmul(PB[0][:, 0:2], ones_row_s[0:1, :], rowscr[0:1, 0:2],
                     start=True, stop=True)
        stage("tensor", mbc)

        def mbc_ev(e, widx=widx):
            e.tensor_copy(wcols[:, 3 + widx:4 + widx], PB[0][:, 0:1])
            e.tensor_copy(wcols[:, widx:widx + 1], PB[0][:, 1:2])
        stage("vector", mbc_ev)

        if store_swizzled:
            for ob in range(ncols // CH):
                def tl(e, ob=ob):
                    e.dma_start(
                        out=fstage[:, 0:FC * CH].rearrange("p (c n) -> p c n", c=FC),
                        in_=whbm.rearrange("(c p) n -> p c n", p=CH)[:, :, ob * CH:(ob + 1) * CH])
                stage("sync", tl, ndma=1)

                def tq(e, widx=widx):
                    e.tensor_scalar(fint[:, 0:FC * CH], fstage[:, 0:FC * CH],
                                    wcols[:, widx:widx + 1], None, ALU.mult)
                    e.drain()
                    e.tensor_scalar(fbf[:, 0:FC * CH], fint[:, 0:FC * CH],
                                    -1.0, 1.0, ALU.max, ALU.min)
                stage("vector", tq)

                def ts_(e, ob=ob):
                    e.dma_start(out=dst[:, ob * FC * CH:(ob + 1) * FC * CH],
                                in_=fbf[:, 0:FC * CH])
                stage("gpsimd", ts_, ndma=1)
        else:
            for rc in range(nch):
                def tl(e, rc=rc):
                    e.dma_start(out=fstage[:, 0:ncols], in_=whbm[rc * CH:(rc + 1) * CH, :])
                stage("sync", tl, ndma=1)

                def tq(e, widx=widx, ncols=ncols):
                    e.tensor_scalar(fint[:, 0:ncols], fstage[:, 0:ncols],
                                    wcols[:, widx:widx + 1], None, ALU.mult)
                    e.drain()
                    e.tensor_scalar(fbf[:, 0:ncols], fint[:, 0:ncols],
                                    -1.0, 1.0, ALU.max, ALU.min)
                stage("vector", tq)

                def ts_(e, rc=rc, ncols=ncols):
                    e.dma_start(out=dst[rc * CH:(rc + 1) * CH, :], in_=fbf[:, 0:ncols])
                stage("gpsimd", ts_, ndma=1)

    EN["on"] = tern
    emit_tern(wgT, 0, D, INNER, True, TERN_G)
    emit_tern(wuT, 1, D, INNER, True, TERN_U)
    emit_tern(wdT, 2, INNER, D, False, TERN_D)
    EN["on"] = True

    def emit_C(g):
        r0 = g * GRP

        def xpl(e, r0=r0):
            e.dma_start(out=xp_t[:].rearrange("p (s o) -> p s o", s=4),
                        in_=S_xp[r0:r0 + GRP, :].rearrange("(s p) o -> p s o", p=CH))
        stage("sync", xpl, ndma=1)

        def accz(e):
            e.memset(maxacc[:, 0:4 * GRP], 0.0)
        stage("vector", accz)

        emit_ln_tmajor(lambda s: xp_t[:, s * D:(s + 1) * D], 4, csq)

        def x3mk(e):
            for s in range(4):
                e.tensor_scalar(x3_t[:, s * D:(s + 1) * D],
                                xp_t[:, s * D:(s + 1) * D],
                                rcol[:, s:s + 1], nmcol[:, s:s + 1],
                                ALU.mult, ALU.add)
        stage("vector", x3mk)

        def qmax(e):
            for s in range(4):
                e.tensor_reduce(tcol[:, s:s + 1], x3_t[:, s * D:(s + 1) * D],
                                AX.X, ALU.max, apply_absolute_value=True)
            e.drain()
            e.tensor_scalar(tcol[:, 0:4], tcol[:, 0:4], 1e-5, None, ALU.max)
        stage("vector", qmax)

        def qscale(e):
            e.reciprocal(qcol[:, 0:4], tcol[:, 0:4])
            e.tensor_scalar(iqcol[:, 0:4], tcol[:, 0:4], 1.0 / 127.0, None, ALU.mult)
            e.drain()
            e.tensor_scalar(qcol[:, 0:4], qcol[:, 0:4], 127.0, None, ALU.mult)
        stage("vector", qscale)

        def xq(e):
            for s in range(4):
                sl = slice(s * D, (s + 1) * D)
                e.tensor_scalar(gint[:], x3_t[:, sl], qcol[:, s:s + 1], None, ALU.mult)
                e.drain()
                e.tensor_scalar(csq[:], gint[:], -128.0, 127.0, ALU.max, ALU.min)
                e.drain()
                e.tensor_scalar(x3q_t[:, sl], csq[:], iqcol[:, s:s + 1], None, ALU.mult)
                e.drain()
        stage("vector", xq)

        def x3tr(e):
            for s in range(4):
                e.dma_start_transpose(
                    x3q_f[:].rearrange("p (c s t) -> p c s t", c=FC, s=4)[:, :, s, :],
                    x3q_t[:, s * D:(s + 1) * D])
        stage("sync", x3tr, ndma=4)

        for ob in range(16):
            def ternl(e, ob=ob):
                e.dma_start(out=tern_sb[:, 0:D], in_=TERN_G[:, 2 * ob * D:(2 * ob + 1) * D])
                e.dma_start(out=tern_sb[:, D:2 * D], in_=TERN_G[:, (2 * ob + 1) * D:(2 * ob + 2) * D])
                e.dma_start(out=tern_sb[:, 2 * D:3 * D], in_=TERN_U[:, 2 * ob * D:(2 * ob + 1) * D])
                e.dma_start(out=tern_sb[:, 3 * D:4 * D], in_=TERN_U[:, (2 * ob + 1) * D:(2 * ob + 2) * D])
            stage("sync", ternl, ndma=4)

            def gumm(e, ob=ob):
                for j in range(2):
                    for dc in range(FC):
                        e.matmul(PB[j * 2][:, :],
                                 tern_sb[:, j * D + dc * CH: j * D + (dc + 1) * CH],
                                 x3q_f[:, dc * GRP:(dc + 1) * GRP],
                                 start=(dc == 0), stop=(dc == FC - 1))
                    for dc in range(FC):
                        e.matmul(PB[j * 2 + 1][:, :],
                                 tern_sb[:, (2 + j) * D + dc * CH: (2 + j) * D + (dc + 1) * CH],
                                 x3q_f[:, dc * GRP:(dc + 1) * GRP],
                                 start=(dc == 0), stop=(dc == FC - 1))
            stage("tensor", gumm)

            def silu_ev(e, ob=ob):
                e.activation(gate_sb[:, 0:GRP], PB[0][:, :], AF.Silu,
                             bias=0.0, scale=wcols[:, 3:4])
                e.activation(gate_sb[:, GRP:2 * GRP], PB[2][:, :], AF.Silu,
                             bias=0.0, scale=wcols[:, 3:4])
            stage("scalar", silu_ev)

            def midmk(e, ob=ob):
                for j in range(2):
                    oc = 2 * ob + j
                    e.scalar_tensor_tensor(
                        mid[:, oc * GRP:(oc + 1) * GRP],
                        PB[j * 2 + 1][:, :], wcols[:, 4:5],
                        gate_sb[:, j * GRP:(j + 1) * GRP],
                        ALU.mult, ALU.mult)
            stage("vector", midmk)

            def absupd(e, ob=ob):
                for j in range(2):
                    oc = 2 * ob + j
                    msl = mid[:, oc * GRP:(oc + 1) * GRP]
                    e.tensor_tensor(maxacc[:, 2 * j * GRP:(2 * j + 1) * GRP],
                                    maxacc[:, 2 * j * GRP:(2 * j + 1) * GRP],
                                    msl, ALU.max)
                    e.tensor_tensor(maxacc[:, (2 * j + 1) * GRP:(2 * j + 2) * GRP],
                                    maxacc[:, (2 * j + 1) * GRP:(2 * j + 2) * GRP],
                                    msl, ALU.min)
            stage("vector", absupd)

        def mmax(e):
            e.tensor_tensor(absm[:, 0:GRP], maxacc[:, 0:GRP],
                            maxacc[:, 2 * GRP:3 * GRP], ALU.max)
            e.tensor_tensor(csq[:, 0:GRP], maxacc[:, GRP:2 * GRP],
                            maxacc[:, 3 * GRP:4 * GRP], ALU.min)
            e.drain()
            e.tensor_scalar(csq[:, 0:GRP], csq[:, 0:GRP], -1.0, None, ALU.mult)
            e.drain()
            e.tensor_tensor(absm[:, 0:GRP], absm[:, 0:GRP], csq[:, 0:GRP], ALU.max)
        stage("vector", mmax)

        def mmtr(e):
            for tb in range(4):
                e.transpose(PB[0][:, tb * CH:(tb + 1) * CH],
                            absm[:, tb * CH:(tb + 1) * CH], ident_fs[:, :])
        stage("tensor", mmtr)

        def mmred(e):
            for tb in range(4):
                e.tensor_reduce(maxcol[:, tb:tb + 1], PB[0][:, tb * CH:(tb + 1) * CH],
                                AX.X, ALU.max)
            e.drain()
            e.tensor_scalar(maxcol[:, 0:4], maxcol[:, 0:4], 1e-5, None, ALU.max)
        stage("vector", mmred)

        def mctr(e):
            for tb in range(4):
                e.transpose(PB[1][0:1, tb * CH:(tb + 1) * CH],
                            maxcol[:, tb:tb + 1], ident_fs[:, :])
        stage("tensor", mctr)

        def mcev(e):
            e.tensor_copy(rowscr[0:1, 0:512], PB[1][0:1, :])
            e.drain()
            e.reciprocal(rowscr[0:1, 512:1024], rowscr[0:1, 0:512])
            e.tensor_scalar(rowscr[0:1, 1024:1536], rowscr[0:1, 0:512],
                            1.0 / 127.0, None, ALU.mult)
            e.drain()
            e.tensor_scalar(rowscr[0:1, 512:1024], rowscr[0:1, 512:1024],
                            127.0, None, ALU.mult)
        stage("vector", mcev)

        def mqbc(e):
            e.matmul(PB[2][:, :], ones_row_s[0:1, :], rowscr[0:1, 512:1024],
                     start=True, stop=True)
            e.matmul(PB[3][:, :], ones_row_s[0:1, :], rowscr[0:1, 1024:1536],
                     start=True, stop=True)
        stage("tensor", mqbc)

        def mq(e):
            for ob in range(16):
                for j in range(2):
                    oc = 2 * ob + j
                    e.tensor_tensor(gint[:, j * GRP:(j + 1) * GRP],
                                    mid[:, oc * GRP:(oc + 1) * GRP],
                                    PB[2][:, :], ALU.mult)
                e.drain()
                for j in range(2):
                    oc = 2 * ob + j
                    e.tensor_scalar(mid[:, oc * GRP:(oc + 1) * GRP],
                                    gint[:, j * GRP:(j + 1) * GRP],
                                    -128.0, 127.0, ALU.max, ALU.min)
                e.drain()
        stage("vector", mq)

        for ic in range(32):
            def dtl(e, ic=ic):
                e.dma_start(out=ternd_sb[:], in_=TERN_D[ic * CH:(ic + 1) * CH, :])
            stage("sync", dtl, ndma=1)

            def dmm(e, ic=ic):
                for s in range(4):
                    for half in range(2):
                        e.matmul(PB[s * 2 + half][:, :],
                                 mid[:, ic * GRP + s * CH: ic * GRP + (s + 1) * CH],
                                 ternd_sb[:, half * 512:(half + 1) * 512],
                                 start=(ic == 0), stop=(ic == 31),
                                 skip_group_check=True)
            stage("tensor", dmm)

        def dev(e):
            for s in range(4):
                for half in range(2):
                    e.tensor_copy(ffn_t[:, s * D + half * 512: s * D + (half + 1) * 512],
                                  PB[s * 2 + half][:, :])
        stage("vector", dev)

        emit_ln_tmajor(lambda s: ffn_t[:, s * D:(s + 1) * D], 4, csq)

        def resid4(e):
            for s in range(4):
                sl = slice(s * D, (s + 1) * D)
                e.scalar_tensor_tensor(xp_t[:, sl], ffn_t[:, sl], rcol[:, s:s + 1],
                                       xp_t[:, sl], ALU.mult, ALU.add)
            e.drain()
            for s in range(4):
                sl = slice(s * D, (s + 1) * D)
                e.tensor_scalar(xp_t[:, sl], xp_t[:, sl], nmcol[:, s:s + 1],
                                None, ALU.add)
        stage("vector", resid4)

        def ostore(e, r0=r0):
            e.dma_start(out=out_t[r0:r0 + GRP, :].rearrange("(s p) o -> p s o", p=CH),
                        in_=xp_t[:].rearrange("p (s o) -> p s o", s=4))
        stage("gpsimd", ostore, ndma=1)

    for g in range(ngr):
        emit_C(g)

    ces.close()

    stage("gpsimd", lambda e: None)

    if nstop is not None:
        del SEGS[nstop:]
        SEGS.append(("gpsimd", lambda e: None, 0, 0))

    contrib = [16 * nd + ex + 1 for (_, _, nd, ex) in SEGS]
    cum = [0]
    for c in contrib:
        cum.append(cum[-1] + c)
    globals()["LAST_NSEGS"] = len(SEGS)
    globals()["LAST_CUMTOT"] = cum[-1]

    class Proxy:
        def __init__(self, eng, sem):
            self._eng = eng
            self._sem = sem
            self.nd = 0
            self.last = None

        def dma_start(self, *a, **k):
            ins = self._eng.dma_start(*a, **k)
            ins.then_inc(self._sem, 16)
            self.nd += 1
            return ins

        def dma_start_transpose(self, *a, **k):
            ins = self._eng.dma_start_transpose(*a, **k)
            ins.then_inc(self._sem, 16)
            self.nd += 1
            return ins

        def __getattr__(self, n):
            f = getattr(self._eng, n)
            if callable(f):
                def wrap(*a, **k):
                    r = f(*a, **k)
                    if hasattr(r, "then_inc"):
                        self.last = r
                    return r
                return wrap
            return f

    with nc.Block() as block:
        def runner(ename):
            def f(eng):
                for si, (en, cb, nd, ex) in enumerate(SEGS):
                    if en != ename:
                        continue
                    if cum[si] > 0:
                        eng.wait_ge(SB, cum[si])
                    p = Proxy(eng, SB)
                    cb(p)
                    assert p.nd == nd, (si, ename, p.nd, nd)
                    if ename == "tensor":
                        assert p.last is not None, si
                        p.last.then_inc(SB, 1)
                    elif ename in ("vector", "scalar") and p.last is not None:
                        eng.drain().then_inc(SB, 1)
                    elif p.last is not None:
                        p.last.then_inc(SB, 1)
                    else:
                        eng.nop().then_inc(SB, 1)
            return f

        block.sync(runner("sync"))
        block.tensor(runner("tensor"))
        block.vector(runner("vector"))
        block.scalar(runner("scalar"))
        block.gpsimd(runner("gpsimd"))

    ges.close()
    return nc


_NC_CACHE = {}


def _consts():
    r = np.arange(CH)[:, None]
    c = np.arange(CH)[None, :]
    incl = (r <= c).astype(np.float32)
    rec = ((r <= 63).astype(np.float32) - incl)
    rear_neg = -(r > c).astype(np.float32)
    strict = (r < c).astype(np.float32)
    o = {}
    o["m_incl_neg"] = -incl
    o["m_rec"] = rec
    o["m_rear_neg"] = rear_neg
    o["sm_incl4"] = np.tile(incl, (1, 4)).astype(BF).view(np.uint16)
    o["sm_strict4"] = np.tile(strict, (1, 4)).astype(BF).view(np.uint16)
    o["ident_f"] = np.eye(CH, dtype=np.float32)
    o["ident_b"] = np.eye(CH, dtype=np.float32).astype(BF).view(np.uint16)
    o["ones_row"] = np.ones((1, CH), np.float32)
    o["onecol_f"] = np.ones((CH, 1), np.float32)
    o["negcol_f"] = -np.ones((CH, 1), np.float32)
    return o


def kernel(L0_attn_x, L0_attn_W_q, L0_attn_W_k, L0_attn_W_v, L0_attn_Wg_K,
           L0_attn_bg_K, L0_attn_Wg_C, L0_attn_bg_C, L0_attn_W_o,
           L0_ffn_w_gate, L0_ffn_w_up, L0_ffn_w_down):
    x = np.ascontiguousarray(np.asarray(L0_attn_x, np.float32))
    B, T, _ = x.shape
    if "nc" not in _NC_CACHE:
        _NC_CACHE["nc"] = build_program()
    nc = _NC_CACHE["nc"]

    shared = {
        "Wq": np.ascontiguousarray(np.asarray(L0_attn_W_q, np.float32)),
        "Wk": np.ascontiguousarray(np.asarray(L0_attn_W_k, np.float32)),
        "Wv": np.ascontiguousarray(np.asarray(L0_attn_W_v, np.float32)),
        "WgK": np.ascontiguousarray(np.asarray(L0_attn_Wg_K, np.float32)),
        "WgC": np.ascontiguousarray(np.asarray(L0_attn_Wg_C, np.float32)),
        "Wo": np.ascontiguousarray(np.asarray(L0_attn_W_o, np.float32)),
        "bgK": np.ascontiguousarray(np.asarray(L0_attn_bg_K, np.float32)).reshape(1, D),
        "bgC": np.ascontiguousarray(np.asarray(L0_attn_bg_C, np.float32)).reshape(1, D),
        "wgT": np.ascontiguousarray(np.asarray(L0_ffn_w_gate, np.float32).T),
        "wuT": np.ascontiguousarray(np.asarray(L0_ffn_w_up, np.float32).T),
        "wdT": np.ascontiguousarray(np.asarray(L0_ffn_w_down, np.float32).T),
        **_consts(),
    }
    in_maps = []
    for core in range(8):
        b, pos = core // 4, core % 4
        sel = np.zeros((CH, 9), np.float32)
        for q in range(3):
            sel[:, 6 + q] = 1.0
            if q < pos:
                sel[:, q] = 1.0
                sel[:, 3 + q] = -1.0
                sel[:, 6 + q] = 0.0
        m = dict(shared)
        m["x_in"] = np.ascontiguousarray(x[b, pos * NT:(pos + 1) * NT, :])
        m["predsel"] = sel
        in_maps.append(m)

    res = run_bass_kernel_spmd(nc, in_maps, core_ids=list(range(8)))
    out = np.empty((B, T, D), np.float32)
    for core in range(8):
        b, pos = core // 4, core % 4
        out[b, pos * NT:(pos + 1) * NT, :] = res.results[core]["out_t"]
    return out


if __name__ == "__main__":
    nc = build_program()
    print("built ok")


# version 13
# speedup vs baseline: 1.0148x; 1.0148x over previous
import numpy as np
import ml_dtypes
import concourse.bass as bass
import concourse.mybir as mybir
from concourse.bass_utils import run_bass_kernel_spmd

F32 = mybir.dt.float32
BF16 = mybir.dt.bfloat16
I32 = mybir.dt.int32
AF = mybir.ActivationFunctionType
ALU = mybir.AluOpType
AX = mybir.AxisListType

D = 1024
H = 16
DK = 64
NT = 4096
CH = 128
NCH = 32
FC = 8
TB = 512
NTB = 8
GRP = 512
NGRP = 8
INNER = 4096
EPS = 1e-5
BF = ml_dtypes.bfloat16


def build_program(na=NCH, relay=True, nb=NCH, tern=True, ngr=NGRP, dbg=False,
                  nstop=None):
    nc = bass.Bass("TRN2", target_bir_lowering=False)

    def din(name, shape, dt=F32):
        return nc.dram_tensor(name, shape, dt, kind="ExternalInput").ap()

    x_in = din("x_in", [NT, D])
    WHBM = {w: din("W" + w, [D, D]) for w in ["q", "k", "v", "gK", "gC", "o"]}
    bgK = din("bgK", [1, D])
    bgC = din("bgC", [1, D])
    wgT = din("wgT", [D, INNER])
    wuT = din("wuT", [D, INNER])
    wdT = din("wdT", [INNER, D])
    m_incl_neg = din("m_incl_neg", [CH, CH])
    m_rec = din("m_rec", [CH, CH])
    m_rear_neg = din("m_rear_neg", [CH, CH])
    sm_incl4 = din("sm_incl4", [CH, 4 * CH], BF16)
    sm_strict4 = din("sm_strict4", [CH, 4 * CH], BF16)
    ident_f = din("ident_f", [CH, CH])
    ident_b = din("ident_b", [CH, CH], BF16)
    ones_row = din("ones_row", [1, CH])
    onecol_f = din("onecol_f", [CH, 1])
    negcol_f = din("negcol_f", [CH, 1])
    predsel = din("predsel", [CH, 9])

    out_t = nc.dram_tensor("out_t", [NT, D], F32, kind="ExternalOutput").ap()

    def scr(name, shape, dt):
        kind = "ExternalOutput" if dbg else "Internal"
        return nc.dram_tensor(name, shape, dt, kind=kind).ap()

    S_qt = scr("S_qt", [NT, D], BF16)
    S_kt = scr("S_kt", [NT, D], BF16)
    S_vt = scr("S_vt", [NT, D], BF16)
    S_lA = scr("S_lA", [NT, D], F32)
    S_lB = scr("S_lB", [NT, D], F32)
    S_xp = scr("S_xp", [NT, D], F32)
    TERN_G = scr("TERN_G", [CH, 32 * 8 * CH], BF16)
    TERN_U = scr("TERN_U", [CH, 32 * 8 * CH], BF16)
    TERN_D = scr("TERN_D", [INNER, D], BF16)
    cc_kind = "ExternalOutput" if (dbg and not relay) else "Internal"
    CC_IN = nc.dram_tensor("CC_IN", [CH, 2080], F32, kind=cc_kind).ap()
    CC_OUT = nc.dram_tensor("CC_OUT", [4 * CH, 2080], F32).ap()

    from contextlib import ExitStack
    ges = ExitStack()

    def gsb(name, shape, dt=F32):
        return ges.enter_context(nc.sbuf_tensor(name, shape, dt))

    cmask = gsb("cmask", [CH, 3 * CH])
    smasks = gsb("smasks", [CH, 8 * CH], BF16)
    ident_fs = gsb("ident_fs", [CH, CH])
    ident_bs = gsb("ident_bs", [CH, CH], BF16)
    ones_row_s = gsb("ones_row_s", [1, CH])
    onecol_fs = gsb("onecol_fs", [CH, 1])
    negcol_fs = gsb("negcol_fs", [CH, 1])
    psc = gsb("psc", [CH, 9])
    rowscr = gsb("rowscr", [1, 2 * D])
    rowsf = gsb("rowsf", [1, 4 * D])
    gcols = gsb("gcols", [CH, FC * 4])
    St_S = gsb("St_S", [CH, FC * DK])
    St_C = gsb("St_C", [CH, FC * DK])
    St_G = gsb("St_G", [CH, FC * DK])
    St_H = gsb("St_H", [CH, FC * DK])
    tmp_st = gsb("tmp_st", [CH, FC * DK])
    avec = gsb("avec", [CH, FC])
    bvec = gsb("bvec", [CH, FC])
    Sb_S = gsb("Sb_S", [CH, FC * DK], BF16)
    Sb_G = gsb("Sb_G", [CH, FC * DK], BF16)
    c0m = gsb("c0m", [CH, FC * DK], BF16)
    comp_bf = gsb("comp_bf", [CH, FC * DK], BF16)
    comp_bf2 = gsb("comp_bf2", [CH, FC * DK], BF16)
    agg = gsb("agg", [CH, 2080])
    mcol = gsb("mcol", [CH, 4])
    vcol = gsb("vcol", [CH, 4])
    rcol = gsb("rcol", [CH, 4])
    nmcol = gsb("nmcol", [CH, 4])
    tcol = gsb("tcol", [CH, 4])
    ncol = gsb("ncol", [CH, 4])
    f_M = gsb("f_M", [CH, D], BF16)
    f_Qh = gsb("f_Qh", [CH, D], BF16)
    t_Kr = gsb("t_Kr", [CH, D], BF16)
    t_Qr = gsb("t_Qr", [CH, D], BF16)
    t_Mt = gsb("t_Mt", [CH, D], BF16)
    mS3 = gsb("mS3", [CH, 16 * CH], BF16)
    dneg = gsb("dneg", [CH, 16 * DK], BF16)

    PB = [ges.enter_context(nc.psum_tensor(f"PB{i}", [CH, 512], F32))
          for i in range(8)]
    SB = ges.enter_context(nc.semaphore("SB"))

    SEGS = []
    EN = {"on": True}

    def stage(engine, cb, ndma=0, extra=0, nodrain=False):
        if EN["on"]:
            SEGS.append((engine, cb, ndma, extra, nodrain))

    def emit_ln_tmajor(src_aps, nsl, sqbuf, w=D):
        def sums(e):
            for s in range(nsl):
                e.tensor_reduce(mcol[:, s:s + 1], src_aps(s), AX.X, ALU.add)
            for s in range(nsl):
                e.tensor_tensor(sqbuf[:, 0:w], src_aps(s), src_aps(s), ALU.mult)
                e.drain()
                e.tensor_reduce(vcol[:, s:s + 1], sqbuf[:, 0:w], AX.X, ALU.add)
                e.drain()
            e.tensor_tensor(tcol[:, 0:nsl], mcol[:, 0:nsl], mcol[:, 0:nsl], ALU.mult)
            e.tensor_scalar(ncol[:, 0:nsl], mcol[:, 0:nsl], -1.0 / w, None, ALU.mult)
            e.drain()
            e.scalar_tensor_tensor(vcol[:, 0:nsl], tcol[:, 0:nsl], -1.0 / w,
                                   vcol[:, 0:nsl], ALU.mult, ALU.add)
            e.drain()
            e.tensor_scalar(vcol[:, 0:nsl], vcol[:, 0:nsl], 1.0 / w, EPS,
                            ALU.mult, ALU.add)
        stage("vector", sums)

        def fln(e):
            e.activation(tcol[:, 0:nsl], vcol[:, 0:nsl], AF.Ln, bias=0.0, scale=1.0)
        stage("scalar", fln)

        def fexp(e):
            e.activation(rcol[:, 0:nsl], tcol[:, 0:nsl], AF.Exp, bias=0.0, scale=-0.5)
        stage("scalar", fexp, nodrain=True)

        def fin(e):
            e.tensor_tensor(nmcol[:, 0:nsl], ncol[:, 0:nsl], rcol[:, 0:nsl], ALU.mult)
        stage("vector", fin)

    def emit_rows(lA_ap, lB_ap):
        def rows_mm(e):
            e.matmul(PB[0][0:1, :], negcol_fs[:], lA_ap()[:, 0:512], start=True, stop=True)
            e.matmul(PB[1][0:1, :], negcol_fs[:], lA_ap()[:, 512:1024], start=True, stop=True)
            e.matmul(PB[2][0:1, :], negcol_fs[:], lB_ap()[:, 0:512], start=True, stop=True)
            e.matmul(PB[3][0:1, :], negcol_fs[:], lB_ap()[:, 512:1024], start=True, stop=True)
            e.matmul(PB[4][0:1, :], negcol_fs[0:DK, :], lA_ap()[0:DK, 0:512], start=True, stop=True)
            e.matmul(PB[5][0:1, :], negcol_fs[0:DK, :], lA_ap()[0:DK, 512:1024], start=True, stop=True)
            e.matmul(PB[6][0:1, :], negcol_fs[0:DK, :], lB_ap()[0:DK, 0:512], start=True, stop=True)
            e.matmul(PB[7][0:1, :], negcol_fs[0:DK, :], lB_ap()[0:DK, 512:1024], start=True, stop=True)
        stage("tensor", rows_mm)

        def rows_exp(e):
            for j, (b0, b1) in enumerate([(0, 1), (2, 3), (4, 5), (6, 7)]):
                e.activation(rowsf[0:1, j * D:j * D + 512], PB[b0][0:1, :],
                             AF.Exp, bias=0.0, scale=1.0)
                e.activation(rowsf[0:1, j * D + 512:(j + 1) * D], PB[b1][0:1, :],
                             AF.Exp, bias=0.0, scale=1.0)
        stage("scalar", rows_exp, nodrain=True)

        def rows_tr(e):
            for fc in range(FC):
                for j in range(4):
                    e.transpose(PB[0][:, fc * 4 + j: fc * 4 + j + 1],
                                rowsf[0:1, j * D + fc * CH: j * D + (fc + 1) * CH],
                                ident_fs[0:1, 0:1])
        stage("tensor", rows_tr)

        def rows_ev(e):
            e.tensor_copy(gcols[:, 0:32], PB[0][:, 0:32])
        stage("vector", rows_ev)

    def emit_states(kr_buf, qr_buf, v_buf, is_A, htmp=None):
        def stmm(e):
            for h in range(H):
                par, fc = h % 2, h // 2
                p0, c0 = par * DK, fc * DK
                kr = kr_buf[:, h * DK:(h + 1) * DK]
                qr = qr_buf[:, h * DK:(h + 1) * DK]
                vv = v_buf[:, h * DK:(h + 1) * DK]
                dn = dneg[:, h * DK:(h + 1) * DK]
                e.matmul(PB[0][p0:p0 + DK, c0:c0 + DK], kr, kr,
                         start=True, stop=True, tile_position=(0, p0))
                e.matmul(PB[1][p0:p0 + DK, c0:c0 + DK], qr, vv,
                         start=True, stop=True, tile_position=(0, p0))
                e.matmul(PB[2][p0:p0 + DK, c0:c0 + DK], kr, dn,
                         start=True, stop=True, tile_position=(0, p0))
                if is_A:
                    mt = t_Mt[:, h * DK:(h + 1) * DK]
                    e.matmul(PB[3][p0:p0 + DK, c0:c0 + DK], mt, kr,
                             start=True, stop=True, tile_position=(0, p0))
            for h in range(H):
                par, fc = h % 2, h // 2
                p0, c0 = par * DK, fc * DK
                grow = rowsf[0:1, fc * CH + par * DK: fc * CH + (par + 1) * DK]
                e.matmul(PB[4][p0:p0 + DK, c0:c0 + DK], grow, grow,
                         start=True, stop=True, tile_position=(0, p0))
                if is_A:
                    e.matmul(PB[5][p0:p0 + DK, c0:c0 + DK],
                             ones_row_s[0:1, 0:DK], grow,
                             start=True, stop=True, tile_position=(0, p0))
        stage("tensor", stmm)

        def stev(e):
            e.tensor_tensor(tmp_st[:], St_S[:], PB[4][:, :], ALU.mult)
            if is_A:
                e.tensor_tensor(htmp[:, 0:FC * DK], St_H[:], PB[5][:, :], ALU.mult)
            e.drain()
            e.tensor_tensor(St_S[:], tmp_st[:], PB[0][:, :], ALU.add)
            for fc in range(FC):
                s = slice(fc * DK, (fc + 1) * DK)
                e.scalar_tensor_tensor(St_C[:, s], St_C[:, s],
                                       gcols[:, fc * 4 + 1: fc * 4 + 2],
                                       PB[1][:, s], ALU.mult, ALU.add)
                e.scalar_tensor_tensor(St_G[:, s], St_G[:, s],
                                       gcols[:, fc * 4 + 0: fc * 4 + 1],
                                       PB[2][:, s], ALU.mult, ALU.add)
                if is_A:
                    e.scalar_tensor_tensor(St_H[:, s], PB[3][:, s],
                                           bvec[:, fc:fc + 1],
                                           htmp[:, s], ALU.mult, ALU.add)
            if is_A:
                for fc in range(FC):
                    e.tensor_tensor(avec[:, fc:fc + 1], avec[:, fc:fc + 1],
                                    gcols[:, fc * 4:fc * 4 + 1], ALU.mult)
                    e.tensor_tensor(bvec[:, fc:fc + 1], bvec[:, fc:fc + 1],
                                    gcols[:, fc * 4 + 1:fc * 4 + 2], ALU.mult)
        stage("vector", stev)

    aes = ExitStack()

    def asb(name, shape, dt=F32):
        return aes.enter_context(nc.sbuf_tensor(name, shape, dt))

    Wsb = {w: asb(f"Wsb_{w}", [CH, FC * D], BF16) for w in ["q", "k", "v", "gK", "gC"]}
    bg_bcast = asb("bg_bcast", [CH, 2 * D])
    xr_t = asb("xr_t", [CH, D])
    xq1 = asb("xq1", [CH, D])
    xhat_t = asb("xhat_t", [CH, D], BF16)
    xhat_f = asb("xhat_f", [CH, D], BF16)
    qt_t = asb("qt_t", [CH, D], BF16)
    kt_t = asb("kt_t", [CH, D], BF16)
    vt_t = asb("vt_t", [CH, D], BF16)
    lA_t = asb("lA_t", [CH, D])
    lB_t = asb("lB_t", [CH, D])
    qf = asb("qf", [CH, D], BF16)
    kf = asb("kf", [CH, D], BF16)

    def p0_loads(e):
        e.dma_start(out=cmask[:, 0:CH], in_=m_incl_neg)
        e.dma_start(out=cmask[:, CH:2 * CH], in_=m_rec)
        e.dma_start(out=cmask[:, 2 * CH:3 * CH], in_=m_rear_neg)
        e.dma_start(out=smasks[:, 0:4 * CH], in_=sm_incl4)
        e.dma_start(out=smasks[:, 4 * CH:8 * CH], in_=sm_strict4)
        e.dma_start(out=ident_fs[:], in_=ident_f)
        e.dma_start(out=ident_bs[:], in_=ident_b)
        e.dma_start(out=ones_row_s[:], in_=ones_row)
        e.dma_start(out=onecol_fs[:], in_=onecol_f)
        e.dma_start(out=negcol_fs[:], in_=negcol_f)
        e.dma_start(out=psc[:], in_=predsel)
        e.dma_start(out=rowscr[0:1, 0:D], in_=bgK)
        e.dma_start(out=rowscr[0:1, D:2 * D], in_=bgC)
    stage("sync", p0_loads, ndma=13)

    def zero_states(e):
        e.memset(St_S[:], 0.0)
        e.memset(St_C[:], 0.0)
        e.memset(St_G[:], 0.0)
        e.memset(St_H[:], 0.0)
        e.memset(avec[:], 1.0)
        e.memset(bvec[:], 1.0)
    stage("vector", zero_states, nodrain=True)

    def bg_mm(e):
        for i in range(4):
            e.matmul(PB[i][:, :], ones_row_s[0:1, :],
                     rowscr[0:1, i * 512:(i + 1) * 512], start=True, stop=True)
    stage("tensor", bg_mm)

    def bg_ev(e):
        for i in range(4):
            e.tensor_copy(bg_bcast[:, i * 512:(i + 1) * 512], PB[i][:, :])
    stage("vector", bg_ev)

    for wname in ["q", "k", "v", "gK", "gC"]:
        for hf in range(4):
            def wload(e, wname=wname, hf=hf):
                src = WHBM[wname].rearrange("(c p) o -> p c o", p=CH)
                e.dma_start(out=xr_t[:].rearrange("p (c o) -> p c o", c=1),
                            in_=src[:, 2 * hf:2 * hf + 1, :])
                e.dma_start(out=xq1[:].rearrange("p (c o) -> p c o", c=1),
                            in_=src[:, 2 * hf + 1:2 * hf + 2, :])
            stage("sync", wload, ndma=2)

            def wcast(e, wname=wname, hf=hf):
                e.tensor_copy(Wsb[wname][:, (2 * hf) * D:(2 * hf + 1) * D], xr_t[:])
                e.tensor_copy(Wsb[wname][:, (2 * hf + 1) * D:(2 * hf + 2) * D], xq1[:])
            stage("vector", wcast, nodrain=True)

    def emit_A(cc):
        r0 = cc * CH

        def xload(e, r0=r0):
            e.dma_start(out=xr_t[:], in_=x_in[r0:r0 + CH, :])
        stage("sync", xload, ndma=1)

        emit_ln_tmajor(lambda s: xr_t[:, 0:D], 1, xq1)

        def xhat_mk(e):
            e.tensor_scalar(xhat_t[:], xr_t[:], rcol[:, 0:1], nmcol[:, 0:1],
                            ALU.mult, ALU.add)
        stage("vector", xhat_mk, nodrain=True)

        def xtr(e):
            e.dma_start_transpose(
                xhat_f[:].rearrange("p (c t) -> p c t", c=FC), xhat_t[:])
        stage("sync", xtr, ndma=1)

        for half in range(2):
            def tproj(e, half=half):
                for wi, wname in enumerate(["q", "k", "v", "gK", "gC"]):
                    for dc in range(FC):
                        e.matmul(PB[wi][:, :],
                                 xhat_f[:, dc * CH:(dc + 1) * CH],
                                 Wsb[wname][:, dc * D + half * 512: dc * D + (half + 1) * 512],
                                 start=(dc == 0), stop=(dc == FC - 1))
            stage("tensor", tproj)

            def bias_add(e, half=half):
                e.tensor_tensor(PB[3][:, :], PB[3][:, :],
                                bg_bcast[:, half * 512:(half + 1) * 512], ALU.add)
                e.tensor_tensor(PB[4][:, :], PB[4][:, :],
                                bg_bcast[:, D + half * 512: D + (half + 1) * 512], ALU.add)
            stage("vector", bias_add, nodrain=True)

            def evac(e, half=half):
                hs = slice(half * 512, (half + 1) * 512)
                e.activation(qt_t[:, hs], PB[0][:, :], AF.Identity, bias=0.0, scale=1.0)
                e.activation(kt_t[:, hs], PB[1][:, :], AF.Identity, bias=0.0, scale=1.0)
                e.activation(vt_t[:, hs], PB[2][:, :], AF.Identity, bias=0.0, scale=1.0)
                e.activation(xq1[:, 0:512], PB[3][:, :], AF.Exp, bias=0.0, scale=-1.0)
                e.activation(xq1[:, 512:1024], PB[4][:, :], AF.Exp, bias=0.0, scale=-1.0)
            stage("scalar", evac, nodrain=True)

            def evac_p1(e, half=half):
                e.tensor_scalar(xq1[:], xq1[:], 1.0, None, ALU.add)
            stage("vector", evac_p1, nodrain=True)

            def evac_ln(e, half=half):
                hs = slice(half * 512, (half + 1) * 512)
                e.activation(lA_t[:, hs], xq1[:, 0:512], AF.Ln, bias=0.0, scale=1.0)
                e.activation(lB_t[:, hs], xq1[:, 512:1024], AF.Ln, bias=0.0, scale=1.0)
            stage("scalar", evac_ln, nodrain=True)

        def qk_tr(e):
            e.dma_start_transpose(
                qf[:].rearrange("p (c t) -> p c t", c=FC), qt_t[:])
            e.dma_start_transpose(
                kf[:].rearrange("p (c t) -> p c t", c=FC), kt_t[:])
        stage("sync", qk_tr, ndma=2)

        def store(e, r0=r0):
            e.dma_start(out=S_qt[r0:r0 + CH, :], in_=qt_t[:])
            e.dma_start(out=S_kt[r0:r0 + CH, :], in_=kt_t[:])
            e.dma_start(out=S_vt[r0:r0 + CH, :], in_=vt_t[:])
            e.dma_start(out=S_lA[r0:r0 + CH, :], in_=lA_t[:])
            e.dma_start(out=S_lB[r0:r0 + CH, :], in_=lB_t[:])
        stage("gpsimd", store, ndma=5)

        emit_rows(lambda: lA_t[:], lambda: lB_t[:])

        def cums(e):
            for fc in range(FC):
                sl = PB[fc // 4][:, (fc % 4) * CH:(fc % 4 + 1) * CH]
                e.matmul(sl, lB_t[:, fc * CH:(fc + 1) * CH], cmask[:, CH:2 * CH],
                         start=True, stop=True)
            e.matmul(PB[2][:, :], cmask[:, 2 * CH:3 * CH], lA_t[:, 0:512], start=True, stop=True)
            e.matmul(PB[3][:, :], cmask[:, 2 * CH:3 * CH], lA_t[:, 512:1024], start=True, stop=True)
            e.matmul(PB[4][:, :], cmask[:, 2 * CH:3 * CH], lB_t[:, 0:512], start=True, stop=True)
            e.matmul(PB[5][:, :], cmask[:, 2 * CH:3 * CH], lB_t[:, 512:1024], start=True, stop=True)
            e.matmul(PB[6][:, :], cmask[:, 0:CH], lB_t[:, 0:512], start=True, stop=True)
            e.matmul(PB[7][:, :], cmask[:, 0:CH], lB_t[:, 512:1024], start=True, stop=True)
        stage("tensor", cums)

        def exps(e):
            for fc in range(FC):
                sl = PB[fc // 4][:, (fc % 4) * CH:(fc % 4 + 1) * CH]
                e.activation(f_M[:, fc * CH:(fc + 1) * CH], sl, AF.Exp, bias=0.0, scale=1.0)
                e.activation(f_Qh[:, fc * CH:(fc + 1) * CH], sl, AF.Exp, bias=0.0, scale=-1.0)
            e.activation(t_Kr[:, 0:512], PB[2][:, :], AF.Exp, bias=0.0, scale=1.0)
            e.activation(t_Kr[:, 512:1024], PB[3][:, :], AF.Exp, bias=0.0, scale=1.0)
            e.activation(t_Qr[:, 0:512], PB[4][:, :], AF.Exp, bias=0.0, scale=1.0)
            e.activation(t_Qr[:, 512:1024], PB[5][:, :], AF.Exp, bias=0.0, scale=1.0)
            e.activation(t_Mt[:, 0:512], PB[6][:, :], AF.Exp, bias=0.0, scale=1.0)
            e.activation(t_Mt[:, 512:1024], PB[7][:, :], AF.Exp, bias=0.0, scale=1.0)
        stage("scalar", exps, nodrain=True)

        def facA(e):
            e.tensor_tensor(f_Qh[:], qf[:], f_Qh[:], ALU.mult)
            e.tensor_tensor(f_M[:], kf[:], f_M[:], ALU.mult)
            e.tensor_tensor(t_Kr[:], kt_t[:], t_Kr[:], ALU.mult)
            e.tensor_tensor(t_Qr[:], qt_t[:], t_Qr[:], ALU.mult)
            e.tensor_tensor(t_Mt[:], kt_t[:], t_Mt[:], ALU.mult)
            for fc in range(FC):
                e.tensor_scalar(c0m[:, fc * DK:(fc + 1) * DK],
                                St_C[:, fc * DK:(fc + 1) * DK],
                                gcols[:, fc * 4 + 3: fc * 4 + 4], None, ALU.mult)
        stage("vector", facA, nodrain=True)

        for hg in range(2):
            def s3mm(e, hg=hg):
                for hh in range(8):
                    h = hg * 8 + hh
                    par, fc = h % 2, h // 2
                    p0 = par * DK
                    sl = PB[par][:, (hh // 2) * CH:(hh // 2 + 1) * CH]
                    e.matmul(sl,
                             f_Qh[p0:p0 + DK, fc * CH:(fc + 1) * CH],
                             f_M[p0:p0 + DK, fc * CH:(fc + 1) * CH],
                             start=True, stop=True,
                             tile_position=(p0, 0))
            stage("tensor", s3mm)

            def s3mask(e, hg=hg):
                for hh in range(8):
                    h = hg * 8 + hh
                    e.tensor_tensor(
                        mS3[:, h * CH:(h + 1) * CH],
                        PB[hh % 2][:, (hh // 2) * CH:(hh // 2 + 1) * CH],
                        smasks[:, 4 * CH:5 * CH], ALU.mult)
            stage("vector", s3mask, nodrain=True)

            def dmm(e, hg=hg):
                for hh in range(8):
                    h = hg * 8 + hh
                    par, fc = h % 2, h // 2
                    p0 = par * DK
                    dsl = PB[2 + par][:, (hh // 2) * DK:(hh // 2 + 1) * DK]
                    e.matmul(dsl, mS3[:, h * CH:(h + 1) * CH],
                             vt_t[:, h * DK:(h + 1) * DK], start=True, stop=False)
                    e.matmul(dsl,
                             f_M[p0:p0 + DK, fc * CH:(fc + 1) * CH],
                             c0m[p0:p0 + DK, fc * DK:(fc + 1) * DK],
                             start=False, stop=True, tile_position=(p0, 0))
            stage("tensor", dmm)

            def dneg_ev(e, hg=hg):
                for hh in range(8):
                    h = hg * 8 + hh
                    e.tensor_scalar(dneg[:, h * DK:(h + 1) * DK],
                                    PB[2 + hh % 2][:, (hh // 2) * DK:(hh // 2 + 1) * DK],
                                    -1.0, None, ALU.mult)
            stage("vector", dneg_ev, nodrain=True)

        emit_states(t_Kr, t_Qr, vt_t, is_A=True, htmp=xq1)

    for cc in range(na):
        emit_A(cc)

    aes.close()

    bes = ExitStack()

    def bsb(name, shape, dt=F32):
        return bes.enter_context(nc.sbuf_tensor(name, shape, dt))

    Wosb = bsb("Wosb", [CH, FC * D], BF16)
    xB_t = bsb("xB_t", [CH, 4 * D])
    a2f = bsb("a2f", [CH, 4 * D])
    bsq = bsb("bsq", [CH, D])
    attn_f = bsb("attn_f", [CH, FC * TB], BF16)
    cf_q = bsb("cf_q", [CH, D], BF16)
    cf_k = bsb("cf_k", [CH, D], BF16)
    t_q = bsb("t_q", [CH, D], BF16)
    t_k = bsb("t_k", [CH, D], BF16)
    t_v = bsb("t_v", [CH, D], BF16)
    t_lA = bsb("t_lA", [CH, D])
    t_lB = bsb("t_lB", [CH, D])
    f_U = bsb("f_U", [CH, D], BF16)
    f_U0 = bsb("f_U0", [CH, D], BF16)
    f_Kh = bsb("f_Kh", [CH, D], BF16)
    f_F1 = bsb("f_F1", [CH, D], BF16)
    f_F2 = bsb("f_F2", [CH, D], BF16)
    t_Kh = bsb("t_Kh", [CH, D], BF16)
    ftmp = bsb("ftmp", [CH, D])
    ftmp2 = bsb("ftmp2", [CH, D])
    mS1 = bsb("mS1", [CH, 16 * CH], BF16)
    mS2 = bsb("mS2", [CH, 16 * CH], BF16)
    pt_f = bsb("pt_f", [CH, D], BF16)

    for hf in range(2):
        def wo_load(e, hf=hf):
            src = WHBM["o"].rearrange("(c p) o -> p c o", p=CH)
            e.dma_start(out=xB_t[:].rearrange("p (c o) -> p c o", c=4),
                        in_=src[:, 4 * hf:4 * hf + 4, :])
        stage("sync", wo_load, ndma=1)

        def wo_cast(e, hf=hf):
            e.tensor_copy(Wosb[:, hf * 4 * D:(hf + 1) * 4 * D], xB_t[:])
        stage("vector", wo_cast, nodrain=True)

    EN["on"] = relay or dbg

    def pack_agg(e):
        e.tensor_copy(agg[:, 0:512], St_S[:])
        e.tensor_copy(agg[:, 512:1024], St_C[:])
        e.tensor_copy(agg[:, 1024:1536], St_G[:])
        e.tensor_copy(agg[:, 1536:2048], St_H[:])
        e.tensor_copy(agg[:, 2048:2056], avec[:])
        e.tensor_copy(agg[:, 2056:2064], bvec[:])
        e.memset(agg[:, 2064:2080], 0.0)
    stage("vector", pack_agg, nodrain=True)

    def relay_out(e):
        e.dma_start(out=CC_IN[:, :], in_=agg[:, :])
    stage("gpsimd", relay_out, ndma=1)

    EN["on"] = relay

    def relay_cc(e):
        e.collective_compute(
            "AllGather", ALU.bypass,
            replica_groups=[[0, 1, 2, 3], [4, 5, 6, 7]],
            ins=[CC_IN[:, :]], outs=[CC_OUT[:, :]],
        )
    stage("gpsimd", relay_cc)

    EN["on"] = relay or nb > 0

    def comp_zero(e):
        e.memset(St_S[:], 0.0)
        e.memset(St_C[:], 0.0)
        e.memset(St_G[:], 0.0)
    stage("vector", comp_zero, nodrain=True)

    EN["on"] = relay

    for q in range(3):
        def agg_load(e, q=q):
            e.dma_start(out=agg[:, :], in_=CC_OUT[q * CH:(q + 1) * CH, :])
        stage("sync", agg_load, ndma=1)

        def comp_pre(e, q=q):
            e.tensor_scalar(avec[:], agg[:, 2048:2056], psc[:, q:q + 1],
                            psc[:, 6 + q:7 + q], ALU.mult, ALU.add)
            e.tensor_scalar(bvec[:], agg[:, 2056:2064], psc[:, q:q + 1],
                            psc[:, 6 + q:7 + q], ALU.mult, ALU.add)
            e.tensor_copy(comp_bf[:], St_C[:])
            e.tensor_copy(comp_bf2[:], agg[:, 1536:2048])
        stage("vector", comp_pre)

        def comp_tr(e, q=q):
            for fc in range(FC):
                e.transpose(PB[fc // 4][0:1, (fc % 4) * CH:(fc % 4 + 1) * CH],
                            avec[:, fc:fc + 1], ident_fs[:, :])
        stage("tensor", comp_tr)

        def comp_trev(e, q=q):
            e.tensor_copy(rowscr[0:1, 0:512], PB[0][0:1, :])
            e.tensor_copy(rowscr[0:1, 512:1024], PB[1][0:1, :])
        stage("vector", comp_trev, nodrain=True)

        def comp_mm(e, q=q):
            for h in range(H):
                par, fc = h % 2, h // 2
                p0, c0 = par * DK, fc * DK
                ar = rowscr[0:1, fc * CH + p0: fc * CH + p0 + DK]
                e.matmul(PB[1][p0:p0 + DK, c0:c0 + DK], ar, ar,
                         start=True, stop=True, tile_position=(0, p0))
                e.matmul(PB[3 + par][p0:p0 + DK, c0:c0 + DK],
                         comp_bf2[p0:p0 + DK, c0:c0 + DK],
                         comp_bf[p0:p0 + DK, c0:c0 + DK],
                         start=True, stop=True, tile_position=(p0, p0))
        stage("tensor", comp_mm)

        def comp_upd(e, q=q):
            sel = psc[:, q:q + 1]
            nsel = psc[:, 3 + q:4 + q]
            e.tensor_tensor(St_S[:], St_S[:], PB[1][:, :], ALU.mult)
            e.scalar_tensor_tensor(St_S[:], agg[:, 0:512], sel, St_S[:],
                                   ALU.mult, ALU.add)
            for fc in range(FC):
                s = slice(fc * DK, (fc + 1) * DK)
                e.tensor_scalar(St_G[:, s], St_G[:, s], avec[:, fc:fc + 1],
                                None, ALU.mult)
            e.scalar_tensor_tensor(St_G[0:DK, :], PB[3][0:DK, :],
                                   psc[0:DK, 3 + q:4 + q], St_G[0:DK, :],
                                   ALU.mult, ALU.add)
            e.scalar_tensor_tensor(St_G[DK:CH, :], PB[4][DK:CH, :],
                                   psc[DK:CH, 3 + q:4 + q], St_G[DK:CH, :],
                                   ALU.mult, ALU.add)
            e.scalar_tensor_tensor(St_G[:], agg[:, 1024:1536], sel, St_G[:],
                                   ALU.mult, ALU.add)
            for fc in range(FC):
                s = slice(fc * DK, (fc + 1) * DK)
                e.tensor_scalar(St_C[:, s], St_C[:, s], bvec[:, fc:fc + 1],
                                None, ALU.mult)
            e.scalar_tensor_tensor(St_C[:], agg[:, 512:1024], sel, St_C[:],
                                   ALU.mult, ALU.add)
        stage("vector", comp_upd, nodrain=True)

    def emit_B(cc):
        r0 = cc * CH
        tc = cc % 4

        def bload(e, r0=r0):
            e.dma_start(out=t_q[:], in_=S_qt[r0:r0 + CH, :])
            e.dma_start(out=t_k[:], in_=S_kt[r0:r0 + CH, :])
            e.dma_start(out=t_v[:], in_=S_vt[r0:r0 + CH, :])
            e.dma_start(out=t_lA[:], in_=S_lA[r0:r0 + CH, :])
            e.dma_start(out=t_lB[:], in_=S_lB[r0:r0 + CH, :])
        stage("sync", bload, ndma=5)

        def cf_tr(e):
            e.dma_start_transpose(
                cf_q[:].rearrange("p (c t) -> p c t", c=FC), t_q[:])
            e.dma_start_transpose(
                cf_k[:].rearrange("p (c t) -> p c t", c=FC), t_k[:])
        stage("sync", cf_tr, ndma=2)

        emit_rows(lambda: t_lA[:], lambda: t_lB[:])

        def cums_f(e):
            for fc in range(FC):
                sl0 = PB[0 + fc // 4][:, (fc % 4) * CH:(fc % 4 + 1) * CH]
                sl1 = PB[2 + fc // 4][:, (fc % 4) * CH:(fc % 4 + 1) * CH]
                sl2 = PB[4 + fc // 4][:, (fc % 4) * CH:(fc % 4 + 1) * CH]
                e.matmul(sl0, t_lA[:, fc * CH:(fc + 1) * CH], cmask[:, CH:2 * CH],
                         start=True, stop=True)
                e.matmul(sl1, t_lB[:, fc * CH:(fc + 1) * CH], cmask[:, CH:2 * CH],
                         start=True, stop=True)
                e.matmul(sl2, t_lA[:, fc * CH:(fc + 1) * CH], cmask[:, 0:CH],
                         start=True, stop=True)
        stage("tensor", cums_f)

        def fargs(e):
            e.tensor_copy(bsq[:, 0:512], PB[2][:, :])
            e.tensor_copy(bsq[:, 512:1024], PB[3][:, :])
            e.drain()
            e.tensor_tensor(ftmp[:, 0:512], PB[4][:, :], bsq[:, 0:512], ALU.add)
            e.tensor_tensor(ftmp[:, 512:1024], PB[5][:, :], bsq[:, 512:1024], ALU.add)
            e.tensor_tensor(ftmp2[:, 0:512], PB[0][:, :], bsq[:, 0:512], ALU.add)
            e.tensor_tensor(ftmp2[:, 512:1024], PB[1][:, :], bsq[:, 512:1024], ALU.add)
        stage("vector", fargs, nodrain=True)

        def exps_f(e):
            for fc in range(FC):
                sl = PB[fc // 4][:, (fc % 4) * CH:(fc % 4 + 1) * CH]
                sl2 = PB[2 + fc // 4][:, (fc % 4) * CH:(fc % 4 + 1) * CH]
                e.activation(f_U[:, fc * CH:(fc + 1) * CH], sl, AF.Exp, bias=0.0, scale=1.0)
                e.activation(f_Kh[:, fc * CH:(fc + 1) * CH], sl, AF.Exp, bias=0.0, scale=-1.0)
                e.activation(f_M[:, fc * CH:(fc + 1) * CH], sl2, AF.Exp, bias=0.0, scale=1.0)
                e.activation(f_Qh[:, fc * CH:(fc + 1) * CH], sl2, AF.Exp, bias=0.0, scale=-1.0)
            e.activation(f_F1[:, 0:512], ftmp[:, 0:512], AF.Exp, bias=0.0, scale=1.0)
            e.activation(f_F1[:, 512:1024], ftmp[:, 512:1024], AF.Exp, bias=0.0, scale=1.0)
            e.activation(f_F2[:, 0:512], ftmp2[:, 0:512], AF.Exp, bias=0.0, scale=1.0)
            e.activation(f_F2[:, 512:1024], ftmp2[:, 512:1024], AF.Exp, bias=0.0, scale=1.0)
        stage("scalar", exps_f, nodrain=True)

        def cums_t(e):
            e.matmul(PB[0][:, :], cmask[:, CH:2 * CH], t_lA[:, 0:512], start=True, stop=True)
            e.matmul(PB[1][:, :], cmask[:, CH:2 * CH], t_lA[:, 512:1024], start=True, stop=True)
            e.matmul(PB[2][:, :], cmask[:, 2 * CH:3 * CH], t_lA[:, 0:512], start=True, stop=True)
            e.matmul(PB[3][:, :], cmask[:, 2 * CH:3 * CH], t_lA[:, 512:1024], start=True, stop=True)
            e.matmul(PB[4][:, :], cmask[:, 2 * CH:3 * CH], t_lB[:, 0:512], start=True, stop=True)
            e.matmul(PB[5][:, :], cmask[:, 2 * CH:3 * CH], t_lB[:, 512:1024], start=True, stop=True)
        stage("tensor", cums_t)

        def exps_t(e):
            e.activation(t_Kh[:, 0:512], PB[0][:, :], AF.Exp, bias=0.0, scale=-1.0)
            e.activation(t_Kh[:, 512:1024], PB[1][:, :], AF.Exp, bias=0.0, scale=-1.0)
            e.activation(t_Kr[:, 0:512], PB[2][:, :], AF.Exp, bias=0.0, scale=1.0)
            e.activation(t_Kr[:, 512:1024], PB[3][:, :], AF.Exp, bias=0.0, scale=1.0)
            e.activation(t_Qr[:, 0:512], PB[4][:, :], AF.Exp, bias=0.0, scale=1.0)
            e.activation(t_Qr[:, 512:1024], PB[5][:, :], AF.Exp, bias=0.0, scale=1.0)
        stage("scalar", exps_t, nodrain=True)

        def facB(e):
            e.tensor_tensor(f_U[:], cf_q[:], f_U[:], ALU.mult)
            e.tensor_tensor(f_Kh[:], cf_k[:], f_Kh[:], ALU.mult)
            e.tensor_tensor(f_M[:], cf_k[:], f_M[:], ALU.mult)
            e.tensor_tensor(f_Qh[:], cf_q[:], f_Qh[:], ALU.mult)
            e.tensor_tensor(t_Kh[:], t_k[:], t_Kh[:], ALU.mult)
            e.tensor_tensor(t_Kr[:], t_k[:], t_Kr[:], ALU.mult)
            e.tensor_tensor(t_Qr[:], t_q[:], t_Qr[:], ALU.mult)
            for fc in range(FC):
                e.tensor_scalar(c0m[:, fc * DK:(fc + 1) * DK],
                                St_C[:, fc * DK:(fc + 1) * DK],
                                gcols[:, fc * 4 + 3: fc * 4 + 4], None, ALU.mult)
            e.tensor_copy(Sb_S[:], St_S[:])
            e.tensor_copy(Sb_G[:], St_G[:])
            e.drain()
            for fc in range(FC):
                e.tensor_scalar(f_U0[:, fc * CH:(fc + 1) * CH],
                                f_U[:, fc * CH:(fc + 1) * CH],
                                gcols[:, fc * 4 + 2: fc * 4 + 3], None, ALU.mult)
        stage("vector", facB, nodrain=True)

        for hg in range(2):
            def s23(e, hg=hg):
                for hh in range(8):
                    h = hg * 8 + hh
                    par, fc = h % 2, h // 2
                    p0 = par * DK
                    kh = f_Kh[p0:p0 + DK, fc * CH:(fc + 1) * CH]
                    uu = f_U[p0:p0 + DK, fc * CH:(fc + 1) * CH]
                    qh = f_Qh[p0:p0 + DK, fc * CH:(fc + 1) * CH]
                    mm = f_M[p0:p0 + DK, fc * CH:(fc + 1) * CH]
                    csl = (hh // 2) * CH
                    e.matmul(PB[par][:, csl:csl + CH], kh, uu,
                             start=True, stop=True, tile_position=(p0, 0))
                    e.matmul(PB[2 + par][:, csl:csl + CH], qh, mm,
                             start=True, stop=True, tile_position=(p0, 0))
            stage("tensor", s23)

            def s23mask(e, hg=hg):
                for hh in range(8):
                    h = hg * 8 + hh
                    csl = (hh // 2) * CH
                    e.tensor_tensor(mS2[:, h * CH:(h + 1) * CH],
                                    PB[hh % 2][:, csl:csl + CH],
                                    smasks[:, 0:CH], ALU.mult)
                    e.tensor_tensor(mS3[:, h * CH:(h + 1) * CH],
                                    PB[2 + hh % 2][:, csl:csl + CH],
                                    smasks[:, 4 * CH:5 * CH], ALU.mult)
            stage("vector", s23mask, nodrain=True)

            def pmm(e, hg=hg):
                for hh in range(8):
                    h = hg * 8 + hh
                    par, fc = h % 2, h // 2
                    p0 = par * DK
                    csl = (hh // 2) * CH
                    e.matmul(PB[4 + 2 * par][p0:p0 + DK, csl:csl + CH],
                             Sb_S[p0:p0 + DK, fc * DK:(fc + 1) * DK],
                             f_U0[p0:p0 + DK, fc * CH:(fc + 1) * CH],
                             start=True, stop=True, tile_position=(p0, p0))
                    e.matmul(PB[5][p0:p0 + DK, csl:csl + CH],
                             t_Kh[:, h * DK:(h + 1) * DK],
                             mS2[:, h * CH:(h + 1) * CH],
                             start=True, stop=True, tile_position=(0, p0))
            stage("tensor", pmm)

            def ptilde(e, hg=hg):
                o = hg * 512
                e.tensor_tensor(ftmp[0:DK, 0:512], f_F1[0:DK, o:o + 512],
                                PB[4][0:DK, :], ALU.mult)
                e.tensor_tensor(ftmp[DK:CH, 0:512], f_F1[DK:CH, o:o + 512],
                                PB[6][DK:CH, :], ALU.mult)
                e.tensor_tensor(ftmp2[:, 0:512], f_F2[:, o:o + 512], PB[5][:, :],
                                ALU.mult)
                e.drain()
                e.tensor_tensor(pt_f[:, o:o + 512], ftmp[:, 0:512],
                                ftmp2[:, 0:512], ALU.add)
            stage("vector", ptilde, nodrain=True)

            def s1mm(e, hg=hg):
                for hh in range(8):
                    h = hg * 8 + hh
                    par, fc = h % 2, h // 2
                    p0 = par * DK
                    qh = f_Qh[p0:p0 + DK, fc * CH:(fc + 1) * CH]
                    pt = pt_f[p0:p0 + DK, fc * CH:(fc + 1) * CH]
                    csl = (hh // 2) * CH
                    e.matmul(PB[par][:, csl:csl + CH], qh, pt,
                             start=True, stop=True, tile_position=(p0, 0))
            stage("tensor", s1mm)

            def s1mask(e, hg=hg):
                for hh in range(8):
                    h = hg * 8 + hh
                    csl = (hh // 2) * CH
                    e.tensor_tensor(mS1[:, h * CH:(h + 1) * CH],
                                    PB[hh % 2][:, csl:csl + CH],
                                    smasks[:, 0:CH], ALU.mult)
            stage("vector", s1mask, nodrain=True)

            def dmmB(e, hg=hg):
                for hh in range(8):
                    h = hg * 8 + hh
                    par, fc = h % 2, h // 2
                    p0 = par * DK
                    dsl = PB[2 + par][:, (hh // 2) * DK:(hh // 2 + 1) * DK]
                    e.matmul(dsl, mS3[:, h * CH:(h + 1) * CH],
                             t_v[:, h * DK:(h + 1) * DK], start=True, stop=False)
                    e.matmul(dsl,
                             f_M[p0:p0 + DK, fc * CH:(fc + 1) * CH],
                             c0m[p0:p0 + DK, fc * DK:(fc + 1) * DK],
                             start=False, stop=True, tile_position=(p0, 0))
            stage("tensor", dmmB)

            def dnegB(e, hg=hg):
                for hh in range(8):
                    h = hg * 8 + hh
                    e.tensor_scalar(dneg[:, h * DK:(h + 1) * DK],
                                    PB[2 + hh % 2][:, (hh // 2) * DK:(hh // 2 + 1) * DK],
                                    -1.0, None, ALU.mult)
            stage("vector", dnegB, nodrain=True)

            def omm(e, hg=hg):
                for hh in range(8):
                    h = hg * 8 + hh
                    par, fc = h % 2, h // 2
                    p0 = par * DK
                    osl = PB[6 + par][p0:p0 + DK, (hh // 2) * CH:(hh // 2 + 1) * CH]
                    e.matmul(osl, t_v[:, h * DK:(h + 1) * DK],
                             mS1[:, h * CH:(h + 1) * CH], start=True, stop=False,
                             tile_position=(0, p0))
                    e.matmul(osl, c0m[p0:p0 + DK, fc * DK:(fc + 1) * DK],
                             pt_f[p0:p0 + DK, fc * CH:(fc + 1) * CH],
                             start=False, stop=False, tile_position=(p0, p0))
                    e.matmul(osl, Sb_G[p0:p0 + DK, fc * DK:(fc + 1) * DK],
                             f_U0[p0:p0 + DK, fc * CH:(fc + 1) * CH],
                             start=False, stop=False, tile_position=(p0, p0))
                    e.matmul(osl, dneg[:, h * DK:(h + 1) * DK],
                             mS2[:, h * CH:(h + 1) * CH], start=False, stop=True,
                             tile_position=(0, p0))
            stage("tensor", omm)

            def oev(e, hg=hg, tc=tc):
                e.tensor_copy(
                    attn_f[0:DK, :].rearrange("p (c n) -> p c n", c=FC)[
                        :, hg * 4:(hg + 1) * 4, tc * CH:(tc + 1) * CH],
                    PB[6][0:DK, :].rearrange("p (c n) -> p c n", c=4))
                e.tensor_copy(
                    attn_f[DK:CH, :].rearrange("p (c n) -> p c n", c=FC)[
                        :, hg * 4:(hg + 1) * 4, tc * CH:(tc + 1) * CH],
                    PB[7][DK:CH, :].rearrange("p (c n) -> p c n", c=4))
            stage("vector", oev, nodrain=True)

        emit_states(t_Kr, t_Qr, t_v, is_A=False)

    def emit_B_epilogue(t):
        r0 = t * TB

        def xload(e, r0=r0):
            e.dma_start(out=xB_t[:].rearrange("p (s o) -> p s o", s=4),
                        in_=x_in[r0:r0 + TB, :].rearrange("(s p) o -> p s o", p=CH))
        stage("sync", xload, ndma=1)

        def womm(e):
            for s in range(4):
                for half in range(2):
                    for pc in range(FC):
                        e.matmul(PB[s * 2 + half][:, :],
                                 attn_f[:, pc * TB + s * CH: pc * TB + (s + 1) * CH],
                                 Wosb[:, pc * D + half * 512: pc * D + (half + 1) * 512],
                                 start=(pc == 0), stop=(pc == FC - 1))
        stage("tensor", womm)

        def ln2_copy(e):
            for s in range(4):
                for half in range(2):
                    e.tensor_copy(a2f[:, s * D + half * 512: s * D + (half + 1) * 512],
                                  PB[s * 2 + half][:, :])
        stage("vector", ln2_copy, nodrain=True)

        emit_ln_tmajor(lambda s: a2f[:, s * D:(s + 1) * D], 4, bsq)

        def resid(e):
            for s in range(4):
                sl = slice(s * D, (s + 1) * D)
                e.scalar_tensor_tensor(xB_t[:, sl], a2f[:, sl], rcol[:, s:s + 1],
                                       xB_t[:, sl], ALU.mult, ALU.add)
            e.drain()
            for s in range(4):
                sl = slice(s * D, (s + 1) * D)
                e.tensor_scalar(xB_t[:, sl], xB_t[:, sl], nmcol[:, s:s + 1],
                                None, ALU.add)
        stage("vector", resid, nodrain=True)

        def xpstore(e, r0=r0):
            e.dma_start(out=S_xp[r0:r0 + TB, :].rearrange("(s p) o -> p s o", p=CH),
                        in_=xB_t[:].rearrange("p (s o) -> p s o", s=4))
        stage("gpsimd", xpstore, ndma=1)

    EN["on"] = True
    for t in range(nb // 4):
        for tc in range(4):
            emit_B(t * 4 + tc)
        emit_B_epilogue(t)

    bes.close()

    ces = ExitStack()

    def csb(name, shape, dt=F32):
        return ces.enter_context(nc.sbuf_tensor(name, shape, dt))

    fstage = csb("fstage", [CH, INNER])
    fint = csb("fint", [CH, INNER], I32)
    fbf = csb("fbf", [CH, INNER], BF16)
    wf_acc = csb("wf_acc", [CH, 32])
    wcols = csb("wcols", [CH, 8])
    scol1 = csb("scol1", [CH, 1])
    xp_t = csb("xp_t", [CH, 4 * D])
    csq = csb("csq", [CH, D])
    x3_t = csb("x3_t", [CH, 4 * D], BF16)
    x3q_t = csb("x3q_t", [CH, 4 * D], BF16)
    x3q_f = csb("x3q_f", [CH, FC * GRP], BF16)
    gint = csb("gint", [CH, D], I32)
    mid = csb("mid", [CH, 32 * GRP], BF16)
    tern_sb = csb("tern_sb", [CH, 4 * D], BF16)
    ternd_sb = csb("ternd_sb", [CH, D], BF16)
    gate_sb = csb("gate_sb", [CH, 2 * GRP], BF16)
    maxacc = fstage
    ffn_t = x3_t
    absm = csb("absm", [CH, GRP])
    maxcol = csb("maxcol", [CH, 4])
    qcol = csb("qcol", [CH, 4])
    iqcol = csb("iqcol", [CH, 4])

    def emit_tern(whbm, widx, nrows, ncols, store_swizzled, dst):
        nch = nrows // CH
        for rc in range(nch):
            def ml(e, rc=rc):
                e.dma_start(out=fstage[:, 0:ncols], in_=whbm[rc * CH:(rc + 1) * CH, :])
            stage("sync", ml, ndma=1)

            def mr(e, rc=rc):
                e.tensor_reduce(wf_acc[:, rc:rc + 1], fstage[:, 0:ncols],
                                AX.X, ALU.add, apply_absolute_value=True)
            stage("vector", mr)

        def mfin(e, nch=nch):
            e.tensor_reduce(scol1[:, 0:1], wf_acc[:, 0:nch], AX.X, ALU.add)
        stage("vector", mfin)

        def msum(e):
            e.matmul(PB[0][0:1, 0:1], onecol_fs[:], scol1[:, 0:1], start=True, stop=True)
        stage("tensor", msum)

        def mscal(e, nrows=nrows, ncols=ncols):
            e.tensor_scalar(rowscr[0:1, 0:1], PB[0][0:1, 0:1],
                            1.0 / (nrows * ncols), 1e-5, ALU.mult, ALU.max)
            e.drain()
            e.reciprocal(rowscr[0:1, 1:2], rowscr[0:1, 0:1])
        stage("vector", mscal)

        def mbc(e):
            e.matmul(PB[0][:, 0:2], ones_row_s[0:1, :], rowscr[0:1, 0:2],
                     start=True, stop=True)
        stage("tensor", mbc)

        def mbc_ev(e, widx=widx):
            e.tensor_copy(wcols[:, 3 + widx:4 + widx], PB[0][:, 0:1])
            e.tensor_copy(wcols[:, widx:widx + 1], PB[0][:, 1:2])
        stage("vector", mbc_ev)

        if store_swizzled:
            for ob in range(ncols // CH):
                def tl(e, ob=ob):
                    e.dma_start(
                        out=fstage[:, 0:FC * CH].rearrange("p (c n) -> p c n", c=FC),
                        in_=whbm.rearrange("(c p) n -> p c n", p=CH)[:, :, ob * CH:(ob + 1) * CH])
                stage("sync", tl, ndma=1)

                def tq(e, widx=widx):
                    e.tensor_scalar(fint[:, 0:FC * CH], fstage[:, 0:FC * CH],
                                    wcols[:, widx:widx + 1], None, ALU.mult)
                    e.drain()
                    e.tensor_scalar(fbf[:, 0:FC * CH], fint[:, 0:FC * CH],
                                    -1.0, 1.0, ALU.max, ALU.min)
                stage("vector", tq, nodrain=True)

                def ts_(e, ob=ob):
                    e.dma_start(out=dst[:, ob * FC * CH:(ob + 1) * FC * CH],
                                in_=fbf[:, 0:FC * CH])
                stage("gpsimd", ts_, ndma=1)
        else:
            for rc in range(nch):
                def tl(e, rc=rc):
                    e.dma_start(out=fstage[:, 0:ncols], in_=whbm[rc * CH:(rc + 1) * CH, :])
                stage("sync", tl, ndma=1)

                def tq(e, widx=widx, ncols=ncols):
                    e.tensor_scalar(fint[:, 0:ncols], fstage[:, 0:ncols],
                                    wcols[:, widx:widx + 1], None, ALU.mult)
                    e.drain()
                    e.tensor_scalar(fbf[:, 0:ncols], fint[:, 0:ncols],
                                    -1.0, 1.0, ALU.max, ALU.min)
                stage("vector", tq, nodrain=True)

                def ts_(e, rc=rc, ncols=ncols):
                    e.dma_start(out=dst[rc * CH:(rc + 1) * CH, :], in_=fbf[:, 0:ncols])
                stage("gpsimd", ts_, ndma=1)

    EN["on"] = tern
    emit_tern(wgT, 0, D, INNER, True, TERN_G)
    emit_tern(wuT, 1, D, INNER, True, TERN_U)
    emit_tern(wdT, 2, INNER, D, False, TERN_D)
    EN["on"] = True

    def emit_C(g):
        r0 = g * GRP

        def xpl(e, r0=r0):
            e.dma_start(out=xp_t[:].rearrange("p (s o) -> p s o", s=4),
                        in_=S_xp[r0:r0 + GRP, :].rearrange("(s p) o -> p s o", p=CH))
        stage("sync", xpl, ndma=1)

        def accz(e):
            e.memset(maxacc[:, 0:4 * GRP], 0.0)
        stage("vector", accz, nodrain=True)

        emit_ln_tmajor(lambda s: xp_t[:, s * D:(s + 1) * D], 4, csq)

        def x3mk(e):
            for s in range(4):
                e.tensor_scalar(x3_t[:, s * D:(s + 1) * D],
                                xp_t[:, s * D:(s + 1) * D],
                                rcol[:, s:s + 1], nmcol[:, s:s + 1],
                                ALU.mult, ALU.add)
        stage("vector", x3mk, nodrain=True)

        def qmax(e):
            for s in range(4):
                e.tensor_reduce(tcol[:, s:s + 1], x3_t[:, s * D:(s + 1) * D],
                                AX.X, ALU.max, apply_absolute_value=True)
            e.drain()
            e.tensor_scalar(tcol[:, 0:4], tcol[:, 0:4], 1e-5, None, ALU.max)
        stage("vector", qmax)

        def qscale(e):
            e.reciprocal(qcol[:, 0:4], tcol[:, 0:4])
            e.tensor_scalar(iqcol[:, 0:4], tcol[:, 0:4], 1.0 / 127.0, None, ALU.mult)
            e.drain()
            e.tensor_scalar(qcol[:, 0:4], qcol[:, 0:4], 127.0, None, ALU.mult)
        stage("vector", qscale)

        def xq(e):
            for s in range(4):
                sl = slice(s * D, (s + 1) * D)
                e.tensor_scalar(gint[:], x3_t[:, sl], qcol[:, s:s + 1], None, ALU.mult)
                e.drain()
                e.tensor_scalar(csq[:], gint[:], -128.0, 127.0, ALU.max, ALU.min)
                e.drain()
                e.tensor_scalar(x3q_t[:, sl], csq[:], iqcol[:, s:s + 1], None, ALU.mult)
                e.drain()
        stage("vector", xq, nodrain=True)

        def x3tr(e):
            for s in range(4):
                e.dma_start_transpose(
                    x3q_f[:].rearrange("p (c s t) -> p c s t", c=FC, s=4)[:, :, s, :],
                    x3q_t[:, s * D:(s + 1) * D])
        stage("sync", x3tr, ndma=4)

        for ob in range(16):
            def ternl(e, ob=ob):
                e.dma_start(out=tern_sb[:, 0:D], in_=TERN_G[:, 2 * ob * D:(2 * ob + 1) * D])
                e.dma_start(out=tern_sb[:, D:2 * D], in_=TERN_G[:, (2 * ob + 1) * D:(2 * ob + 2) * D])
                e.dma_start(out=tern_sb[:, 2 * D:3 * D], in_=TERN_U[:, 2 * ob * D:(2 * ob + 1) * D])
                e.dma_start(out=tern_sb[:, 3 * D:4 * D], in_=TERN_U[:, (2 * ob + 1) * D:(2 * ob + 2) * D])
            stage("sync", ternl, ndma=4)

            def gumm(e, ob=ob):
                for j in range(2):
                    for dc in range(FC):
                        e.matmul(PB[j * 2][:, :],
                                 tern_sb[:, j * D + dc * CH: j * D + (dc + 1) * CH],
                                 x3q_f[:, dc * GRP:(dc + 1) * GRP],
                                 start=(dc == 0), stop=(dc == FC - 1))
                    for dc in range(FC):
                        e.matmul(PB[j * 2 + 1][:, :],
                                 tern_sb[:, (2 + j) * D + dc * CH: (2 + j) * D + (dc + 1) * CH],
                                 x3q_f[:, dc * GRP:(dc + 1) * GRP],
                                 start=(dc == 0), stop=(dc == FC - 1))
            stage("tensor", gumm)

            def silu_ev(e, ob=ob):
                e.activation(gate_sb[:, 0:GRP], PB[0][:, :], AF.Silu,
                             bias=0.0, scale=wcols[:, 3:4])
                e.activation(gate_sb[:, GRP:2 * GRP], PB[2][:, :], AF.Silu,
                             bias=0.0, scale=wcols[:, 3:4])
            stage("scalar", silu_ev, nodrain=True)

            def midmk(e, ob=ob):
                for j in range(2):
                    oc = 2 * ob + j
                    e.scalar_tensor_tensor(
                        mid[:, oc * GRP:(oc + 1) * GRP],
                        PB[j * 2 + 1][:, :], wcols[:, 4:5],
                        gate_sb[:, j * GRP:(j + 1) * GRP],
                        ALU.mult, ALU.mult)
            stage("vector", midmk, nodrain=True)

            def absupd(e, ob=ob):
                for j in range(2):
                    oc = 2 * ob + j
                    msl = mid[:, oc * GRP:(oc + 1) * GRP]
                    e.tensor_tensor(maxacc[:, 2 * j * GRP:(2 * j + 1) * GRP],
                                    maxacc[:, 2 * j * GRP:(2 * j + 1) * GRP],
                                    msl, ALU.max)
                    e.tensor_tensor(maxacc[:, (2 * j + 1) * GRP:(2 * j + 2) * GRP],
                                    maxacc[:, (2 * j + 1) * GRP:(2 * j + 2) * GRP],
                                    msl, ALU.min)
            stage("vector", absupd, nodrain=True)

        def mmax(e):
            e.tensor_tensor(absm[:, 0:GRP], maxacc[:, 0:GRP],
                            maxacc[:, 2 * GRP:3 * GRP], ALU.max)
            e.tensor_tensor(csq[:, 0:GRP], maxacc[:, GRP:2 * GRP],
                            maxacc[:, 3 * GRP:4 * GRP], ALU.min)
            e.drain()
            e.tensor_scalar(csq[:, 0:GRP], csq[:, 0:GRP], -1.0, None, ALU.mult)
            e.drain()
            e.tensor_tensor(absm[:, 0:GRP], absm[:, 0:GRP], csq[:, 0:GRP], ALU.max)
        stage("vector", mmax, nodrain=True)

        def mmtr(e):
            for tb in range(4):
                e.transpose(PB[0][:, tb * CH:(tb + 1) * CH],
                            absm[:, tb * CH:(tb + 1) * CH], ident_fs[:, :])
        stage("tensor", mmtr)

        def mmred(e):
            for tb in range(4):
                e.tensor_reduce(maxcol[:, tb:tb + 1], PB[0][:, tb * CH:(tb + 1) * CH],
                                AX.X, ALU.max)
            e.drain()
            e.tensor_scalar(maxcol[:, 0:4], maxcol[:, 0:4], 1e-5, None, ALU.max)
        stage("vector", mmred)

        def mctr(e):
            for tb in range(4):
                e.transpose(PB[1][0:1, tb * CH:(tb + 1) * CH],
                            maxcol[:, tb:tb + 1], ident_fs[:, :])
        stage("tensor", mctr)

        def mcev(e):
            e.tensor_copy(rowscr[0:1, 0:512], PB[1][0:1, :])
            e.drain()
            e.reciprocal(rowscr[0:1, 512:1024], rowscr[0:1, 0:512])
            e.tensor_scalar(rowscr[0:1, 1024:1536], rowscr[0:1, 0:512],
                            1.0 / 127.0, None, ALU.mult)
            e.drain()
            e.tensor_scalar(rowscr[0:1, 512:1024], rowscr[0:1, 512:1024],
                            127.0, None, ALU.mult)
        stage("vector", mcev)

        def mqbc(e):
            e.matmul(PB[2][:, :], ones_row_s[0:1, :], rowscr[0:1, 512:1024],
                     start=True, stop=True)
            e.matmul(PB[3][:, :], ones_row_s[0:1, :], rowscr[0:1, 1024:1536],
                     start=True, stop=True)
        stage("tensor", mqbc)

        def mq(e):
            for ob in range(16):
                for j in range(2):
                    oc = 2 * ob + j
                    e.tensor_tensor(gint[:, j * GRP:(j + 1) * GRP],
                                    mid[:, oc * GRP:(oc + 1) * GRP],
                                    PB[2][:, :], ALU.mult)
                e.drain()
                for j in range(2):
                    oc = 2 * ob + j
                    e.tensor_scalar(mid[:, oc * GRP:(oc + 1) * GRP],
                                    gint[:, j * GRP:(j + 1) * GRP],
                                    -128.0, 127.0, ALU.max, ALU.min)
                e.drain()
        stage("vector", mq, nodrain=True)

        for ic in range(32):
            def dtl(e, ic=ic):
                e.dma_start(out=ternd_sb[:], in_=TERN_D[ic * CH:(ic + 1) * CH, :])
            stage("sync", dtl, ndma=1)

            def dmm(e, ic=ic):
                for s in range(4):
                    for half in range(2):
                        e.matmul(PB[s * 2 + half][:, :],
                                 mid[:, ic * GRP + s * CH: ic * GRP + (s + 1) * CH],
                                 ternd_sb[:, half * 512:(half + 1) * 512],
                                 start=(ic == 0), stop=(ic == 31),
                                 skip_group_check=True)
            stage("tensor", dmm)

        def dev(e):
            for s in range(4):
                for half in range(2):
                    e.tensor_copy(ffn_t[:, s * D + half * 512: s * D + (half + 1) * 512],
                                  PB[s * 2 + half][:, :])
        stage("vector", dev, nodrain=True)

        emit_ln_tmajor(lambda s: ffn_t[:, s * D:(s + 1) * D], 4, csq)

        def resid4(e):
            for s in range(4):
                sl = slice(s * D, (s + 1) * D)
                e.scalar_tensor_tensor(xp_t[:, sl], ffn_t[:, sl], rcol[:, s:s + 1],
                                       xp_t[:, sl], ALU.mult, ALU.add)
            e.drain()
            for s in range(4):
                sl = slice(s * D, (s + 1) * D)
                e.tensor_scalar(xp_t[:, sl], xp_t[:, sl], nmcol[:, s:s + 1],
                                None, ALU.add)
        stage("vector", resid4, nodrain=True)

        def ostore(e, r0=r0):
            e.dma_start(out=out_t[r0:r0 + GRP, :].rearrange("(s p) o -> p s o", p=CH),
                        in_=xp_t[:].rearrange("p (s o) -> p s o", s=4))
        stage("gpsimd", ostore, ndma=1)

    for g in range(ngr):
        emit_C(g)

    ces.close()

    stage("gpsimd", lambda e: None)

    if nstop is not None:
        del SEGS[nstop:]
        SEGS.append(("gpsimd", lambda e: None, 0, 0, False))

    contrib = [16 * nd + ex + 1 for (_, _, nd, ex, _) in SEGS]
    cum = [0]
    for c in contrib:
        cum.append(cum[-1] + c)
    globals()["LAST_NSEGS"] = len(SEGS)
    globals()["LAST_CUMTOT"] = cum[-1]

    class Proxy:
        def __init__(self, eng, sem):
            self._eng = eng
            self._sem = sem
            self.nd = 0
            self.last = None

        def dma_start(self, *a, **k):
            ins = self._eng.dma_start(*a, **k)
            ins.then_inc(self._sem, 16)
            self.nd += 1
            return ins

        def dma_start_transpose(self, *a, **k):
            ins = self._eng.dma_start_transpose(*a, **k)
            ins.then_inc(self._sem, 16)
            self.nd += 1
            return ins

        def __getattr__(self, n):
            f = getattr(self._eng, n)
            if callable(f):
                def wrap(*a, **k):
                    r = f(*a, **k)
                    if hasattr(r, "then_inc"):
                        self.last = r
                    return r
                return wrap
            return f

    with nc.Block() as block:
        def runner(ename):
            def f(eng):
                for si, (en, cb, nd, ex, ndr) in enumerate(SEGS):
                    if en != ename:
                        continue
                    if cum[si] > 0:
                        eng.wait_ge(SB, cum[si])
                    p = Proxy(eng, SB)
                    cb(p)
                    assert p.nd == nd, (si, ename, p.nd, nd)
                    if ename == "tensor":
                        assert p.last is not None, si
                        p.last.then_inc(SB, 1)
                    elif ename in ("vector", "scalar") and p.last is not None:
                        if ndr:
                            p.last.then_inc(SB, 1)
                        else:
                            eng.drain().then_inc(SB, 1)
                    elif p.last is not None:
                        p.last.then_inc(SB, 1)
                    else:
                        eng.nop().then_inc(SB, 1)
            return f

        block.sync(runner("sync"))
        block.tensor(runner("tensor"))
        block.vector(runner("vector"))
        block.scalar(runner("scalar"))
        block.gpsimd(runner("gpsimd"))

    ges.close()
    return nc


_NC_CACHE = {}


def _consts():
    r = np.arange(CH)[:, None]
    c = np.arange(CH)[None, :]
    incl = (r <= c).astype(np.float32)
    rec = ((r <= 63).astype(np.float32) - incl)
    rear_neg = -(r > c).astype(np.float32)
    strict = (r < c).astype(np.float32)
    o = {}
    o["m_incl_neg"] = -incl
    o["m_rec"] = rec
    o["m_rear_neg"] = rear_neg
    o["sm_incl4"] = np.tile(incl, (1, 4)).astype(BF).view(np.uint16)
    o["sm_strict4"] = np.tile(strict, (1, 4)).astype(BF).view(np.uint16)
    o["ident_f"] = np.eye(CH, dtype=np.float32)
    o["ident_b"] = np.eye(CH, dtype=np.float32).astype(BF).view(np.uint16)
    o["ones_row"] = np.ones((1, CH), np.float32)
    o["onecol_f"] = np.ones((CH, 1), np.float32)
    o["negcol_f"] = -np.ones((CH, 1), np.float32)
    return o


def kernel(L0_attn_x, L0_attn_W_q, L0_attn_W_k, L0_attn_W_v, L0_attn_Wg_K,
           L0_attn_bg_K, L0_attn_Wg_C, L0_attn_bg_C, L0_attn_W_o,
           L0_ffn_w_gate, L0_ffn_w_up, L0_ffn_w_down):
    x = np.ascontiguousarray(np.asarray(L0_attn_x, np.float32))
    B, T, _ = x.shape
    if "nc" not in _NC_CACHE:
        _NC_CACHE["nc"] = build_program()
    nc = _NC_CACHE["nc"]

    shared = {
        "Wq": np.ascontiguousarray(np.asarray(L0_attn_W_q, np.float32)),
        "Wk": np.ascontiguousarray(np.asarray(L0_attn_W_k, np.float32)),
        "Wv": np.ascontiguousarray(np.asarray(L0_attn_W_v, np.float32)),
        "WgK": np.ascontiguousarray(np.asarray(L0_attn_Wg_K, np.float32)),
        "WgC": np.ascontiguousarray(np.asarray(L0_attn_Wg_C, np.float32)),
        "Wo": np.ascontiguousarray(np.asarray(L0_attn_W_o, np.float32)),
        "bgK": np.ascontiguousarray(np.asarray(L0_attn_bg_K, np.float32)).reshape(1, D),
        "bgC": np.ascontiguousarray(np.asarray(L0_attn_bg_C, np.float32)).reshape(1, D),
        "wgT": np.ascontiguousarray(np.asarray(L0_ffn_w_gate, np.float32).T),
        "wuT": np.ascontiguousarray(np.asarray(L0_ffn_w_up, np.float32).T),
        "wdT": np.ascontiguousarray(np.asarray(L0_ffn_w_down, np.float32).T),
        **_consts(),
    }
    in_maps = []
    for core in range(8):
        b, pos = core // 4, core % 4
        sel = np.zeros((CH, 9), np.float32)
        for q in range(3):
            sel[:, 6 + q] = 1.0
            if q < pos:
                sel[:, q] = 1.0
                sel[:, 3 + q] = -1.0
                sel[:, 6 + q] = 0.0
        m = dict(shared)
        m["x_in"] = np.ascontiguousarray(x[b, pos * NT:(pos + 1) * NT, :])
        m["predsel"] = sel
        in_maps.append(m)

    res = run_bass_kernel_spmd(nc, in_maps, core_ids=list(range(8)))
    out = np.empty((B, T, D), np.float32)
    for core in range(8):
        b, pos = core // 4, core % 4
        out[b, pos * NT:(pos + 1) * NT, :] = res.results[core]["out_t"]
    return out


if __name__ == "__main__":
    nc = build_program()
    print("built ok")


# version 14
# speedup vs baseline: 1.0848x; 1.0690x over previous
import numpy as np
import ml_dtypes
import concourse.bass as bass
import concourse.mybir as mybir
from concourse.bass_utils import run_bass_kernel_spmd

F32 = mybir.dt.float32
BF16 = mybir.dt.bfloat16
I32 = mybir.dt.int32
AF = mybir.ActivationFunctionType
ALU = mybir.AluOpType
AX = mybir.AxisListType

D = 1024
H = 16
DK = 64
NT = 4096
CH = 128
NCH = 32
FC = 8
TB = 512
NTB = 8
GRP = 512
NGRP = 8
INNER = 4096
EPS = 1e-5
BF = ml_dtypes.bfloat16


def build_program(na=NCH, relay=True, nb=NCH, tern=True, ngr=NGRP, dbg=False,
                  nstop=None):
    nc = bass.Bass("TRN2", target_bir_lowering=False)

    def din(name, shape, dt=F32):
        return nc.dram_tensor(name, shape, dt, kind="ExternalInput").ap()

    x_in = din("x_in", [NT, D])
    WHBM = {w: din("W" + w, [D, D]) for w in ["q", "k", "v", "gK", "gC", "o"]}
    bgK = din("bgK", [1, D])
    bgC = din("bgC", [1, D])
    wgT = din("wgT", [D, INNER])
    wuT = din("wuT", [D, INNER])
    wdT = din("wdT", [INNER, D])
    m_incl_neg = din("m_incl_neg", [CH, CH])
    m_rec = din("m_rec", [CH, CH])
    m_rear_neg = din("m_rear_neg", [CH, CH])
    sm_incl4 = din("sm_incl4", [CH, 4 * CH], BF16)
    sm_strict4 = din("sm_strict4", [CH, 4 * CH], BF16)
    ident_f = din("ident_f", [CH, CH])
    ident_b = din("ident_b", [CH, CH], BF16)
    ones_row = din("ones_row", [1, CH])
    onecol_f = din("onecol_f", [CH, 1])
    negcol_f = din("negcol_f", [CH, 1])
    predsel = din("predsel", [CH, 9])

    out_t = nc.dram_tensor("out_t", [NT, D], F32, kind="ExternalOutput").ap()

    def scr(name, shape, dt):
        kind = "ExternalOutput" if dbg else "Internal"
        return nc.dram_tensor(name, shape, dt, kind=kind).ap()

    S_qt = scr("S_qt", [NT, D], BF16)
    S_kt = scr("S_kt", [NT, D], BF16)
    S_vt = scr("S_vt", [NT, D], BF16)
    S_lA = scr("S_lA", [NT, D], F32)
    S_lB = scr("S_lB", [NT, D], F32)
    S_xp = scr("S_xp", [NT, D], F32)
    TERN_G = scr("TERN_G", [CH, 32 * 8 * CH], BF16)
    TERN_U = scr("TERN_U", [CH, 32 * 8 * CH], BF16)
    TERN_D = scr("TERN_D", [INNER, D], BF16)
    cc_kind = "ExternalOutput" if (dbg and not relay) else "Internal"
    CC_IN = nc.dram_tensor("CC_IN", [CH, 2080], F32, kind=cc_kind).ap()
    CC_OUT = nc.dram_tensor("CC_OUT", [4 * CH, 2080], F32).ap()

    from contextlib import ExitStack
    ges = ExitStack()

    def gsb(name, shape, dt=F32):
        return ges.enter_context(nc.sbuf_tensor(name, shape, dt))

    cmask = gsb("cmask", [CH, 3 * CH])
    smasks = gsb("smasks", [CH, 8 * CH], BF16)
    ident_fs = gsb("ident_fs", [CH, CH])
    ident_bs = gsb("ident_bs", [CH, CH], BF16)
    ones_row_s = gsb("ones_row_s", [1, CH])
    onecol_fs = gsb("onecol_fs", [CH, 1])
    negcol_fs = gsb("negcol_fs", [CH, 1])
    psc = gsb("psc", [CH, 9])
    rowscr = gsb("rowscr", [1, 2 * D])
    rowsf = gsb("rowsf", [1, 4 * D])
    gcols = gsb("gcols", [CH, FC * 4])
    St_S = gsb("St_S", [CH, FC * DK])
    St_C = gsb("St_C", [CH, FC * DK])
    St_G = gsb("St_G", [CH, FC * DK])
    St_H = gsb("St_H", [CH, FC * DK])
    tmp_st = gsb("tmp_st", [CH, FC * DK])
    avec = gsb("avec", [CH, FC])
    bvec = gsb("bvec", [CH, FC])
    Sb_S = gsb("Sb_S", [CH, FC * DK], BF16)
    Sb_G = gsb("Sb_G", [CH, FC * DK], BF16)
    c0m = gsb("c0m", [CH, FC * DK], BF16)
    comp_bf = gsb("comp_bf", [CH, FC * DK], BF16)
    comp_bf2 = gsb("comp_bf2", [CH, FC * DK], BF16)
    agg = gsb("agg", [CH, 2080])
    mcol = gsb("mcol", [CH, 4])
    vcol = gsb("vcol", [CH, 4])
    rcol = gsb("rcol", [CH, 4])
    nmcol = gsb("nmcol", [CH, 4])
    tcol = gsb("tcol", [CH, 4])
    ncol = gsb("ncol", [CH, 4])
    f_M = gsb("f_M", [CH, D], BF16)
    f_Qh = gsb("f_Qh", [CH, D], BF16)
    t_Kr = gsb("t_Kr", [CH, D], BF16)
    t_Qr = gsb("t_Qr", [CH, D], BF16)
    t_Mt = gsb("t_Mt", [CH, D], BF16)
    mS3 = gsb("mS3", [CH, 16 * CH], BF16)
    dneg = gsb("dneg", [CH, 16 * DK], BF16)

    PB = [ges.enter_context(nc.psum_tensor(f"PB{i}", [CH, 512], F32))
          for i in range(8)]
    SB = ges.enter_context(nc.semaphore("SB"))
    SD = [ges.enter_context(nc.semaphore("SD0")),
          ges.enter_context(nc.semaphore("SD1"))]
    SPE = ges.enter_context(nc.semaphore("SPE"))

    SEGS = []
    EN = {"on": True}

    def stage(engine, cb, ndma=0, extra=0, nodrain=False, relax=None, waits=()):
        if EN["on"]:
            SEGS.append((engine, cb, ndma, extra, nodrain, relax, waits))

    def emit_ln_tmajor(src_aps, nsl, sqbuf, w=D):
        def sums(e):
            for s in range(nsl):
                e.tensor_reduce(mcol[:, s:s + 1], src_aps(s), AX.X, ALU.add)
            for s in range(nsl):
                e.tensor_tensor(sqbuf[:, 0:w], src_aps(s), src_aps(s), ALU.mult)
                e.drain()
                e.tensor_reduce(vcol[:, s:s + 1], sqbuf[:, 0:w], AX.X, ALU.add)
                e.drain()
            e.tensor_tensor(tcol[:, 0:nsl], mcol[:, 0:nsl], mcol[:, 0:nsl], ALU.mult)
            e.tensor_scalar(ncol[:, 0:nsl], mcol[:, 0:nsl], -1.0 / w, None, ALU.mult)
            e.drain()
            e.scalar_tensor_tensor(vcol[:, 0:nsl], tcol[:, 0:nsl], -1.0 / w,
                                   vcol[:, 0:nsl], ALU.mult, ALU.add)
            e.drain()
            e.tensor_scalar(vcol[:, 0:nsl], vcol[:, 0:nsl], 1.0 / w, EPS,
                            ALU.mult, ALU.add)
        stage("vector", sums)

        def fln(e):
            e.activation(tcol[:, 0:nsl], vcol[:, 0:nsl], AF.Ln, bias=0.0, scale=1.0)
        stage("scalar", fln)

        def fexp(e):
            e.activation(rcol[:, 0:nsl], tcol[:, 0:nsl], AF.Exp, bias=0.0, scale=-0.5)
        stage("scalar", fexp, nodrain=True)

        def fin(e):
            e.tensor_tensor(nmcol[:, 0:nsl], ncol[:, 0:nsl], rcol[:, 0:nsl], ALU.mult)
        stage("vector", fin)

    def emit_rows(lA_ap, lB_ap):
        def rows_mm(e):
            e.matmul(PB[0][0:1, :], negcol_fs[:], lA_ap()[:, 0:512], start=True, stop=True)
            e.matmul(PB[1][0:1, :], negcol_fs[:], lA_ap()[:, 512:1024], start=True, stop=True)
            e.matmul(PB[2][0:1, :], negcol_fs[:], lB_ap()[:, 0:512], start=True, stop=True)
            e.matmul(PB[3][0:1, :], negcol_fs[:], lB_ap()[:, 512:1024], start=True, stop=True)
            e.matmul(PB[4][0:1, :], negcol_fs[0:DK, :], lA_ap()[0:DK, 0:512], start=True, stop=True)
            e.matmul(PB[5][0:1, :], negcol_fs[0:DK, :], lA_ap()[0:DK, 512:1024], start=True, stop=True)
            e.matmul(PB[6][0:1, :], negcol_fs[0:DK, :], lB_ap()[0:DK, 0:512], start=True, stop=True)
            e.matmul(PB[7][0:1, :], negcol_fs[0:DK, :], lB_ap()[0:DK, 512:1024], start=True, stop=True)
        stage("tensor", rows_mm)

        def rows_exp(e):
            for j, (b0, b1) in enumerate([(0, 1), (2, 3), (4, 5), (6, 7)]):
                e.activation(rowsf[0:1, j * D:j * D + 512], PB[b0][0:1, :],
                             AF.Exp, bias=0.0, scale=1.0)
                e.activation(rowsf[0:1, j * D + 512:(j + 1) * D], PB[b1][0:1, :],
                             AF.Exp, bias=0.0, scale=1.0)
        stage("scalar", rows_exp, nodrain=True)

        def rows_tr(e):
            for fc in range(FC):
                for j in range(4):
                    e.transpose(PB[0][:, fc * 4 + j: fc * 4 + j + 1],
                                rowsf[0:1, j * D + fc * CH: j * D + (fc + 1) * CH],
                                ident_fs[0:1, 0:1])
        stage("tensor", rows_tr)

        def rows_ev(e):
            e.tensor_copy(gcols[:, 0:32], PB[0][:, 0:32])
        stage("vector", rows_ev)

    def emit_states(kr_buf, qr_buf, v_buf, is_A, htmp=None):
        def stmm(e):
            for h in range(H):
                par, fc = h % 2, h // 2
                p0, c0 = par * DK, fc * DK
                kr = kr_buf[:, h * DK:(h + 1) * DK]
                qr = qr_buf[:, h * DK:(h + 1) * DK]
                vv = v_buf[:, h * DK:(h + 1) * DK]
                dn = dneg[:, h * DK:(h + 1) * DK]
                e.matmul(PB[0][p0:p0 + DK, c0:c0 + DK], kr, kr,
                         start=True, stop=True, tile_position=(0, p0))
                e.matmul(PB[1][p0:p0 + DK, c0:c0 + DK], qr, vv,
                         start=True, stop=True, tile_position=(0, p0))
                e.matmul(PB[2][p0:p0 + DK, c0:c0 + DK], kr, dn,
                         start=True, stop=True, tile_position=(0, p0))
                if is_A:
                    mt = t_Mt[:, h * DK:(h + 1) * DK]
                    e.matmul(PB[3][p0:p0 + DK, c0:c0 + DK], mt, kr,
                             start=True, stop=True, tile_position=(0, p0))
            for h in range(H):
                par, fc = h % 2, h // 2
                p0, c0 = par * DK, fc * DK
                grow = rowsf[0:1, fc * CH + par * DK: fc * CH + (par + 1) * DK]
                e.matmul(PB[4][p0:p0 + DK, c0:c0 + DK], grow, grow,
                         start=True, stop=True, tile_position=(0, p0))
                if is_A:
                    e.matmul(PB[5][p0:p0 + DK, c0:c0 + DK],
                             ones_row_s[0:1, 0:DK], grow,
                             start=True, stop=True, tile_position=(0, p0))
        stage("tensor", stmm)

        def stev(e):
            e.tensor_tensor(tmp_st[:], St_S[:], PB[4][:, :], ALU.mult)
            if is_A:
                e.tensor_tensor(htmp[:, 0:FC * DK], St_H[:], PB[5][:, :], ALU.mult)
            e.drain()
            e.tensor_tensor(St_S[:], tmp_st[:], PB[0][:, :], ALU.add)
            for fc in range(FC):
                s = slice(fc * DK, (fc + 1) * DK)
                e.scalar_tensor_tensor(St_C[:, s], St_C[:, s],
                                       gcols[:, fc * 4 + 1: fc * 4 + 2],
                                       PB[1][:, s], ALU.mult, ALU.add)
                e.scalar_tensor_tensor(St_G[:, s], St_G[:, s],
                                       gcols[:, fc * 4 + 0: fc * 4 + 1],
                                       PB[2][:, s], ALU.mult, ALU.add)
                if is_A:
                    e.scalar_tensor_tensor(St_H[:, s], PB[3][:, s],
                                           bvec[:, fc:fc + 1],
                                           htmp[:, s], ALU.mult, ALU.add)
            if is_A:
                for fc in range(FC):
                    e.tensor_tensor(avec[:, fc:fc + 1], avec[:, fc:fc + 1],
                                    gcols[:, fc * 4:fc * 4 + 1], ALU.mult)
                    e.tensor_tensor(bvec[:, fc:fc + 1], bvec[:, fc:fc + 1],
                                    gcols[:, fc * 4 + 1:fc * 4 + 2], ALU.mult)
        stage("vector", stev)

    aes = ExitStack()

    def asb(name, shape, dt=F32):
        return aes.enter_context(nc.sbuf_tensor(name, shape, dt))

    Wsb = {w: asb(f"Wsb_{w}", [CH, FC * D], BF16) for w in ["q", "k", "v", "gK", "gC"]}
    bg_bcast = asb("bg_bcast", [CH, 2 * D])
    xr_t = asb("xr_t", [CH, D])
    xq1 = asb("xq1", [CH, D])
    xhat_t = asb("xhat_t", [CH, D], BF16)
    xhat_f = asb("xhat_f", [CH, D], BF16)
    qt_t = asb("qt_t", [CH, D], BF16)
    kt_t = asb("kt_t", [CH, D], BF16)
    vt_t = asb("vt_t", [CH, D], BF16)
    lA_t = asb("lA_t", [CH, D])
    lB_t = asb("lB_t", [CH, D])
    qf = asb("qf", [CH, D], BF16)
    kf = asb("kf", [CH, D], BF16)

    def p0_loads(e):
        e.dma_start(out=cmask[:, 0:CH], in_=m_incl_neg)
        e.dma_start(out=cmask[:, CH:2 * CH], in_=m_rec)
        e.dma_start(out=cmask[:, 2 * CH:3 * CH], in_=m_rear_neg)
        e.dma_start(out=smasks[:, 0:4 * CH], in_=sm_incl4)
        e.dma_start(out=smasks[:, 4 * CH:8 * CH], in_=sm_strict4)
        e.dma_start(out=ident_fs[:], in_=ident_f)
        e.dma_start(out=ident_bs[:], in_=ident_b)
        e.dma_start(out=ones_row_s[:], in_=ones_row)
        e.dma_start(out=onecol_fs[:], in_=onecol_f)
        e.dma_start(out=negcol_fs[:], in_=negcol_f)
        e.dma_start(out=psc[:], in_=predsel)
        e.dma_start(out=rowscr[0:1, 0:D], in_=bgK)
        e.dma_start(out=rowscr[0:1, D:2 * D], in_=bgC)
    stage("sync", p0_loads, ndma=13)

    def zero_states(e):
        e.memset(St_S[:], 0.0)
        e.memset(St_C[:], 0.0)
        e.memset(St_G[:], 0.0)
        e.memset(St_H[:], 0.0)
        e.memset(avec[:], 1.0)
        e.memset(bvec[:], 1.0)
    stage("vector", zero_states, nodrain=True)

    def bg_mm(e):
        for i in range(4):
            e.matmul(PB[i][:, :], ones_row_s[0:1, :],
                     rowscr[0:1, i * 512:(i + 1) * 512], start=True, stop=True)
    stage("tensor", bg_mm)

    def bg_ev(e):
        for i in range(4):
            e.tensor_copy(bg_bcast[:, i * 512:(i + 1) * 512], PB[i][:, :])
    stage("vector", bg_ev)

    for wname in ["q", "k", "v", "gK", "gC"]:
        for hf in range(4):
            def wload(e, wname=wname, hf=hf):
                src = WHBM[wname].rearrange("(c p) o -> p c o", p=CH)
                e.dma_start(out=xr_t[:].rearrange("p (c o) -> p c o", c=1),
                            in_=src[:, 2 * hf:2 * hf + 1, :])
                e.dma_start(out=xq1[:].rearrange("p (c o) -> p c o", c=1),
                            in_=src[:, 2 * hf + 1:2 * hf + 2, :])
            stage("sync", wload, ndma=2)

            def wcast(e, wname=wname, hf=hf):
                e.tensor_copy(Wsb[wname][:, (2 * hf) * D:(2 * hf + 1) * D], xr_t[:])
                e.tensor_copy(Wsb[wname][:, (2 * hf + 1) * D:(2 * hf + 2) * D], xq1[:])
            stage("vector", wcast, nodrain=True)

    def emit_A(cc):
        r0 = cc * CH

        def xload(e, r0=r0):
            e.dma_start(out=xr_t[:], in_=x_in[r0:r0 + CH, :])
        stage("sync", xload, ndma=1)

        emit_ln_tmajor(lambda s: xr_t[:, 0:D], 1, xq1)

        def xhat_mk(e):
            e.tensor_scalar(xhat_t[:], xr_t[:], rcol[:, 0:1], nmcol[:, 0:1],
                            ALU.mult, ALU.add)
        stage("vector", xhat_mk, nodrain=True)

        def xtr(e):
            e.dma_start_transpose(
                xhat_f[:].rearrange("p (c t) -> p c t", c=FC), xhat_t[:])
        stage("sync", xtr, ndma=1)

        for half in range(2):
            def tproj(e, half=half):
                for wi, wname in enumerate(["q", "k", "v", "gK", "gC"]):
                    for dc in range(FC):
                        e.matmul(PB[wi][:, :],
                                 xhat_f[:, dc * CH:(dc + 1) * CH],
                                 Wsb[wname][:, dc * D + half * 512: dc * D + (half + 1) * 512],
                                 start=(dc == 0), stop=(dc == FC - 1))
            stage("tensor", tproj)

            def bias_add(e, half=half):
                e.tensor_tensor(PB[3][:, :], PB[3][:, :],
                                bg_bcast[:, half * 512:(half + 1) * 512], ALU.add)
                e.tensor_tensor(PB[4][:, :], PB[4][:, :],
                                bg_bcast[:, D + half * 512: D + (half + 1) * 512], ALU.add)
            stage("vector", bias_add, nodrain=True)

            def evac(e, half=half):
                hs = slice(half * 512, (half + 1) * 512)
                e.activation(qt_t[:, hs], PB[0][:, :], AF.Identity, bias=0.0, scale=1.0)
                e.activation(kt_t[:, hs], PB[1][:, :], AF.Identity, bias=0.0, scale=1.0)
                e.activation(vt_t[:, hs], PB[2][:, :], AF.Identity, bias=0.0, scale=1.0)
                e.activation(xq1[:, 0:512], PB[3][:, :], AF.Exp, bias=0.0, scale=-1.0)
                e.activation(xq1[:, 512:1024], PB[4][:, :], AF.Exp, bias=0.0, scale=-1.0)
            stage("scalar", evac, nodrain=True)

            def evac_p1(e, half=half):
                e.tensor_scalar(xq1[:], xq1[:], 1.0, None, ALU.add)
            stage("vector", evac_p1, nodrain=True)

            def evac_ln(e, half=half):
                hs = slice(half * 512, (half + 1) * 512)
                e.activation(lA_t[:, hs], xq1[:, 0:512], AF.Ln, bias=0.0, scale=1.0)
                e.activation(lB_t[:, hs], xq1[:, 512:1024], AF.Ln, bias=0.0, scale=1.0)
            stage("scalar", evac_ln, nodrain=True)

        def qk_tr(e):
            e.dma_start_transpose(
                qf[:].rearrange("p (c t) -> p c t", c=FC), qt_t[:])
            e.dma_start_transpose(
                kf[:].rearrange("p (c t) -> p c t", c=FC), kt_t[:])
        stage("sync", qk_tr, ndma=2)

        def store(e, r0=r0):
            e.dma_start(out=S_qt[r0:r0 + CH, :], in_=qt_t[:])
            e.dma_start(out=S_kt[r0:r0 + CH, :], in_=kt_t[:])
            e.dma_start(out=S_vt[r0:r0 + CH, :], in_=vt_t[:])
            e.dma_start(out=S_lA[r0:r0 + CH, :], in_=lA_t[:])
            e.dma_start(out=S_lB[r0:r0 + CH, :], in_=lB_t[:])
        stage("gpsimd", store, ndma=5)

        emit_rows(lambda: lA_t[:], lambda: lB_t[:])

        def cums(e):
            for fc in range(FC):
                sl = PB[fc // 4][:, (fc % 4) * CH:(fc % 4 + 1) * CH]
                e.matmul(sl, lB_t[:, fc * CH:(fc + 1) * CH], cmask[:, CH:2 * CH],
                         start=True, stop=True)
            e.matmul(PB[2][:, :], cmask[:, 2 * CH:3 * CH], lA_t[:, 0:512], start=True, stop=True)
            e.matmul(PB[3][:, :], cmask[:, 2 * CH:3 * CH], lA_t[:, 512:1024], start=True, stop=True)
            e.matmul(PB[4][:, :], cmask[:, 2 * CH:3 * CH], lB_t[:, 0:512], start=True, stop=True)
            e.matmul(PB[5][:, :], cmask[:, 2 * CH:3 * CH], lB_t[:, 512:1024], start=True, stop=True)
            e.matmul(PB[6][:, :], cmask[:, 0:CH], lB_t[:, 0:512], start=True, stop=True)
            e.matmul(PB[7][:, :], cmask[:, 0:CH], lB_t[:, 512:1024], start=True, stop=True)
        stage("tensor", cums)

        def exps(e):
            for fc in range(FC):
                sl = PB[fc // 4][:, (fc % 4) * CH:(fc % 4 + 1) * CH]
                e.activation(f_M[:, fc * CH:(fc + 1) * CH], sl, AF.Exp, bias=0.0, scale=1.0)
                e.activation(f_Qh[:, fc * CH:(fc + 1) * CH], sl, AF.Exp, bias=0.0, scale=-1.0)
            e.activation(t_Kr[:, 0:512], PB[2][:, :], AF.Exp, bias=0.0, scale=1.0)
            e.activation(t_Kr[:, 512:1024], PB[3][:, :], AF.Exp, bias=0.0, scale=1.0)
            e.activation(t_Qr[:, 0:512], PB[4][:, :], AF.Exp, bias=0.0, scale=1.0)
            e.activation(t_Qr[:, 512:1024], PB[5][:, :], AF.Exp, bias=0.0, scale=1.0)
            e.activation(t_Mt[:, 0:512], PB[6][:, :], AF.Exp, bias=0.0, scale=1.0)
            e.activation(t_Mt[:, 512:1024], PB[7][:, :], AF.Exp, bias=0.0, scale=1.0)
        stage("scalar", exps, nodrain=True)

        def facA(e):
            e.tensor_tensor(f_Qh[:], qf[:], f_Qh[:], ALU.mult)
            e.tensor_tensor(f_M[:], kf[:], f_M[:], ALU.mult)
            e.tensor_tensor(t_Kr[:], kt_t[:], t_Kr[:], ALU.mult)
            e.tensor_tensor(t_Qr[:], qt_t[:], t_Qr[:], ALU.mult)
            e.tensor_tensor(t_Mt[:], kt_t[:], t_Mt[:], ALU.mult)
            for fc in range(FC):
                e.tensor_scalar(c0m[:, fc * DK:(fc + 1) * DK],
                                St_C[:, fc * DK:(fc + 1) * DK],
                                gcols[:, fc * 4 + 3: fc * 4 + 4], None, ALU.mult)
        stage("vector", facA, nodrain=True)

        for hg in range(2):
            def s3mm(e, hg=hg):
                for hh in range(8):
                    h = hg * 8 + hh
                    par, fc = h % 2, h // 2
                    p0 = par * DK
                    sl = PB[par][:, (hh // 2) * CH:(hh // 2 + 1) * CH]
                    e.matmul(sl,
                             f_Qh[p0:p0 + DK, fc * CH:(fc + 1) * CH],
                             f_M[p0:p0 + DK, fc * CH:(fc + 1) * CH],
                             start=True, stop=True,
                             tile_position=(p0, 0))
            stage("tensor", s3mm)

            def s3mask(e, hg=hg):
                for hh in range(8):
                    h = hg * 8 + hh
                    e.tensor_tensor(
                        mS3[:, h * CH:(h + 1) * CH],
                        PB[hh % 2][:, (hh // 2) * CH:(hh // 2 + 1) * CH],
                        smasks[:, 4 * CH:5 * CH], ALU.mult)
            stage("vector", s3mask, nodrain=True)

            def dmm(e, hg=hg):
                for hh in range(8):
                    h = hg * 8 + hh
                    par, fc = h % 2, h // 2
                    p0 = par * DK
                    dsl = PB[2 + par][:, (hh // 2) * DK:(hh // 2 + 1) * DK]
                    e.matmul(dsl, mS3[:, h * CH:(h + 1) * CH],
                             vt_t[:, h * DK:(h + 1) * DK], start=True, stop=False)
                    e.matmul(dsl,
                             f_M[p0:p0 + DK, fc * CH:(fc + 1) * CH],
                             c0m[p0:p0 + DK, fc * DK:(fc + 1) * DK],
                             start=False, stop=True, tile_position=(p0, 0))
            stage("tensor", dmm)

            def dneg_ev(e, hg=hg):
                for hh in range(8):
                    h = hg * 8 + hh
                    e.tensor_scalar(dneg[:, h * DK:(h + 1) * DK],
                                    PB[2 + hh % 2][:, (hh // 2) * DK:(hh // 2 + 1) * DK],
                                    -1.0, None, ALU.mult)
            stage("vector", dneg_ev, nodrain=True)

        emit_states(t_Kr, t_Qr, vt_t, is_A=True, htmp=xq1)

    for cc in range(na):
        emit_A(cc)

    aes.close()

    bes = ExitStack()

    def bsb(name, shape, dt=F32):
        return bes.enter_context(nc.sbuf_tensor(name, shape, dt))

    Wosb = bsb("Wosb", [CH, FC * D], BF16)
    xB_t = bsb("xB_t", [CH, 4 * D])
    a2f = bsb("a2f", [CH, 4 * D])
    bsq = bsb("bsq", [CH, D])
    attn_f = bsb("attn_f", [CH, FC * TB], BF16)
    cf_q = bsb("cf_q", [CH, D], BF16)
    cf_k = bsb("cf_k", [CH, D], BF16)
    t_q = bsb("t_q", [CH, D], BF16)
    t_k = bsb("t_k", [CH, D], BF16)
    t_v = bsb("t_v", [CH, D], BF16)
    t_lA = bsb("t_lA", [CH, D])
    t_lB = bsb("t_lB", [CH, D])
    f_U = bsb("f_U", [CH, D], BF16)
    f_U0 = bsb("f_U0", [CH, D], BF16)
    f_Kh = bsb("f_Kh", [CH, D], BF16)
    f_F1 = bsb("f_F1", [CH, D], BF16)
    f_F2 = bsb("f_F2", [CH, D], BF16)
    t_Kh = bsb("t_Kh", [CH, D], BF16)
    ftmp = bsb("ftmp", [CH, D])
    ftmp2 = bsb("ftmp2", [CH, D])
    mS1 = bsb("mS1", [CH, 16 * CH], BF16)
    mS2 = bsb("mS2", [CH, 16 * CH], BF16)
    pt_f = bsb("pt_f", [CH, D], BF16)

    for hf in range(2):
        def wo_load(e, hf=hf):
            src = WHBM["o"].rearrange("(c p) o -> p c o", p=CH)
            e.dma_start(out=xB_t[:].rearrange("p (c o) -> p c o", c=4),
                        in_=src[:, 4 * hf:4 * hf + 4, :])
        stage("sync", wo_load, ndma=1)

        def wo_cast(e, hf=hf):
            e.tensor_copy(Wosb[:, hf * 4 * D:(hf + 1) * 4 * D], xB_t[:])
        stage("vector", wo_cast, nodrain=True)

    EN["on"] = relay or dbg

    def pack_agg(e):
        e.tensor_copy(agg[:, 0:512], St_S[:])
        e.tensor_copy(agg[:, 512:1024], St_C[:])
        e.tensor_copy(agg[:, 1024:1536], St_G[:])
        e.tensor_copy(agg[:, 1536:2048], St_H[:])
        e.tensor_copy(agg[:, 2048:2056], avec[:])
        e.tensor_copy(agg[:, 2056:2064], bvec[:])
        e.memset(agg[:, 2064:2080], 0.0)
    stage("vector", pack_agg, nodrain=True)

    def relay_out(e):
        e.dma_start(out=CC_IN[:, :], in_=agg[:, :])
    stage("gpsimd", relay_out, ndma=1)

    EN["on"] = relay

    def relay_cc(e):
        e.collective_compute(
            "AllGather", ALU.bypass,
            replica_groups=[[0, 1, 2, 3], [4, 5, 6, 7]],
            ins=[CC_IN[:, :]], outs=[CC_OUT[:, :]],
        )
    stage("gpsimd", relay_cc)

    EN["on"] = relay or nb > 0

    def comp_zero(e):
        e.memset(St_S[:], 0.0)
        e.memset(St_C[:], 0.0)
        e.memset(St_G[:], 0.0)
    stage("vector", comp_zero, nodrain=True)

    EN["on"] = relay

    for q in range(3):
        def agg_load(e, q=q):
            e.dma_start(out=agg[:, :], in_=CC_OUT[q * CH:(q + 1) * CH, :])
        stage("sync", agg_load, ndma=1)

        def comp_pre(e, q=q):
            e.tensor_scalar(avec[:], agg[:, 2048:2056], psc[:, q:q + 1],
                            psc[:, 6 + q:7 + q], ALU.mult, ALU.add)
            e.tensor_scalar(bvec[:], agg[:, 2056:2064], psc[:, q:q + 1],
                            psc[:, 6 + q:7 + q], ALU.mult, ALU.add)
            e.tensor_copy(comp_bf[:], St_C[:])
            e.tensor_copy(comp_bf2[:], agg[:, 1536:2048])
        stage("vector", comp_pre)

        def comp_tr(e, q=q):
            for fc in range(FC):
                e.transpose(PB[fc // 4][0:1, (fc % 4) * CH:(fc % 4 + 1) * CH],
                            avec[:, fc:fc + 1], ident_fs[:, :])
        stage("tensor", comp_tr)

        def comp_trev(e, q=q):
            e.tensor_copy(rowscr[0:1, 0:512], PB[0][0:1, :])
            e.tensor_copy(rowscr[0:1, 512:1024], PB[1][0:1, :])
        stage("vector", comp_trev, nodrain=True)

        def comp_mm(e, q=q):
            for h in range(H):
                par, fc = h % 2, h // 2
                p0, c0 = par * DK, fc * DK
                ar = rowscr[0:1, fc * CH + p0: fc * CH + p0 + DK]
                e.matmul(PB[1][p0:p0 + DK, c0:c0 + DK], ar, ar,
                         start=True, stop=True, tile_position=(0, p0))
                e.matmul(PB[3 + par][p0:p0 + DK, c0:c0 + DK],
                         comp_bf2[p0:p0 + DK, c0:c0 + DK],
                         comp_bf[p0:p0 + DK, c0:c0 + DK],
                         start=True, stop=True, tile_position=(p0, p0))
        stage("tensor", comp_mm)

        def comp_upd(e, q=q):
            sel = psc[:, q:q + 1]
            nsel = psc[:, 3 + q:4 + q]
            e.tensor_tensor(St_S[:], St_S[:], PB[1][:, :], ALU.mult)
            e.scalar_tensor_tensor(St_S[:], agg[:, 0:512], sel, St_S[:],
                                   ALU.mult, ALU.add)
            for fc in range(FC):
                s = slice(fc * DK, (fc + 1) * DK)
                e.tensor_scalar(St_G[:, s], St_G[:, s], avec[:, fc:fc + 1],
                                None, ALU.mult)
            e.scalar_tensor_tensor(St_G[0:DK, :], PB[3][0:DK, :],
                                   psc[0:DK, 3 + q:4 + q], St_G[0:DK, :],
                                   ALU.mult, ALU.add)
            e.scalar_tensor_tensor(St_G[DK:CH, :], PB[4][DK:CH, :],
                                   psc[DK:CH, 3 + q:4 + q], St_G[DK:CH, :],
                                   ALU.mult, ALU.add)
            e.scalar_tensor_tensor(St_G[:], agg[:, 1024:1536], sel, St_G[:],
                                   ALU.mult, ALU.add)
            for fc in range(FC):
                s = slice(fc * DK, (fc + 1) * DK)
                e.tensor_scalar(St_C[:, s], St_C[:, s], bvec[:, fc:fc + 1],
                                None, ALU.mult)
            e.scalar_tensor_tensor(St_C[:], agg[:, 512:1024], sel, St_C[:],
                                   ALU.mult, ALU.add)
        stage("vector", comp_upd, nodrain=True)

    def emit_B(cc):
        r0 = cc * CH
        tc = cc % 4

        def bload(e, r0=r0):
            e.dma_start(out=t_q[:], in_=S_qt[r0:r0 + CH, :])
            e.dma_start(out=t_k[:], in_=S_kt[r0:r0 + CH, :])
            e.dma_start(out=t_v[:], in_=S_vt[r0:r0 + CH, :])
            e.dma_start(out=t_lA[:], in_=S_lA[r0:r0 + CH, :])
            e.dma_start(out=t_lB[:], in_=S_lB[r0:r0 + CH, :])
        stage("sync", bload, ndma=5)

        def cf_tr(e):
            e.dma_start_transpose(
                cf_q[:].rearrange("p (c t) -> p c t", c=FC), t_q[:])
            e.dma_start_transpose(
                cf_k[:].rearrange("p (c t) -> p c t", c=FC), t_k[:])
        stage("sync", cf_tr, ndma=2)

        emit_rows(lambda: t_lA[:], lambda: t_lB[:])

        def cums_f(e):
            for fc in range(FC):
                sl0 = PB[0 + fc // 4][:, (fc % 4) * CH:(fc % 4 + 1) * CH]
                sl1 = PB[2 + fc // 4][:, (fc % 4) * CH:(fc % 4 + 1) * CH]
                sl2 = PB[4 + fc // 4][:, (fc % 4) * CH:(fc % 4 + 1) * CH]
                e.matmul(sl0, t_lA[:, fc * CH:(fc + 1) * CH], cmask[:, CH:2 * CH],
                         start=True, stop=True)
                e.matmul(sl1, t_lB[:, fc * CH:(fc + 1) * CH], cmask[:, CH:2 * CH],
                         start=True, stop=True)
                e.matmul(sl2, t_lA[:, fc * CH:(fc + 1) * CH], cmask[:, 0:CH],
                         start=True, stop=True)
        stage("tensor", cums_f)

        def fargs(e):
            e.tensor_copy(bsq[:, 0:512], PB[2][:, :])
            e.tensor_copy(bsq[:, 512:1024], PB[3][:, :])
            e.drain()
            e.tensor_tensor(ftmp[:, 0:512], PB[4][:, :], bsq[:, 0:512], ALU.add)
            e.tensor_tensor(ftmp[:, 512:1024], PB[5][:, :], bsq[:, 512:1024], ALU.add)
            e.tensor_tensor(ftmp2[:, 0:512], PB[0][:, :], bsq[:, 0:512], ALU.add)
            e.tensor_tensor(ftmp2[:, 512:1024], PB[1][:, :], bsq[:, 512:1024], ALU.add)
        stage("vector", fargs, nodrain=True)

        def exps_f(e):
            for fc in range(FC):
                sl = PB[fc // 4][:, (fc % 4) * CH:(fc % 4 + 1) * CH]
                sl2 = PB[2 + fc // 4][:, (fc % 4) * CH:(fc % 4 + 1) * CH]
                e.activation(f_U[:, fc * CH:(fc + 1) * CH], sl, AF.Exp, bias=0.0, scale=1.0)
                e.activation(f_Kh[:, fc * CH:(fc + 1) * CH], sl, AF.Exp, bias=0.0, scale=-1.0)
                e.activation(f_M[:, fc * CH:(fc + 1) * CH], sl2, AF.Exp, bias=0.0, scale=1.0)
                e.activation(f_Qh[:, fc * CH:(fc + 1) * CH], sl2, AF.Exp, bias=0.0, scale=-1.0)
            e.activation(f_F1[:, 0:512], ftmp[:, 0:512], AF.Exp, bias=0.0, scale=1.0)
            e.activation(f_F1[:, 512:1024], ftmp[:, 512:1024], AF.Exp, bias=0.0, scale=1.0)
            e.activation(f_F2[:, 0:512], ftmp2[:, 0:512], AF.Exp, bias=0.0, scale=1.0)
            e.activation(f_F2[:, 512:1024], ftmp2[:, 512:1024], AF.Exp, bias=0.0, scale=1.0)
        stage("scalar", exps_f, nodrain=True)

        def cums_t(e):
            e.matmul(PB[0][:, :], cmask[:, CH:2 * CH], t_lA[:, 0:512], start=True, stop=True)
            e.matmul(PB[1][:, :], cmask[:, CH:2 * CH], t_lA[:, 512:1024], start=True, stop=True)
            e.matmul(PB[2][:, :], cmask[:, 2 * CH:3 * CH], t_lA[:, 0:512], start=True, stop=True)
            e.matmul(PB[3][:, :], cmask[:, 2 * CH:3 * CH], t_lA[:, 512:1024], start=True, stop=True)
            e.matmul(PB[4][:, :], cmask[:, 2 * CH:3 * CH], t_lB[:, 0:512], start=True, stop=True)
            e.matmul(PB[5][:, :], cmask[:, 2 * CH:3 * CH], t_lB[:, 512:1024], start=True, stop=True)
        stage("tensor", cums_t)

        def exps_t(e):
            e.activation(t_Kh[:, 0:512], PB[0][:, :], AF.Exp, bias=0.0, scale=-1.0)
            e.activation(t_Kh[:, 512:1024], PB[1][:, :], AF.Exp, bias=0.0, scale=-1.0)
            e.activation(t_Kr[:, 0:512], PB[2][:, :], AF.Exp, bias=0.0, scale=1.0)
            e.activation(t_Kr[:, 512:1024], PB[3][:, :], AF.Exp, bias=0.0, scale=1.0)
            e.activation(t_Qr[:, 0:512], PB[4][:, :], AF.Exp, bias=0.0, scale=1.0)
            e.activation(t_Qr[:, 512:1024], PB[5][:, :], AF.Exp, bias=0.0, scale=1.0)
        stage("scalar", exps_t, nodrain=True)

        def facB(e):
            e.tensor_tensor(f_U[:], cf_q[:], f_U[:], ALU.mult)
            e.tensor_tensor(f_Kh[:], cf_k[:], f_Kh[:], ALU.mult)
            e.tensor_tensor(f_M[:], cf_k[:], f_M[:], ALU.mult)
            e.tensor_tensor(f_Qh[:], cf_q[:], f_Qh[:], ALU.mult)
            e.tensor_tensor(t_Kh[:], t_k[:], t_Kh[:], ALU.mult)
            e.tensor_tensor(t_Kr[:], t_k[:], t_Kr[:], ALU.mult)
            e.tensor_tensor(t_Qr[:], t_q[:], t_Qr[:], ALU.mult)
            for fc in range(FC):
                e.tensor_scalar(c0m[:, fc * DK:(fc + 1) * DK],
                                St_C[:, fc * DK:(fc + 1) * DK],
                                gcols[:, fc * 4 + 3: fc * 4 + 4], None, ALU.mult)
            e.tensor_copy(Sb_S[:], St_S[:])
            e.tensor_copy(Sb_G[:], St_G[:])
            e.drain()
            for fc in range(FC):
                e.tensor_scalar(f_U0[:, fc * CH:(fc + 1) * CH],
                                f_U[:, fc * CH:(fc + 1) * CH],
                                gcols[:, fc * 4 + 2: fc * 4 + 3], None, ALU.mult)
        stage("vector", facB, nodrain=True)

        for hg in range(2):
            def s23(e, hg=hg):
                for hh in range(8):
                    h = hg * 8 + hh
                    par, fc = h % 2, h // 2
                    p0 = par * DK
                    kh = f_Kh[p0:p0 + DK, fc * CH:(fc + 1) * CH]
                    uu = f_U[p0:p0 + DK, fc * CH:(fc + 1) * CH]
                    qh = f_Qh[p0:p0 + DK, fc * CH:(fc + 1) * CH]
                    mm = f_M[p0:p0 + DK, fc * CH:(fc + 1) * CH]
                    csl = (hh // 2) * CH
                    e.matmul(PB[par][:, csl:csl + CH], kh, uu,
                             start=True, stop=True, tile_position=(p0, 0))
                    e.matmul(PB[2 + par][:, csl:csl + CH], qh, mm,
                             start=True, stop=True, tile_position=(p0, 0))
            stage("tensor", s23)

            def s23mask(e, hg=hg):
                for hh in range(8):
                    h = hg * 8 + hh
                    csl = (hh // 2) * CH
                    e.tensor_tensor(mS2[:, h * CH:(h + 1) * CH],
                                    PB[hh % 2][:, csl:csl + CH],
                                    smasks[:, 0:CH], ALU.mult)
                    e.tensor_tensor(mS3[:, h * CH:(h + 1) * CH],
                                    PB[2 + hh % 2][:, csl:csl + CH],
                                    smasks[:, 4 * CH:5 * CH], ALU.mult)
            stage("vector", s23mask, nodrain=True)

            def pmm(e, hg=hg):
                for hh in range(8):
                    h = hg * 8 + hh
                    par, fc = h % 2, h // 2
                    p0 = par * DK
                    csl = (hh // 2) * CH
                    e.matmul(PB[4 + 2 * par][p0:p0 + DK, csl:csl + CH],
                             Sb_S[p0:p0 + DK, fc * DK:(fc + 1) * DK],
                             f_U0[p0:p0 + DK, fc * CH:(fc + 1) * CH],
                             start=True, stop=True, tile_position=(p0, p0))
                    e.matmul(PB[5][p0:p0 + DK, csl:csl + CH],
                             t_Kh[:, h * DK:(h + 1) * DK],
                             mS2[:, h * CH:(h + 1) * CH],
                             start=True, stop=True, tile_position=(0, p0))
            stage("tensor", pmm)

            def ptilde(e, hg=hg):
                o = hg * 512
                e.tensor_tensor(ftmp[0:DK, 0:512], f_F1[0:DK, o:o + 512],
                                PB[4][0:DK, :], ALU.mult)
                e.tensor_tensor(ftmp[DK:CH, 0:512], f_F1[DK:CH, o:o + 512],
                                PB[6][DK:CH, :], ALU.mult)
                e.tensor_tensor(ftmp2[:, 0:512], f_F2[:, o:o + 512], PB[5][:, :],
                                ALU.mult)
                e.drain()
                e.tensor_tensor(pt_f[:, o:o + 512], ftmp[:, 0:512],
                                ftmp2[:, 0:512], ALU.add)
            stage("vector", ptilde, nodrain=True)

            def s1mm(e, hg=hg):
                for hh in range(8):
                    h = hg * 8 + hh
                    par, fc = h % 2, h // 2
                    p0 = par * DK
                    qh = f_Qh[p0:p0 + DK, fc * CH:(fc + 1) * CH]
                    pt = pt_f[p0:p0 + DK, fc * CH:(fc + 1) * CH]
                    csl = (hh // 2) * CH
                    e.matmul(PB[par][:, csl:csl + CH], qh, pt,
                             start=True, stop=True, tile_position=(p0, 0))
            stage("tensor", s1mm)

            def s1mask(e, hg=hg):
                for hh in range(8):
                    h = hg * 8 + hh
                    csl = (hh // 2) * CH
                    e.tensor_tensor(mS1[:, h * CH:(h + 1) * CH],
                                    PB[hh % 2][:, csl:csl + CH],
                                    smasks[:, 0:CH], ALU.mult)
            stage("vector", s1mask, nodrain=True)

            def dmmB(e, hg=hg):
                for hh in range(8):
                    h = hg * 8 + hh
                    par, fc = h % 2, h // 2
                    p0 = par * DK
                    dsl = PB[2 + par][:, (hh // 2) * DK:(hh // 2 + 1) * DK]
                    e.matmul(dsl, mS3[:, h * CH:(h + 1) * CH],
                             t_v[:, h * DK:(h + 1) * DK], start=True, stop=False)
                    e.matmul(dsl,
                             f_M[p0:p0 + DK, fc * CH:(fc + 1) * CH],
                             c0m[p0:p0 + DK, fc * DK:(fc + 1) * DK],
                             start=False, stop=True, tile_position=(p0, 0))
            stage("tensor", dmmB)

            def dnegB(e, hg=hg):
                for hh in range(8):
                    h = hg * 8 + hh
                    e.tensor_scalar(dneg[:, h * DK:(h + 1) * DK],
                                    PB[2 + hh % 2][:, (hh // 2) * DK:(hh // 2 + 1) * DK],
                                    -1.0, None, ALU.mult)
            stage("vector", dnegB, nodrain=True)

            def omm(e, hg=hg):
                for hh in range(8):
                    h = hg * 8 + hh
                    par, fc = h % 2, h // 2
                    p0 = par * DK
                    osl = PB[6 + par][p0:p0 + DK, (hh // 2) * CH:(hh // 2 + 1) * CH]
                    e.matmul(osl, t_v[:, h * DK:(h + 1) * DK],
                             mS1[:, h * CH:(h + 1) * CH], start=True, stop=False,
                             tile_position=(0, p0))
                    e.matmul(osl, c0m[p0:p0 + DK, fc * DK:(fc + 1) * DK],
                             pt_f[p0:p0 + DK, fc * CH:(fc + 1) * CH],
                             start=False, stop=False, tile_position=(p0, p0))
                    e.matmul(osl, Sb_G[p0:p0 + DK, fc * DK:(fc + 1) * DK],
                             f_U0[p0:p0 + DK, fc * CH:(fc + 1) * CH],
                             start=False, stop=False, tile_position=(p0, p0))
                    e.matmul(osl, dneg[:, h * DK:(h + 1) * DK],
                             mS2[:, h * CH:(h + 1) * CH], start=False, stop=True,
                             tile_position=(0, p0))
            stage("tensor", omm)

            def oev(e, hg=hg, tc=tc):
                e.tensor_copy(
                    attn_f[0:DK, :].rearrange("p (c n) -> p c n", c=FC)[
                        :, hg * 4:(hg + 1) * 4, tc * CH:(tc + 1) * CH],
                    PB[6][0:DK, :].rearrange("p (c n) -> p c n", c=4))
                e.tensor_copy(
                    attn_f[DK:CH, :].rearrange("p (c n) -> p c n", c=FC)[
                        :, hg * 4:(hg + 1) * 4, tc * CH:(tc + 1) * CH],
                    PB[7][DK:CH, :].rearrange("p (c n) -> p c n", c=4))
            stage("vector", oev, nodrain=True)

        emit_states(t_Kr, t_Qr, t_v, is_A=False)

    def emit_B_epilogue(t):
        r0 = t * TB

        def xload(e, r0=r0):
            e.dma_start(out=xB_t[:].rearrange("p (s o) -> p s o", s=4),
                        in_=x_in[r0:r0 + TB, :].rearrange("(s p) o -> p s o", p=CH))
        stage("sync", xload, ndma=1)

        def womm(e):
            for s in range(4):
                for half in range(2):
                    for pc in range(FC):
                        e.matmul(PB[s * 2 + half][:, :],
                                 attn_f[:, pc * TB + s * CH: pc * TB + (s + 1) * CH],
                                 Wosb[:, pc * D + half * 512: pc * D + (half + 1) * 512],
                                 start=(pc == 0), stop=(pc == FC - 1))
        stage("tensor", womm)

        def ln2_copy(e):
            for s in range(4):
                for half in range(2):
                    e.tensor_copy(a2f[:, s * D + half * 512: s * D + (half + 1) * 512],
                                  PB[s * 2 + half][:, :])
        stage("vector", ln2_copy, nodrain=True)

        emit_ln_tmajor(lambda s: a2f[:, s * D:(s + 1) * D], 4, bsq)

        def resid(e):
            for s in range(4):
                sl = slice(s * D, (s + 1) * D)
                e.scalar_tensor_tensor(xB_t[:, sl], a2f[:, sl], rcol[:, s:s + 1],
                                       xB_t[:, sl], ALU.mult, ALU.add)
            e.drain()
            for s in range(4):
                sl = slice(s * D, (s + 1) * D)
                e.tensor_scalar(xB_t[:, sl], xB_t[:, sl], nmcol[:, s:s + 1],
                                None, ALU.add)
        stage("vector", resid, nodrain=True)

        def xpstore(e, r0=r0):
            e.dma_start(out=S_xp[r0:r0 + TB, :].rearrange("(s p) o -> p s o", p=CH),
                        in_=xB_t[:].rearrange("p (s o) -> p s o", s=4))
        stage("gpsimd", xpstore, ndma=1)

    EN["on"] = True
    for t in range(nb // 4):
        for tc in range(4):
            emit_B(t * 4 + tc)
        emit_B_epilogue(t)

    bes.close()

    ces = ExitStack()

    def csb(name, shape, dt=F32):
        return ces.enter_context(nc.sbuf_tensor(name, shape, dt))

    fstage = csb("fstage", [CH, INNER])
    fint = csb("fint", [CH, INNER], I32)
    fbf = csb("fbf", [CH, INNER], BF16)
    wf_acc = csb("wf_acc", [CH, 32])
    wcols = csb("wcols", [CH, 8])
    scol1 = csb("scol1", [CH, 1])
    xp_t = csb("xp_t", [CH, 4 * D])
    csq = csb("csq", [CH, D])
    x3_t = csb("x3_t", [CH, 4 * D], BF16)
    x3q_t = csb("x3q_t", [CH, 4 * D], BF16)
    x3q_f = csb("x3q_f", [CH, FC * GRP], BF16)
    gint = csb("gint", [CH, D], I32)
    mid = csb("mid", [CH, 32 * GRP], BF16)
    tern_sb = csb("tern_sb", [CH, 4 * D], BF16)
    ternd_sb = csb("ternd_sb", [CH, D], BF16)
    ternd_sb2 = csb("ternd_sb2", [CH, D], BF16)
    gate_sb = csb("gate_sb", [CH, 2 * GRP], BF16)
    maxacc = fstage
    ffn_t = x3_t
    absm = csb("absm", [CH, GRP])
    maxcol = csb("maxcol", [CH, 4])
    qcol = csb("qcol", [CH, 4])
    iqcol = csb("iqcol", [CH, 4])

    def emit_tern(whbm, widx, nrows, ncols, store_swizzled, dst):
        nch = nrows // CH
        for rc in range(nch):
            def ml(e, rc=rc):
                e.dma_start(out=fstage[:, 0:ncols], in_=whbm[rc * CH:(rc + 1) * CH, :])
            stage("sync", ml, ndma=1)

            def mr(e, rc=rc):
                e.tensor_reduce(wf_acc[:, rc:rc + 1], fstage[:, 0:ncols],
                                AX.X, ALU.add, apply_absolute_value=True)
            stage("vector", mr)

        def mfin(e, nch=nch):
            e.tensor_reduce(scol1[:, 0:1], wf_acc[:, 0:nch], AX.X, ALU.add)
        stage("vector", mfin)

        def msum(e):
            e.matmul(PB[0][0:1, 0:1], onecol_fs[:], scol1[:, 0:1], start=True, stop=True)
        stage("tensor", msum)

        def mscal(e, nrows=nrows, ncols=ncols):
            e.tensor_scalar(rowscr[0:1, 0:1], PB[0][0:1, 0:1],
                            1.0 / (nrows * ncols), 1e-5, ALU.mult, ALU.max)
            e.drain()
            e.reciprocal(rowscr[0:1, 1:2], rowscr[0:1, 0:1])
        stage("vector", mscal)

        def mbc(e):
            e.matmul(PB[0][:, 0:2], ones_row_s[0:1, :], rowscr[0:1, 0:2],
                     start=True, stop=True)
        stage("tensor", mbc)

        def mbc_ev(e, widx=widx):
            e.tensor_copy(wcols[:, 3 + widx:4 + widx], PB[0][:, 0:1])
            e.tensor_copy(wcols[:, widx:widx + 1], PB[0][:, 1:2])
        stage("vector", mbc_ev)

        if store_swizzled:
            for ob in range(ncols // CH):
                def tl(e, ob=ob):
                    e.dma_start(
                        out=fstage[:, 0:FC * CH].rearrange("p (c n) -> p c n", c=FC),
                        in_=whbm.rearrange("(c p) n -> p c n", p=CH)[:, :, ob * CH:(ob + 1) * CH])
                stage("sync", tl, ndma=1)

                def tq(e, widx=widx):
                    e.tensor_scalar(fint[:, 0:FC * CH], fstage[:, 0:FC * CH],
                                    wcols[:, widx:widx + 1], None, ALU.mult)
                    e.drain()
                    e.tensor_scalar(fbf[:, 0:FC * CH], fint[:, 0:FC * CH],
                                    -1.0, 1.0, ALU.max, ALU.min)
                stage("vector", tq, nodrain=True)

                def ts_(e, ob=ob):
                    e.dma_start(out=dst[:, ob * FC * CH:(ob + 1) * FC * CH],
                                in_=fbf[:, 0:FC * CH])
                stage("gpsimd", ts_, ndma=1)
        else:
            for rc in range(nch):
                def tl(e, rc=rc):
                    e.dma_start(out=fstage[:, 0:ncols], in_=whbm[rc * CH:(rc + 1) * CH, :])
                stage("sync", tl, ndma=1)

                def tq(e, widx=widx, ncols=ncols):
                    e.tensor_scalar(fint[:, 0:ncols], fstage[:, 0:ncols],
                                    wcols[:, widx:widx + 1], None, ALU.mult)
                    e.drain()
                    e.tensor_scalar(fbf[:, 0:ncols], fint[:, 0:ncols],
                                    -1.0, 1.0, ALU.max, ALU.min)
                stage("vector", tq, nodrain=True)

                def ts_(e, rc=rc, ncols=ncols):
                    e.dma_start(out=dst[rc * CH:(rc + 1) * CH, :], in_=fbf[:, 0:ncols])
                stage("gpsimd", ts_, ndma=1)

    EN["on"] = tern
    emit_tern(wgT, 0, D, INNER, True, TERN_G)
    emit_tern(wuT, 1, D, INNER, True, TERN_U)
    emit_tern(wdT, 2, INNER, D, False, TERN_D)
    EN["on"] = True

    def emit_C(g):
        r0 = g * GRP

        def xpl(e, r0=r0):
            e.dma_start(out=xp_t[:].rearrange("p (s o) -> p s o", s=4),
                        in_=S_xp[r0:r0 + GRP, :].rearrange("(s p) o -> p s o", p=CH))
        stage("sync", xpl, ndma=1)

        def accz(e):
            e.memset(maxacc[:, 0:4 * GRP], 0.0)
        stage("vector", accz, nodrain=True)

        emit_ln_tmajor(lambda s: xp_t[:, s * D:(s + 1) * D], 4, csq)

        def x3mk(e):
            for s in range(4):
                e.tensor_scalar(x3_t[:, s * D:(s + 1) * D],
                                xp_t[:, s * D:(s + 1) * D],
                                rcol[:, s:s + 1], nmcol[:, s:s + 1],
                                ALU.mult, ALU.add)
        stage("vector", x3mk, nodrain=True)

        def qmax(e):
            for s in range(4):
                e.tensor_reduce(tcol[:, s:s + 1], x3_t[:, s * D:(s + 1) * D],
                                AX.X, ALU.max, apply_absolute_value=True)
            e.drain()
            e.tensor_scalar(tcol[:, 0:4], tcol[:, 0:4], 1e-5, None, ALU.max)
        stage("vector", qmax)

        def qscale(e):
            e.reciprocal(qcol[:, 0:4], tcol[:, 0:4])
            e.tensor_scalar(iqcol[:, 0:4], tcol[:, 0:4], 1.0 / 127.0, None, ALU.mult)
            e.drain()
            e.tensor_scalar(qcol[:, 0:4], qcol[:, 0:4], 127.0, None, ALU.mult)
        stage("vector", qscale)

        def xq(e):
            for s in range(4):
                sl = slice(s * D, (s + 1) * D)
                e.tensor_scalar(gint[:], x3_t[:, sl], qcol[:, s:s + 1], None, ALU.mult)
                e.drain()
                e.tensor_scalar(csq[:], gint[:], -128.0, 127.0, ALU.max, ALU.min)
                e.drain()
                e.tensor_scalar(x3q_t[:, sl], csq[:], iqcol[:, s:s + 1], None, ALU.mult)
                e.drain()
        stage("vector", xq, nodrain=True)

        def x3tr(e):
            for s in range(4):
                e.dma_start_transpose(
                    x3q_f[:].rearrange("p (c s t) -> p c s t", c=FC, s=4)[:, :, s, :],
                    x3q_t[:, s * D:(s + 1) * D])
        stage("sync", x3tr, ndma=4)

        for ob in range(16):
            def ternl(e, ob=ob):
                e.dma_start(out=tern_sb[:, 0:D], in_=TERN_G[:, 2 * ob * D:(2 * ob + 1) * D])
                e.dma_start(out=tern_sb[:, D:2 * D], in_=TERN_G[:, (2 * ob + 1) * D:(2 * ob + 2) * D])
                e.dma_start(out=tern_sb[:, 2 * D:3 * D], in_=TERN_U[:, 2 * ob * D:(2 * ob + 1) * D])
                e.dma_start(out=tern_sb[:, 3 * D:4 * D], in_=TERN_U[:, (2 * ob + 1) * D:(2 * ob + 2) * D])
            stage("sync", ternl, ndma=4)

            def gumm(e, ob=ob):
                for j in range(2):
                    for dc in range(FC):
                        e.matmul(PB[j * 2][:, :],
                                 tern_sb[:, j * D + dc * CH: j * D + (dc + 1) * CH],
                                 x3q_f[:, dc * GRP:(dc + 1) * GRP],
                                 start=(dc == 0), stop=(dc == FC - 1))
                    for dc in range(FC):
                        e.matmul(PB[j * 2 + 1][:, :],
                                 tern_sb[:, (2 + j) * D + dc * CH: (2 + j) * D + (dc + 1) * CH],
                                 x3q_f[:, dc * GRP:(dc + 1) * GRP],
                                 start=(dc == 0), stop=(dc == FC - 1))
            stage("tensor", gumm)

            def silu_ev(e, ob=ob):
                e.activation(gate_sb[:, 0:GRP], PB[0][:, :], AF.Silu,
                             bias=0.0, scale=wcols[:, 3:4])
                e.activation(gate_sb[:, GRP:2 * GRP], PB[2][:, :], AF.Silu,
                             bias=0.0, scale=wcols[:, 3:4])
            stage("scalar", silu_ev, nodrain=True)

            def midmk(e, ob=ob):
                for j in range(2):
                    oc = 2 * ob + j
                    e.scalar_tensor_tensor(
                        mid[:, oc * GRP:(oc + 1) * GRP],
                        PB[j * 2 + 1][:, :], wcols[:, 4:5],
                        gate_sb[:, j * GRP:(j + 1) * GRP],
                        ALU.mult, ALU.mult)
            stage("vector", midmk, nodrain=True)

            def absupd(e, ob=ob):
                for j in range(2):
                    oc = 2 * ob + j
                    msl = mid[:, oc * GRP:(oc + 1) * GRP]
                    e.tensor_tensor(maxacc[:, 2 * j * GRP:(2 * j + 1) * GRP],
                                    maxacc[:, 2 * j * GRP:(2 * j + 1) * GRP],
                                    msl, ALU.max)
                    e.tensor_tensor(maxacc[:, (2 * j + 1) * GRP:(2 * j + 2) * GRP],
                                    maxacc[:, (2 * j + 1) * GRP:(2 * j + 2) * GRP],
                                    msl, ALU.min)
            stage("vector", absupd, nodrain=True)

        def mmax(e):
            e.tensor_tensor(absm[:, 0:GRP], maxacc[:, 0:GRP],
                            maxacc[:, 2 * GRP:3 * GRP], ALU.max)
            e.tensor_tensor(csq[:, 0:GRP], maxacc[:, GRP:2 * GRP],
                            maxacc[:, 3 * GRP:4 * GRP], ALU.min)
            e.drain()
            e.tensor_scalar(csq[:, 0:GRP], csq[:, 0:GRP], -1.0, None, ALU.mult)
            e.drain()
            e.tensor_tensor(absm[:, 0:GRP], absm[:, 0:GRP], csq[:, 0:GRP], ALU.max)
        stage("vector", mmax, nodrain=True)

        def mmtr(e):
            for tb in range(4):
                e.transpose(PB[0][:, tb * CH:(tb + 1) * CH],
                            absm[:, tb * CH:(tb + 1) * CH], ident_fs[:, :])
        stage("tensor", mmtr)

        def mmred(e):
            for tb in range(4):
                e.tensor_reduce(maxcol[:, tb:tb + 1], PB[0][:, tb * CH:(tb + 1) * CH],
                                AX.X, ALU.max)
            e.drain()
            e.tensor_scalar(maxcol[:, 0:4], maxcol[:, 0:4], 1e-5, None, ALU.max)
        stage("vector", mmred)

        def mctr(e):
            for tb in range(4):
                e.transpose(PB[1][0:1, tb * CH:(tb + 1) * CH],
                            maxcol[:, tb:tb + 1], ident_fs[:, :])
        stage("tensor", mctr)

        def mcev(e):
            e.tensor_copy(rowscr[0:1, 0:512], PB[1][0:1, :])
            e.drain()
            e.reciprocal(rowscr[0:1, 512:1024], rowscr[0:1, 0:512])
            e.tensor_scalar(rowscr[0:1, 1024:1536], rowscr[0:1, 0:512],
                            1.0 / 127.0, None, ALU.mult)
            e.drain()
            e.tensor_scalar(rowscr[0:1, 512:1024], rowscr[0:1, 512:1024],
                            127.0, None, ALU.mult)
        stage("vector", mcev)

        def mqbc(e):
            e.matmul(PB[2][:, :], ones_row_s[0:1, :], rowscr[0:1, 512:1024],
                     start=True, stop=True)
            e.matmul(PB[3][:, :], ones_row_s[0:1, :], rowscr[0:1, 1024:1536],
                     start=True, stop=True)
        stage("tensor", mqbc)

        def mq(e):
            for ob in range(16):
                for j in range(2):
                    oc = 2 * ob + j
                    e.tensor_tensor(gint[:, j * GRP:(j + 1) * GRP],
                                    mid[:, oc * GRP:(oc + 1) * GRP],
                                    PB[2][:, :], ALU.mult)
                e.drain()
                for j in range(2):
                    oc = 2 * ob + j
                    e.tensor_scalar(mid[:, oc * GRP:(oc + 1) * GRP],
                                    gint[:, j * GRP:(j + 1) * GRP],
                                    -128.0, 127.0, ALU.max, ALU.min)
                e.drain()
        stage("vector", mq, nodrain=True)

        for ic in range(32):
            gi = g * 32 + ic
            par = gi % 2
            tbuf = ternd_sb if par == 0 else ternd_sb2

            def dtl(e, ic=ic, tbuf=tbuf):
                e.dma_start(out=tbuf[:], in_=TERN_D[ic * CH:(ic + 1) * CH, :])
            w = ((SPE, gi - 1),) if gi >= 2 else ()
            stage("sync", dtl, ndma=1, relax=(SD[par], 16), waits=w)

            def dmm(e, ic=ic, tbuf=tbuf):
                for s in range(4):
                    for half in range(2):
                        e.matmul(PB[s * 2 + half][:, :],
                                 mid[:, ic * GRP + s * CH: ic * GRP + (s + 1) * CH],
                                 tbuf[:, half * 512:(half + 1) * 512],
                                 start=(ic == 0), stop=(ic == 31),
                                 skip_group_check=True)
            stage("tensor", dmm, relax=(SPE, 1),
                  waits=((SD[par], 16 * (gi // 2 + 1)),))

        def dev(e):
            for s in range(4):
                for half in range(2):
                    e.tensor_copy(ffn_t[:, s * D + half * 512: s * D + (half + 1) * 512],
                                  PB[s * 2 + half][:, :])
        stage("vector", dev, nodrain=True, waits=((SPE, 32 * (g + 1)),))

        emit_ln_tmajor(lambda s: ffn_t[:, s * D:(s + 1) * D], 4, csq)

        def resid4(e):
            for s in range(4):
                sl = slice(s * D, (s + 1) * D)
                e.scalar_tensor_tensor(xp_t[:, sl], ffn_t[:, sl], rcol[:, s:s + 1],
                                       xp_t[:, sl], ALU.mult, ALU.add)
            e.drain()
            for s in range(4):
                sl = slice(s * D, (s + 1) * D)
                e.tensor_scalar(xp_t[:, sl], xp_t[:, sl], nmcol[:, s:s + 1],
                                None, ALU.add)
        stage("vector", resid4, nodrain=True)

        def ostore(e, r0=r0):
            e.dma_start(out=out_t[r0:r0 + GRP, :].rearrange("(s p) o -> p s o", p=CH),
                        in_=xp_t[:].rearrange("p (s o) -> p s o", s=4))
        stage("gpsimd", ostore, ndma=1)

    for g in range(ngr):
        emit_C(g)

    ces.close()

    stage("gpsimd", lambda e: None)

    if nstop is not None:
        del SEGS[nstop:]
        SEGS.append(("gpsimd", lambda e: None, 0, 0, False, None, ()))

    contrib = [0 if rx is not None else 16 * nd + ex + 1
               for (_, _, nd, ex, _, rx, _) in SEGS]
    cum = [0]
    for c in contrib:
        cum.append(cum[-1] + c)
    globals()["LAST_NSEGS"] = len(SEGS)
    globals()["LAST_CUMTOT"] = cum[-1]

    class Proxy:
        def __init__(self, eng, sem, dma_sem=None):
            self._eng = eng
            self._sem = dma_sem if dma_sem is not None else sem
            self.nd = 0
            self.last = None

        def dma_start(self, *a, **k):
            ins = self._eng.dma_start(*a, **k)
            ins.then_inc(self._sem, 16)
            self.nd += 1
            return ins

        def dma_start_transpose(self, *a, **k):
            ins = self._eng.dma_start_transpose(*a, **k)
            ins.then_inc(self._sem, 16)
            self.nd += 1
            return ins

        def __getattr__(self, n):
            f = getattr(self._eng, n)
            if callable(f):
                def wrap(*a, **k):
                    r = f(*a, **k)
                    if hasattr(r, "then_inc"):
                        self.last = r
                    return r
                return wrap
            return f

    with nc.Block() as block:
        def runner(ename):
            def f(eng):
                for si, (en, cb, nd, ex, ndr, rx, wts) in enumerate(SEGS):
                    if en != ename:
                        continue
                    for (ws, wv) in wts:
                        eng.wait_ge(ws, wv)
                    if cum[si] > 0:
                        eng.wait_ge(SB, cum[si])
                    p = Proxy(eng, SB, dma_sem=rx[0] if rx else None)
                    cb(p)
                    assert p.nd == nd, (si, ename, p.nd, nd)
                    if rx is not None:
                        if ename == "tensor":
                            p.last.then_inc(rx[0], rx[1])
                    elif ename == "tensor":
                        assert p.last is not None, si
                        p.last.then_inc(SB, 1)
                    elif ename in ("vector", "scalar") and p.last is not None:
                        if ndr:
                            p.last.then_inc(SB, 1)
                        else:
                            eng.drain().then_inc(SB, 1)
                    elif p.last is not None:
                        p.last.then_inc(SB, 1)
                    else:
                        eng.nop().then_inc(SB, 1)
            return f

        block.sync(runner("sync"))
        block.tensor(runner("tensor"))
        block.vector(runner("vector"))
        block.scalar(runner("scalar"))
        block.gpsimd(runner("gpsimd"))

    ges.close()
    return nc


_NC_CACHE = {}


def _consts():
    r = np.arange(CH)[:, None]
    c = np.arange(CH)[None, :]
    incl = (r <= c).astype(np.float32)
    rec = ((r <= 63).astype(np.float32) - incl)
    rear_neg = -(r > c).astype(np.float32)
    strict = (r < c).astype(np.float32)
    o = {}
    o["m_incl_neg"] = -incl
    o["m_rec"] = rec
    o["m_rear_neg"] = rear_neg
    o["sm_incl4"] = np.tile(incl, (1, 4)).astype(BF).view(np.uint16)
    o["sm_strict4"] = np.tile(strict, (1, 4)).astype(BF).view(np.uint16)
    o["ident_f"] = np.eye(CH, dtype=np.float32)
    o["ident_b"] = np.eye(CH, dtype=np.float32).astype(BF).view(np.uint16)
    o["ones_row"] = np.ones((1, CH), np.float32)
    o["onecol_f"] = np.ones((CH, 1), np.float32)
    o["negcol_f"] = -np.ones((CH, 1), np.float32)
    return o


def kernel(L0_attn_x, L0_attn_W_q, L0_attn_W_k, L0_attn_W_v, L0_attn_Wg_K,
           L0_attn_bg_K, L0_attn_Wg_C, L0_attn_bg_C, L0_attn_W_o,
           L0_ffn_w_gate, L0_ffn_w_up, L0_ffn_w_down):
    x = np.ascontiguousarray(np.asarray(L0_attn_x, np.float32))
    B, T, _ = x.shape
    if "nc" not in _NC_CACHE:
        _NC_CACHE["nc"] = build_program()
    nc = _NC_CACHE["nc"]

    shared = {
        "Wq": np.ascontiguousarray(np.asarray(L0_attn_W_q, np.float32)),
        "Wk": np.ascontiguousarray(np.asarray(L0_attn_W_k, np.float32)),
        "Wv": np.ascontiguousarray(np.asarray(L0_attn_W_v, np.float32)),
        "WgK": np.ascontiguousarray(np.asarray(L0_attn_Wg_K, np.float32)),
        "WgC": np.ascontiguousarray(np.asarray(L0_attn_Wg_C, np.float32)),
        "Wo": np.ascontiguousarray(np.asarray(L0_attn_W_o, np.float32)),
        "bgK": np.ascontiguousarray(np.asarray(L0_attn_bg_K, np.float32)).reshape(1, D),
        "bgC": np.ascontiguousarray(np.asarray(L0_attn_bg_C, np.float32)).reshape(1, D),
        "wgT": np.ascontiguousarray(np.asarray(L0_ffn_w_gate, np.float32).T),
        "wuT": np.ascontiguousarray(np.asarray(L0_ffn_w_up, np.float32).T),
        "wdT": np.ascontiguousarray(np.asarray(L0_ffn_w_down, np.float32).T),
        **_consts(),
    }
    in_maps = []
    for core in range(8):
        b, pos = core // 4, core % 4
        sel = np.zeros((CH, 9), np.float32)
        for q in range(3):
            sel[:, 6 + q] = 1.0
            if q < pos:
                sel[:, q] = 1.0
                sel[:, 3 + q] = -1.0
                sel[:, 6 + q] = 0.0
        m = dict(shared)
        m["x_in"] = np.ascontiguousarray(x[b, pos * NT:(pos + 1) * NT, :])
        m["predsel"] = sel
        in_maps.append(m)

    res = run_bass_kernel_spmd(nc, in_maps, core_ids=list(range(8)))
    out = np.empty((B, T, D), np.float32)
    for core in range(8):
        b, pos = core // 4, core % 4
        out[b, pos * NT:(pos + 1) * NT, :] = res.results[core]["out_t"]
    return out


if __name__ == "__main__":
    nc = build_program()
    print("built ok")
